# version 10
# baseline (speedup 1.0000x reference)
import numpy as np
import ml_dtypes
import concourse.bass as bass
import concourse.mybir as mybir
from concourse.bass_utils import run_bass_kernel_spmd
from contextlib import ExitStack

F32 = mybir.dt.float32
BF16 = mybir.dt.bfloat16
AF = mybir.ActivationFunctionType
ALU = mybir.AluOpType
AX = mybir.AxisListType

NCORES = 8
B = 2
N = 8192
C = 1024
H = 16
E = 8
M = 16
TOK = 2048
TC = 16
TCB = 8
NTT = 2
TTW = 1024
TCC = 8
EPS = 1e-5
DEBUG = False
PHASES = "123"
P2STOP = 9


class Emit:

    def __init__(self):
        self.real = False
        self.cnt = {}
        self.marks = {}
        self.sems = {}
        self.eng = None
        self.chain = None
        self.last = None

    def op(self, f, inc=None, k=1):
        if self.chain is not None:
            if self.last is not None:
                self.wait(self.last[0], value=self.last[1])
            if inc is None:
                inc = self.chain
        inst = f() if self.real else None
        if inc is not None:
            if self.real:
                inst.then_inc(self.sems[inc], k)
            self.cnt[inc] = self.cnt.get(inc, 0) + k
            if self.chain is not None:
                self.last = (inc, self.cnt[inc])
        return inst

    def chain_on(self, sem):
        self.chain = sem
        self.last = None

    def chain_off(self):
        self.chain = None
        self.last = None

    def dma(self, f, inc):
        return self.op(f, inc, 16)

    def mark(self, name):
        if not self.real:
            self.marks[name] = dict(self.cnt)

    def wait(self, sem, mark_name=None, value=None):
        if value is None:
            if mark_name not in self.marks:
                return
            value = self.marks[mark_name].get(sem, 0)
        if value <= 0:
            return
        if self.real:
            self.eng.wait_ge(self.sems[sem], value)

    def selfwait(self, sem):
        self.wait(sem, value=self.cnt.get(sem, 0))

    def reset(self):
        self.cnt = {}


SEM_NAMES = [
    "sCst", "sW", "sXa", "sXb", "sSa", "sSb", "sOa", "sOb", "sColl", "sXl", "sDa", "sDb", "sG0", "sG1", "sG2",
    "sWk", "sWv", "sWq", "sBkv", "sXg0", "sXg1", "sXg2", "sXg3",
    "c_q", "c_eq", "c_k", "c_ek", "c_v", "c_va", "c_U", "c_Ue", "c_stA",
    "p2_pe", "p2_v", "p2_s", "sCC",
    "c_xe", "c_a", "c_b", "c_sa", "c_h", "c_dn", "c_dnE",
    "c_wbc", "c_wbcV", "c_ln4s", "c_ln4v", "c_sV", "c_sS",
]
SG = ["sG0", "sG1", "sG2"]
_PFX = [""]


def build_nc():
    nc = bass.Bass("TRN2", target_bir_lowering=False)

    d = {}
    def din(name, shape, dt=F32):
        d[name] = nc.dram_tensor(name, list(shape), dt, kind="ExternalInput").ap()
        return d[name]

    xT_d = din("xT", [128, 8, TOK])
    x_d = din("x_tm", [128, TC, C])
    wqT_d = din("wqT", [128, 8, C])
    wkT_d = din("wkT", [128, 8, C])
    wvT_d = din("wvT", [128, 8, C])
    woT_d = din("woT", [128, 8, C])
    bq_d = din("bq", [128, 8])
    bk_d = din("bk_row", [1, C])
    bv_d = din("bv_row", [1, C])
    bo_d = din("bo_row", [1, C])
    lngf_d = din("lng_fm", [128, 8])
    lnbf_d = din("lnb_fm", [128, 8])
    fw_d = din("fw", [128, C])
    fb_d = din("fb", [128, C])
    rout_d = din("router", [128, 8, E])
    gate_d = din("gate", [E, M, 128, 8, 128])
    up_d = din("up", [E, M, 128, 8, 128])
    down_d = din("down", [E, M, 128, C])
    shg_d = din("shg", [M, 128, 8, 128])
    shu_d = din("shu", [M, 128, 8, 128])
    shd_d = din("shd", [M, 128, C])
    ones_d = din("ones_bf", [1, 128], BF16)
    onesc_d = din("onescol_bf", [128, 1], BF16)
    ones2_d = din("ones2_bf", [128, 2], BF16)
    idt_d = din("idt_bf", [128, 128], BF16)
    sel_d = din("sel_bf", [8, E * 128], BF16)
    eps_d = din("eps_col", [128, 1])

    out_d = nc.dram_tensor("out", [128, TC, C], F32, kind="ExternalOutput").ap()
    kd = "ExternalOutput" if DEBUG else "Internal"
    x1sp_d = nc.dram_tensor("x1_spill", [128, TC, C], F32, kind=kd).ap()
    cc_in = nc.dram_tensor("cc_in", [64, 2080], F32, kind="Internal").ap()
    cc_out = nc.dram_tensor("cc_out", [64, 2080], F32, kind="Internal").ap()
    if DEBUG:
        dbg_ccin = nc.dram_tensor("dbg_ccin", [64, 2080], F32,
                                  kind="ExternalOutput").ap()
        dbg_ccout = nc.dram_tensor("dbg_ccout", [64, 2080], F32,
                                   kind="ExternalOutput").ap()
        dbg_state = nc.dram_tensor("dbg_state", [128, B, H, 64], F32,
                                   kind="ExternalOutput").ap()
        dbg_eq = nc.dram_tensor("dbg_eq", [128, 8, TOK], F32,
                                kind="ExternalOutput").ap()

    EM = Emit()

    with ExitStack() as st:
        e_ = st.enter_context

        xf_fm = e_(nc.sbuf_tensor(_PFX[0] + "xf_fm", [128, 8, TOK], BF16))
        lngf_sb = e_(nc.sbuf_tensor(_PFX[0] + "lngf_sb", [128, 8], F32))
        lnbf_sb = e_(nc.sbuf_tensor(_PFX[0] + "lnbf_sb", [128, 8], BF16))
        ob_sb = e_(nc.sbuf_tensor(_PFX[0] + "ob_sb", [1, C], BF16))
        bq_sb = e_(nc.sbuf_tensor(_PFX[0] + "bq_sb", [128, 8], F32))
        rout_sb = e_(nc.sbuf_tensor(_PFX[0] + "rout_sb", [128, 8, E], BF16))
        ones_sb = e_(nc.sbuf_tensor(_PFX[0] + "ones_sb", [1, 128], BF16))
        onesc_sb = e_(nc.sbuf_tensor(_PFX[0] + "onesc_sb", [128, 1], BF16))
        ones2_sb = e_(nc.sbuf_tensor(_PFX[0] + "ones2_sb", [128, 2], BF16))
        idt_sb = e_(nc.sbuf_tensor(_PFX[0] + "idt_sb", [128, 128], BF16))
        sel_sb = e_(nc.sbuf_tensor(_PFX[0] + "sel_sb", [8, E * 128], BF16))
        eps_sb = e_(nc.sbuf_tensor(_PFX[0] + "eps_sb", [128, 1], F32))
        wstage = e_(nc.sbuf_tensor(_PFX[0] + "wstage", [8, TOK], BF16))
        state_sb = e_(nc.sbuf_tensor(_PFX[0] + "state_sb", [128, B, H, 64], BF16))

        for s in SEM_NAMES:
            EM.sems[s] = e_(nc.semaphore(s))
        block = e_(nc.Block())

        def p12_program(EM, scope):
            e2 = scope.enter_context
            x_fm = e2(nc.sbuf_tensor(_PFX[0] + "x_fm", [128, 8, TOK], BF16))
            fw_sb = e2(nc.sbuf_tensor(_PFX[0] + "fw_sb", [128, C], F32))
            fb_sb = e2(nc.sbuf_tensor(_PFX[0] + "fb_sb", [128, C], F32))
            bk_sb = e2(nc.sbuf_tensor(_PFX[0] + "bk_sb", [1, C], BF16))
            bv_sb = e2(nc.sbuf_tensor(_PFX[0] + "bv_sb", [1, C], BF16))
            bo_sb = e2(nc.sbuf_tensor(_PFX[0] + "bo_sb", [1, C], BF16))
            wq_sb = e2(nc.sbuf_tensor(_PFX[0] + "wq_sb", [128, 8, C], BF16))
            eq_sb = e2(nc.sbuf_tensor(_PFX[0] + "eq_sb", [128, 8, TOK], BF16))

            with ExitStack() as ps1:
                e3 = ps1.enter_context
                wk_sb = e3(nc.sbuf_tensor(_PFX[0] + "wk_sb", [128, 8, C], BF16))
                wv_sb = e3(nc.sbuf_tensor(_PFX[0] + "wv_sb", [128, 8, C], BF16))
                ek_sb = e3(nc.sbuf_tensor(_PFX[0] + "ek_sb", [128, 2, C], BF16))
                v_sb = e3(nc.sbuf_tensor(_PFX[0] + "v_sb", [128, 2, C], BF16))
                ustage = e3(nc.sbuf_tensor(_PFX[0] + "ustage", [64, 2080], F32))
                stateA = e3(nc.sbuf_tensor(_PFX[0] + "stateA", [64, B * C], BF16))
                rz_sb = e3(nc.sbuf_tensor(_PFX[0] + "rz_sb", [64, 32], F32))
                state_raw = ustage
                k_ps = e3(nc.psum_tensor(_PFX[0] + "k_ps", [128, C], F32))
                v_ps = e3(nc.psum_tensor(_PFX[0] + "v_ps", [128, C], F32))
                U_ps = e3(nc.psum_tensor(_PFX[0] + "U_ps", [64, C], F32))
                z_ps = e3(nc.psum_tensor(_PFX[0] + "z_ps", [64, 32], F32))
                q_ps = k_ps

                def p1b_sync(s):
                    EM.eng = s
                    EM.chain_off()
                    for src, dst in [(ones_d, ones_sb), (onesc_d, onesc_sb),
                                     (bq_d, bq_sb), (lngf_d, lngf_sb),
                                     (fw_d, fw_sb),
                                     (fb_d, fb_sb),
                                     (ones2_d, ones2_sb),
                                     (idt_d, idt_sb), (sel_d, sel_sb),
                                     (eps_d, eps_sb)]:
                        EM.dma(lambda src=src, dst=dst:
                               s.dma_start(out=dst[:], in_=src), "sCst")
                    EM.mark("consts")

                def p1b_gp(g):
                    EM.eng = g
                    EM.chain_off()
                    EM.dma(lambda: g.dma_start(out=wk_sb[:], in_=wkT_d),
                           "sWk")
                    EM.mark("wk")
                    EM.dma(lambda: g.dma_start(
                        out=x_fm[:, :, 0:512], in_=xT_d[:, :, 0:512]), "sXg0")
                    EM.mark(("xg", 0))
                    for src, dst in [(bk_d, bk_sb), (bv_d, bv_sb)]:
                        EM.dma(lambda src=src, dst=dst:
                               g.dma_start(out=dst[:], in_=src), "sBkv")
                    EM.mark("bkv")
                    EM.dma(lambda: g.dma_start(out=wv_sb[:], in_=wvT_d),
                           "sWv")
                    EM.mark("wv")
                    for gi in range(1, 4):
                        EM.dma(lambda gi=gi: g.dma_start(
                            out=x_fm[:, :, gi * 512:(gi + 1) * 512],
                            in_=xT_d[:, :, gi * 512:(gi + 1) * 512]),
                            "sXg%d" % gi)
                        EM.mark(("xg", gi))
                    EM.dma(lambda: g.dma_start(out=wq_sb[:], in_=wqT_d),
                           "sWq")
                    EM.mark("wq")
                    for src, dst in [(bo_d, bo_sb), (rout_d, rout_sb),
                                     (lnbf_d, lnbf_sb)]:
                        EM.dma(lambda src=src, dst=dst:
                               g.dma_start(out=dst[:], in_=src), "sW")
                    EM.mark("p1w")

                def p1b_pe(t):
                    EM.eng = t
                    EM.chain_off()
                    EM.wait("sCst", "consts")
                    EM.wait("sBkv", "bkv")
                    for tc in range(TC):
                        b = tc // TCB
                        EM.wait("sXg%d" % (tc // 4), ("xg", tc // 4))
                        if tc == 0:
                            EM.wait("sWk", "wk")
                        if tc >= 1:
                            EM.wait("c_ek", value=tc)
                        for half in range(2):
                            for jc in range(8):
                                EM.op(lambda jc=jc, half=half, tc=tc: t.matmul(
                                    k_ps[:, half * 512:(half + 1) * 512],
                                    x_fm[:, jc, tc * 128:(tc + 1) * 128],
                                    wk_sb[:, jc, half * 512:(half + 1) * 512],
                                    start=(jc == 0), stop=False))
                            EM.op(lambda half=half: t.matmul(
                                k_ps[:, half * 512:(half + 1) * 512],
                                ones_sb[0:1, :],
                                bk_sb[0:1, half * 512:(half + 1) * 512],
                                start=False, stop=True),
                                inc="c_k" if half == 1 else None)
                        if tc == 0:
                            EM.wait("sWv", "wv")
                        if tc >= 1:
                            EM.wait("c_va", value=tc)
                        for half in range(2):
                            for jc in range(8):
                                EM.op(lambda jc=jc, half=half, tc=tc: t.matmul(
                                    v_ps[:, half * 512:(half + 1) * 512],
                                    x_fm[:, jc, tc * 128:(tc + 1) * 128],
                                    wv_sb[:, jc, half * 512:(half + 1) * 512],
                                    start=(jc == 0), stop=False))
                            EM.op(lambda half=half: t.matmul(
                                v_ps[:, half * 512:(half + 1) * 512],
                                ones_sb[0:1, :],
                                bv_sb[0:1, half * 512:(half + 1) * 512],
                                start=False, stop=True),
                                inc="c_v" if half == 1 else None)
                        EM.wait("c_ek", value=tc + 1)
                        EM.wait("c_va", value=tc + 1)
                        if tc == TCB:
                            EM.wait("c_Ue", value=1)
                        for h in range(H):
                            ust = (tc % TCB == 0) and (h in (0, 8))
                            usp = (tc % TCB == TCB - 1) and (h in (7, 15))
                            EM.op(lambda h=h, tc=tc, ust=ust, usp=usp: t.matmul(
                                U_ps[0:64, h * 64:(h + 1) * 64],
                                ek_sb[:, tc % 2, h * 64:(h + 1) * 64],
                                v_sb[:, tc % 2, h * 64:(h + 1) * 64],
                                start=ust, stop=usp))
                            zst = (tc == 0) and (h == 0)
                            zsp = (tc == TC - 1) and (h == H - 1)
                            EM.op(lambda h=h, b=b, zst=zst, zsp=zsp: t.matmul(
                                z_ps[0:64, b * 16 + h:b * 16 + h + 1],
                                ek_sb[:, tc % 2, h * 64:(h + 1) * 64],
                                onesc_sb[:, 0:1],
                                start=zst, stop=zsp),
                                inc="c_U" if h == H - 1 else None)

                def p1b_s(sc):
                    EM.eng = sc
                    EM.chain_on("c_sS")
                    for tc in range(TC):
                        EM.wait("c_k", value=tc + 1)
                        if tc >= 2:
                            EM.wait("c_U", value=tc - 1)
                        EM.op(lambda tc=tc: sc.activation(
                            ek_sb[:, tc % 2, :], k_ps[:, :], AF.Exp),
                            inc="c_ek")

                def p1b_v(v):
                    EM.eng = v
                    EM.chain_on("c_sV")
                    for tc in range(TC):
                        EM.wait("c_v", value=tc + 1)
                        if tc >= 2:
                            EM.wait("c_U", value=tc - 1)
                        EM.op(lambda tc=tc: v.tensor_copy(
                            v_sb[:, tc % 2, :], v_ps[:, :]), inc="c_va")
                        if tc % TCB == TCB - 1:
                            b = tc // TCB
                            EM.wait("c_U", value=tc + 1)
                            EM.op(lambda b=b: v.tensor_copy(
                                ustage[0:64, b * C:(b + 1) * C], U_ps[0:64, :]),
                                inc="c_Ue")
                    EM.op(lambda: v.tensor_copy(ustage[0:64, 2048:2080],
                                                z_ps[0:64, :]), inc="c_Ue")

                if EM.real:
                    block.sync(p1b_sync); block.gpsimd(p1b_gp)
                    block.tensor(p1b_pe); block.scalar(p1b_s)
                    block.vector(p1b_v)
                else:
                    p1b_sync(None); p1b_gp(None); p1b_pe(None)
                    p1b_s(None); p1b_v(None)

                def cc_gp(g):
                    EM.eng = g
                    EM.chain_off()
                    EM.wait("c_Ue", value=3)
                    EM.dma(lambda: g.dma_start(out=cc_in, in_=ustage[0:64, :]),
                           "sColl")
                    EM.wait("sColl", value=16)
                    EM.op(lambda: g.collective_compute(
                        "AllReduce", ALU.add,
                        replica_groups=[list(range(NCORES))],
                        ins=[cc_in.opt()], outs=[cc_out.opt()]),
                        inc="sCC", k=1)
                    EM.wait("sCC", value=1)
                    EM.dma(lambda: g.dma_start(out=state_raw[0:64, :],
                                               in_=cc_out), "sColl")
                    if DEBUG:
                        EM.dma(lambda: g.dma_start(out=dbg_ccin, in_=cc_in),
                               "sColl")
                        EM.dma(lambda: g.dma_start(out=dbg_ccout, in_=cc_out),
                               "sColl")
                    EM.mark("cc_done")

                def cc_v(v):
                    EM.eng = v
                    EM.chain_on("c_sV")
                    EM.op(lambda: v.memset(state_sb[:], 0.0))
                    EM.wait("sColl", "cc_done")
                    EM.op(lambda: v.reciprocal(rz_sb[0:64, :],
                                               state_raw[0:64, 2048:2080]),
                          inc="c_sV")
                    EM.selfwait("c_sV")
                    for b in range(B):
                        for h in range(H):
                            EM.op(lambda b=b, h=h: v.tensor_single_scalar(
                                stateA[0:64, b * C + h * 64:b * C + (h + 1) * 64],
                                state_raw[0:64, b * C + h * 64:b * C + (h + 1) * 64],
                                rz_sb[0:64, b * 16 + h:b * 16 + h + 1],
                                ALU.mult))
                    EM.op(lambda: v.engine_nop(), inc="c_stA", k=1)

                def cc_gp2(g):
                    EM.eng = g
                    EM.chain_off()
                    EM.wait("c_stA", value=1)
                    srcA = stateA[0:64, :].rearrange(
                        "p (b h2 par e) -> p par b h2 e", b=B, h2=8, par=2)
                    dstE = state_sb[0:64, :, :, :].rearrange(
                        "p b (h2 par) e -> p par b h2 e", par=2)
                    dstO = state_sb[64:128, :, :, :].rearrange(
                        "p b (h2 par) e -> p par b h2 e", par=2)
                    EM.dma(lambda: g.dma_start(
                        out=dstE[:, 0], in_=srcA[:, 0]), "sColl")
                    EM.dma(lambda: g.dma_start(
                        out=dstO[:, 1], in_=srcA[:, 1]), "sColl")
                    if DEBUG:
                        EM.wait("sColl", value=EM.cnt.get("sColl", 0))
                        EM.wait("c_eq", value=32)
                        EM.dma(lambda: g.dma_start(
                            out=dbg_state, in_=state_sb[:]), "sColl")
                        EM.dma(lambda: g.dma_start(
                            out=dbg_eq, in_=eq_sb[:]), "sColl")
                    EM.mark("state_done")

                def p1a_pe(t):
                    EM.eng = t
                    EM.chain_off()
                    EM.wait("c_ek", value=TC)
                    EM.wait("sWq", "wq")
                    for ic in range(8):
                        for tq in range(4):
                            qi = ic * 4 + tq
                            if qi >= 2:
                                EM.wait("c_eq", value=qi - 1)
                            for jc in range(8):
                                EM.op(lambda jc=jc, ic=ic, tq=tq, qi=qi:
                                      t.matmul(
                                    q_ps[:, (qi % 2) * 512:(qi % 2) * 512 + 512],
                                    wq_sb[:, jc, ic * 128:(ic + 1) * 128],
                                    x_fm[:, jc, tq * 512:(tq + 1) * 512],
                                    start=(jc == 0), stop=(jc == 7)),
                                    inc="c_q" if jc == 7 else None)

                def p1a_s(sc):
                    EM.eng = sc
                    EM.chain_on("c_sS")
                    for ic in range(8):
                        for tq in range(4):
                            qi = ic * 4 + tq
                            EM.wait("c_q", value=qi + 1)
                            EM.op(lambda ic=ic, tq=tq, qi=qi: sc.activation(
                                eq_sb[:, ic, tq * 512:(tq + 1) * 512],
                                q_ps[:, (qi % 2) * 512:(qi % 2) * 512 + 512],
                                AF.Exp,
                                bias=bq_sb[:, ic:ic + 1], scale=1.0),
                                inc="c_eq")

                if EM.real:
                    block.gpsimd(cc_gp); block.tensor(p1a_pe)
                    block.scalar(p1a_s); block.vector(cc_v)
                    block.gpsimd(cc_gp2)
                else:
                    cc_gp(None); p1a_pe(None); p1a_s(None); cc_v(None)
                    cc_gp2(None)

            wo_sb = wq_sb
            x_sl = e2(nc.sbuf_tensor(_PFX[0] + "x_sl", [128, 2, C], F32))
            attn_sb = e2(nc.sbuf_tensor(_PFX[0] + "attn_sb", [128, C], F32))
            scr_sb = e2(nc.sbuf_tensor(_PFX[0] + "scr_sb", [128, C], F32))
            scr2_sb = e2(nc.sbuf_tensor(_PFX[0] + "scr2_sb", [128, C], F32))
            sdump_sb = e2(nc.sbuf_tensor(_PFX[0] + "sdump_sb", [128, C], BF16))
            sdump2_sb = e2(nc.sbuf_tensor(_PFX[0] + "sdump2_sb", [128, C], BF16))
            ln1_bf = e2(nc.sbuf_tensor(_PFX[0] + "ln1_bf", [128, C], BF16))
            lnfm_sb = e2(nc.sbuf_tensor(_PFX[0] + "lnfm_sb", [128, 8, 128], BF16))
            r_sb = e2(nc.sbuf_tensor(_PFX[0] + "r_sb", [128, C], F32))
            x1tc_sb = e2(nc.sbuf_tensor(_PFX[0] + "x1tc_sb", [128, 2, C], F32))
            xftm_sb = e2(nc.sbuf_tensor(_PFX[0] + "xftm_sb", [128, C], BF16))
            rzq_sb = e2(nc.sbuf_tensor(_PFX[0] + "rzq_sb", [128, 16], F32))
            stat_sb = e2(nc.sbuf_tensor(_PFX[0] + "stat_sb", [128, 24], F32))
            wall_sb = e2(nc.sbuf_tensor(_PFX[0] + "wall_sb", [128, E], F32))
            wbf_sb = e2(nc.sbuf_tensor(_PFX[0] + "wbf_sb", [128, E], BF16))

            with ExitStack() as ps2:
                e3 = ps2.enter_context
                out_ps = e3(nc.psum_tensor(_PFX[0] + "out_ps", [128, C], F32))
                o_ps = e3(nc.psum_tensor(_PFX[0] + "o_ps", [128, C], F32))
                small_ps = e3(nc.psum_tensor(_PFX[0] + "small_ps", [128, 16], F32))
                lg_ps = e3(nc.psum_tensor(_PFX[0] + "lg_ps", [128, E], F32))
                bfT_ps = e3(nc.psum_tensor(_PFX[0] + "bfT_ps", [128, C], BF16))

                def p2_pe(t):
                    EM.eng = t
                    EM.chain_off()
                    EM.wait("sColl", "state_done")
                    EM.wait("sW", "p1w")
                    EM.wait("sW", "woT_g")
                    EM.wait("c_eq", value=32)
                    for half in range(2):
                        for jc in range(8):
                            EM.op(lambda jc=jc, half=half: t.matmul(
                                o_ps[0:1, half * 512:(half + 1) * 512],
                                lnbf_sb[:, jc:jc + 1],
                                wo_sb[:, jc, half * 512:(half + 1) * 512],
                                start=(jc == 0), stop=(jc == 7)),
                                inc="p2_pe" if (jc == 7 and half == 1)
                                else None)
                    EM.mark("obias_mm")
                    for tc in range(TC):
                        b = tc // TCB
                        if tc >= 1:
                            EM.wait("p2_v", ("outn", tc - 1))
                            EM.wait("p2_v", ("rzq", tc - 1))
                        for h in range(H):
                            EM.op(lambda h=h, b=b, tc=tc: t.matmul(
                                out_ps[:, h * 64:(h + 1) * 64],
                                eq_sb[:, h // 2, tc * 128:(tc + 1) * 128],
                                state_sb[:, b, h, :],
                                start=True, stop=True))
                        for ic in range(8):
                            EM.op(lambda ic=ic, tc=tc: t.matmul(
                                small_ps[:, 2 * ic:2 * ic + 2],
                                eq_sb[:, ic, tc * 128:(tc + 1) * 128],
                                ones2_sb[:, :], start=True, stop=True),
                                inc="p2_pe" if ic == 7 else None)
                        EM.mark(("g0", tc))
                        if P2STOP < 2:
                            continue
                        EM.wait("p2_v", ("ln1bf", tc))
                        if tc >= 1:
                            EM.wait("p2_v", ("wst", tc - 1))
                        for jc in range(8):
                            EM.op(lambda jc=jc: t.transpose(
                                bfT_ps[:, jc * 128:(jc + 1) * 128],
                                ln1_bf[:, jc * 128:(jc + 1) * 128],
                                idt_sb[:, :]),
                                inc="p2_pe" if jc == 7 else None)
                        EM.mark(("g1", tc))
                        if P2STOP < 3:
                            continue
                        EM.wait("p2_v", ("lnfm", tc))
                        if tc == 0:
                            EM.wait("p2_v", "wog")
                        if tc >= 1:
                            EM.wait("p2_v", ("r1", tc - 1))
                        for half in range(2):
                            for jc in range(8):
                                EM.op(lambda jc=jc, half=half: t.matmul(
                                    o_ps[:, half * 512:(half + 1) * 512],
                                    lnfm_sb[:, jc, :],
                                    wo_sb[:, jc, half * 512:(half + 1) * 512],
                                    start=(jc == 0), stop=False))
                            EM.op(lambda half=half: t.matmul(
                                o_ps[:, half * 512:(half + 1) * 512],
                                ones_sb[0:1, :],
                                ob_sb[0:1, half * 512:(half + 1) * 512],
                                start=False, stop=True),
                                inc="p2_pe" if half == 1 else None)
                        EM.mark(("g2", tc))
                        if P2STOP < 4:
                            continue
                        EM.wait("p2_v", ("xftm", tc))
                        if tc >= 1:
                            EM.wait("p2_v", ("xffm", tc - 1))
                        for jc in range(8):
                            EM.op(lambda jc=jc: t.transpose(
                                bfT_ps[:, jc * 128:(jc + 1) * 128],
                                xftm_sb[:, jc * 128:(jc + 1) * 128],
                                idt_sb[:, :]),
                                inc="p2_pe" if jc == 7 else None)
                        EM.mark(("g3", tc))
                        if P2STOP < 5:
                            continue
                        EM.wait("p2_v", ("xffm", tc))
                        if tc >= 1:
                            EM.wait("p2_v", ("wall", tc - 1))
                        for jc in range(8):
                            EM.op(lambda jc=jc, tc=tc: t.matmul(
                                lg_ps[:, :],
                                xf_fm[:, jc, tc * 128:(tc + 1) * 128],
                                rout_sb[:, jc, :],
                                start=(jc == 0), stop=(jc == 7)),
                                inc="p2_pe" if jc == 7 else None)
                        EM.mark(("g4", tc))
                        EM.wait("p2_s", ("wbf", tc))
                        if tc >= 1:
                            EM.wait("p2_v", ("xffm", tc))
                        EM.op(lambda: t.transpose(
                            bfT_ps[0:8, 0:128], wbf_sb[:, :], idt_sb[:, :]),
                            inc="p2_pe")
                        EM.mark(("g5", tc))

                def ln_stats_s(sc, EMx, y_ap, base, mark_st, mark_rs, tc):
                    EMx.op(lambda: sc.activation(
                        sdump_sb[:, :], y_ap, AF.Copy,
                        accum_out=stat_sb[:, base:base + 1]))
                    EMx.op(lambda: sc.activation(
                        sdump2_sb[:, :], y_ap, AF.Square,
                        accum_out=stat_sb[:, base + 1:base + 2]), inc="p2_s")
                    EMx.mark((mark_st, tc))
                    EMx.op(lambda: sc.activation(
                        stat_sb[:, base + 2:base + 3],
                        stat_sb[:, base:base + 1], AF.Copy, scale=1.0 / C))
                    EMx.op(lambda: sc.activation(
                        stat_sb[:, base + 3:base + 4],
                        stat_sb[:, base + 2:base + 3], AF.Square))
                    EMx.op(lambda: sc.activation(
                        stat_sb[:, base + 3:base + 4],
                        stat_sb[:, base + 3:base + 4], AF.Copy, scale=-1.0,
                        bias=EPS))
                    EMx.op(lambda: sc.activation(
                        stat_sb[:, base + 4:base + 5],
                        stat_sb[:, base + 1:base + 2], AF.Ln, scale=1.0 / C,
                        bias=stat_sb[:, base + 3:base + 4]))
                    EMx.op(lambda: sc.activation(
                        stat_sb[:, base + 4:base + 5],
                        stat_sb[:, base + 4:base + 5], AF.Exp, scale=-0.5),
                        inc="p2_s")
                    EMx.mark((mark_rs, tc))

                def p2_s(sc):
                    EM.eng = sc
                    EM.chain_on("c_sS")
                    for tc in range(TC):
                        if P2STOP < 2:
                            continue
                        EM.wait("p2_v", ("outn", tc))
                        if tc >= 1:
                            EM.wait("p2_v", ("ln1bf", tc - 1))
                        ln_stats_s(sc, EM, attn_sb[:, :], 0, "st1", "rstd1", tc)
                        if P2STOP < 3:
                            continue
                        EM.wait("p2_v", ("r1", tc))
                        if tc >= 1:
                            EM.wait("p2_v", ("x1", tc - 1))
                        ln_stats_s(sc, EM, r_sb[:, :], 5, "st2", "rstd2", tc)
                        if P2STOP < 4:
                            continue
                        EM.wait("p2_v", ("x1", tc))
                        if tc >= 1:
                            EM.wait("p2_v", ("xftm", tc - 1))
                        ln_stats_s(sc, EM,
                                   x1tc_sb[:, tc % 2, :], 10, "st3", "rstd3",
                                   tc)
                        if P2STOP < 5:
                            continue
                        EM.wait("p2_v", ("top", tc))
                        EM.op(lambda: sc.activation(
                            stat_sb[:, 16:17], stat_sb[:, 15:16], AF.Exp,
                            scale=-1.0))
                        EM.op(lambda: sc.activation(
                            stat_sb[:, 17:18], stat_sb[:, 16:17], AF.Copy,
                            bias=1.0), inc="p2_s")
                        EM.mark(("sneg", tc))
                        EM.wait("p2_v", ("wall", tc))
                        EM.op(lambda: sc.activation(
                            wbf_sb[:, :], wall_sb[:, :], AF.Copy),
                            inc="p2_s")
                        EM.mark(("wbf", tc))

                def p2_v(v):
                    EM.eng = v
                    EM.chain_on("c_sV")
                    EM.wait("p2_pe", "obias_mm")
                    EM.op(lambda: v.tensor_tensor(
                        ob_sb[0:1, :], bo_sb[0:1, :], o_ps[0:1, :], ALU.add))
                    for jc in range(8):
                        EM.op(lambda jc=jc: v.tensor_scalar(
                            wo_sb[:, jc, :], wo_sb[:, jc, :],
                            lngf_sb[:, jc:jc + 1], None, ALU.mult),
                            inc="p2_v" if jc == 7 else None)
                    EM.mark("wog")
                    for tc in range(TC):
                        EM.wait("p2_pe", ("g0", tc))
                        if tc >= 1:
                            EM.wait("p2_s", ("st1", tc - 1))
                        EM.op(lambda: v.reciprocal(
                            rzq_sb[:, :], small_ps[:, 0:16]), inc="p2_v")
                        EM.mark(("rzq", tc))
                        EM.selfwait("p2_v")
                        for h in range(H):
                            EM.op(lambda h=h: v.tensor_single_scalar(
                                attn_sb[:, h * 64:(h + 1) * 64],
                                out_ps[:, h * 64:(h + 1) * 64],
                                rzq_sb[:, h:h + 1], ALU.mult),
                                inc="p2_v" if h == H - 1 else None)
                        EM.mark(("outn", tc))
                        if P2STOP < 2:
                            EM.op(lambda tc=tc: v.tensor_copy(
                                x1tc_sb[:, tc % 2, :], attn_sb[:, :]),
                                inc="p2_v")
                            EM.mark(("x1", tc))
                            continue
                        EM.wait("p2_s", ("rstd1", tc))
                        if tc >= 1:
                            EM.wait("p2_pe", ("g1", tc - 1))
                        EM.op(lambda: v.tensor_scalar(
                            ln1_bf[:, :], attn_sb[:, :], stat_sb[:, 2:3],
                            stat_sb[:, 4:5], ALU.subtract, ALU.mult),
                            inc="p2_v")
                        EM.mark(("ln1bf", tc))
                        EM.wait("p2_pe", ("g1", tc))
                        if tc >= 1:
                            EM.wait("p2_pe", ("g2", tc - 1))
                        EM.op(lambda: v.tensor_copy(
                            lnfm_sb[:].rearrange("p a b -> p (a b)"),
                            bfT_ps[:, :]), inc="p2_v")
                        EM.mark(("lnfm", tc))
                        if P2STOP < 3:
                            EM.op(lambda tc=tc: v.tensor_copy(
                                x1tc_sb[:, tc % 2, :], attn_sb[:, :]),
                                inc="p2_v")
                            EM.mark(("x1", tc))
                            continue
                        EM.wait("p2_pe", ("g2", tc))
                        EM.wait("sXa" if tc % 2 == 0 else "sXb", ("xload", tc))
                        if tc >= 1:
                            EM.wait("p2_s", ("st2", tc - 1))
                        EM.op(lambda tc=tc: v.tensor_tensor(
                            r_sb[:, :], x_sl[:, tc % 2, :], o_ps[:, :],
                            ALU.add), inc="p2_v")
                        EM.mark(("r1", tc))
                        EM.wait("p2_s", ("rstd2", tc))
                        if tc >= 2:
                            EM.wait("p2_s", ("st3", tc - 2))
                            EM.wait("sSa" if tc % 2 == 0 else "sSb",
                                    ("spill", tc - 2))
                        EM.op(lambda tc=tc: v.tensor_scalar(
                            x1tc_sb[:, tc % 2, :], r_sb[:, :],
                            stat_sb[:, 7:8], stat_sb[:, 9:10],
                            ALU.subtract, ALU.mult), inc="p2_v")
                        EM.mark(("x1", tc))
                        if P2STOP < 4:
                            continue
                        EM.wait("p2_s", ("rstd3", tc))
                        EM.op(lambda tc=tc: v.tensor_scalar(
                            scr_sb[:, :], x1tc_sb[:, tc % 2, :],
                            stat_sb[:, 12:13], stat_sb[:, 14:15],
                            ALU.subtract, ALU.mult))
                        EM.op(lambda: v.tensor_tensor(
                            scr2_sb[:, :], scr_sb[:, :], fw_sb[:, :],
                            ALU.mult))
                        if tc >= 1:
                            EM.wait("p2_pe", ("g3", tc - 1))
                        EM.op(lambda: v.tensor_tensor(
                            xftm_sb[:, :], scr2_sb[:, :], fb_sb[:, :],
                            ALU.add), inc="p2_v")
                        EM.mark(("xftm", tc))
                        EM.wait("p2_pe", ("g3", tc))
                        EM.op(lambda tc=tc: v.tensor_copy(
                            xf_fm[:, :, tc * 128:(tc + 1) * 128],
                            bfT_ps[:, :].rearrange("p (a b) -> p a b", b=128)),
                            inc="p2_v")
                        EM.mark(("xffm", tc))
                        if P2STOP < 5:
                            continue
                        EM.wait("p2_pe", ("g4", tc))
                        if tc >= 1:
                            EM.wait("p2_s", ("sneg", tc - 1))
                        EM.op(lambda: v.reduce_max(
                            stat_sb[:, 15:16], lg_ps[:, :], axis=AX.X),
                            inc="p2_v")
                        EM.mark(("top", tc))
                        if tc >= 1:
                            EM.wait("p2_s", ("wbf", tc - 1))
                        EM.selfwait("p2_v")
                        EM.op(lambda: v.tensor_single_scalar(
                            wall_sb[:, :], lg_ps[:, :], stat_sb[:, 15:16],
                            ALU.is_equal), inc="c_sV")
                        EM.wait("p2_s", ("sneg", tc))
                        EM.op(lambda: v.reciprocal(
                            stat_sb[:, 18:19], stat_sb[:, 17:18]),
                            inc="p2_v")
                        EM.wait("p2_v", value=EM.cnt.get("p2_v", 0))
                        EM.op(lambda: v.tensor_single_scalar(
                            wall_sb[:, :], wall_sb[:, :], stat_sb[:, 18:19],
                            ALU.mult), inc="p2_v")
                        EM.mark(("wall", tc))
                        EM.wait("p2_pe", ("g5", tc))
                        EM.op(lambda tc=tc: v.tensor_copy(
                            wstage[0:8, tc * 128:(tc + 1) * 128],
                            bfT_ps[0:8, 0:128]), inc="p2_v")
                        EM.mark(("wst", tc))

                def p2_sync(s):
                    EM.eng = s
                    EM.chain_off()
                    EM.wait("c_k", value=TC)
                    EM.wait("c_v", value=TC)
                    EM.wait("sColl", "state_done")
                    EM.dma(lambda: s.dma_start(out=x_sl[:, 0, :],
                                               in_=x_d[:, 0, :]), "sXa")
                    EM.dma(lambda: s.dma_start(out=x_sl[:, 1, :],
                                               in_=x_d[:, 1, :]), "sXb")
                    EM.mark(("xload", 0))
                    EM.mark(("xload", 1))
                    for tc in range(TC):
                        EM.wait("p2_v", ("x1", tc))
                        EM.dma(lambda tc=tc: s.dma_start(
                            out=x1sp_d[:, tc, :], in_=x1tc_sb[:, tc % 2, :]),
                            "sSa" if tc % 2 == 0 else "sSb")
                        EM.mark(("spill", tc))
                        if tc + 2 < TC:
                            EM.wait("p2_v", ("r1", tc))
                            EM.dma(lambda tc=tc: s.dma_start(
                                out=x_sl[:, tc % 2, :], in_=x_d[:, tc + 2, :]),
                                "sXa" if tc % 2 == 0 else "sXb")
                            EM.mark(("xload", tc + 2))
                    EM.mark("spill_all")

                def p2_gp(g):
                    EM.eng = g
                    EM.chain_off()
                    EM.wait("c_q", value=32)
                    EM.dma(lambda: g.dma_start(out=wo_sb[:], in_=woT_d), "sW")
                    EM.mark("woT_g")

                if "2" not in PHASES:
                    def p1_tail(s):
                        EM.eng = s
                        EM.wait("sColl", "state_done")
                        EM.wait("c_eq", value=32)
                    if EM.real:
                        block.sync(p1_tail)
                    else:
                        p1_tail(None)
                elif EM.real:
                    block.gpsimd(p2_gp); block.tensor(p2_pe)
                    block.scalar(p2_s); block.vector(p2_v)
                    block.sync(p2_sync)
                else:
                    p2_gp(None); p2_pe(None); p2_s(None); p2_v(None)
                    p2_sync(None)

        def p3_program(EM, scope):
            e2 = scope.enter_context
            x1t_sb = e2(nc.sbuf_tensor(_PFX[0] + "x1t_sb", [128, TCC, C], F32))
            d_sb = e2(nc.sbuf_tensor(_PFX[0] + "d_sb", [128, 2, M, C], BF16))
            h_sb = e2(nc.sbuf_tensor(_PFX[0] + "h_sb", [128, M, TTW], BF16))
            xe_sb = e2(nc.sbuf_tensor(_PFX[0] + "xe_sb", [128, 8, TTW], BF16))
            g_sb = e2(nc.sbuf_tensor(_PFX[0] + "g_sb", [128, 3, 8, 128], BF16))
            u_sb = e2(nc.sbuf_tensor(_PFX[0] + "u_sb", [128, 3, 8, 128], BF16))
            sa_sb = e2(nc.sbuf_tensor(_PFX[0] + "sa_sb", [128, 2, 512], BF16))
            wbc_sb = e2(nc.sbuf_tensor(_PFX[0] + "wbc_sb", [128, TTW], BF16))
            st4_sb = e2(nc.sbuf_tensor(_PFX[0] + "st4_sb", [128, 8], F32))
            sd4a_sb = e2(nc.sbuf_tensor(_PFX[0] + "sd4a_sb", [128, C], BF16))
            sd4b_sb = sd4a_sb

            with ExitStack() as ps3:
                e3 = ps3.enter_context
                a_ps = [e3(nc.psum_tensor(_PFX[0] + f"a_ps{k}", [128, 512], F32))
                        for k in range(2)]
                b_ps = [e3(nc.psum_tensor(_PFX[0] + f"b_ps{k}", [128, 512], F32))
                        for k in range(2)]
                dn_ps = [e3(nc.psum_tensor(_PFX[0] + f"dn_ps{k}", [128, C], F32))
                         for k in range(2)]

                NPH = NTT * (E + 1)

                def ph_te(ph):
                    return divmod(ph, E + 1)

                def rix(ph):
                    tt, eidx = ph_te(ph)
                    return tt * E + eidx

                def p3_gp(g):
                    EM.eng = g
                    EM.chain_off()
                    EM.wait("p2_v", ("wst", TC - 1))
                    EM.wait("p2_s", ("wbf", TC - 1))
                    EM.wait("p2_pe", ("g5", TC - 1))
                    EM.wait("sSa", "spill_all")
                    EM.wait("sSb", "spill_all")
                    for ph in range(NPH):
                        tt, eidx = ph_te(ph)
                        if ph >= 2:
                            EM.wait("c_dn", value=(ph - 1) * TCC)
                        for m in range(M):
                            dsrc = shd_d[m] if eidx == 0 else down_d[eidx - 1, m]
                            EM.dma(lambda dsrc=dsrc, m=m, ph=ph: g.dma_start(
                                out=d_sb[:, ph % 2, m], in_=dsrc),
                                "sDa" if ph % 2 == 0 else "sDb")
                        EM.mark(("dld", ph))
                        for m in range(M):
                            i = ph * M + m
                            if i >= 3:
                                EM.wait("c_b", value=2 * (i - 2))
                            gsrc = shg_d[m] if eidx == 0 else gate_d[eidx - 1, m]
                            usrc = shu_d[m] if eidx == 0 else up_d[eidx - 1, m]
                            EM.dma(lambda gsrc=gsrc, i=i: g.dma_start(
                                out=g_sb[:, i % 3], in_=gsrc), SG[i % 3])
                            EM.dma(lambda usrc=usrc, i=i: g.dma_start(
                                out=u_sb[:, i % 3], in_=usrc), SG[i % 3])

                def p3_sync(s):
                    EM.eng = s
                    EM.chain_off()
                    EM.wait("sSa", "spill_all")
                    EM.wait("sSb", "spill_all")
                    for tt in range(NTT):
                        if tt >= 1:
                            EM.wait("c_ln4v", ("ln4", tt * TCC - 1))
                        if tt >= 1:
                            EM.wait("sOa", value=16 * TCC)
                            EM.wait("sOb", value=16 * TCC)
                        EM.dma(lambda tt=tt: s.dma_start(
                            out=x1t_sb[:],
                            in_=x1sp_d[:, tt * TCC:(tt + 1) * TCC, :]), "sXl")
                        EM.mark(("x1ld", tt))
                        for tcc in range(TCC):
                            tti = tt * TCC + tcc
                            EM.wait("c_ln4v", ("ln4", tti))
                            EM.dma(lambda tt=tt, tcc=tcc: s.dma_start(
                                out=out_d[:, tt * TCC + tcc, 0:512],
                                in_=x1t_sb[:, tcc, 0:512]), "sOa")
                            EM.dma(lambda tt=tt, tcc=tcc: s.dma_start(
                                out=out_d[:, tt * TCC + tcc, 512:1024],
                                in_=x1t_sb[:, tcc, 512:1024]), "sOb")
                            EM.mark(("ost", tti))
                    EM.wait("sOa", value=16 * 2 * TCC)
                    EM.wait("sOb", value=16 * 2 * TCC)

                def p3_pe(t):
                    EM.eng = t
                    EM.chain_off()
                    EM.wait("p2_v", ("xffm", TC - 1))
                    EM.wait("p2_v", ("wst", TC - 1))
                    EM.wait("p2_s", ("wbf", TC - 1))
                    for ph in range(NPH):
                        tt, eidx = ph_te(ph)
                        if eidx > 0:
                            EM.wait("c_xe", value=rix(ph))
                        for m in range(M):
                            i = ph * M + m
                            EM.wait(SG[i % 3], value=32 * (i // 3 + 1))
                            sl0 = 2 * i
                            if eidx > 0 and m == 0:
                                EM.wait("c_wbcV", value=rix(ph))
                            elif sl0 >= 2:
                                EM.wait("c_sa", value=sl0)
                            if sl0 >= 2:
                                EM.wait("c_h", value=sl0)
                            rhs = xf_fm if eidx == 0 else xe_sb
                            roff = tt * TTW if eidx == 0 else 0
                            for jc in range(8):
                                for th in range(2):
                                    EM.op(lambda jc=jc, i=i, th=th, rhs=rhs,
                                          roff=roff: t.matmul(
                                        a_ps[th][:, :],
                                        g_sb[:, i % 3, jc, :],
                                        rhs[:, jc,
                                            roff + th * 512:roff + (th + 1) * 512],
                                        start=(jc == 0), stop=(jc == 7)),
                                        inc="c_a" if jc == 7 else None)
                                for th in range(2):
                                    EM.op(lambda jc=jc, i=i, th=th, rhs=rhs,
                                          roff=roff: t.matmul(
                                        b_ps[th][:, :],
                                        u_sb[:, i % 3, jc, :],
                                        rhs[:, jc,
                                            roff + th * 512:roff + (th + 1) * 512],
                                        start=(jc == 0), stop=(jc == 7)),
                                        inc="c_b" if jc == 7 else None)
                        if ph + 1 < NPH and ph_te(ph + 1)[1] > 0:
                            ntt, neidx = ph_te(ph + 1)
                            EM.wait("c_sa", value=2 * (ph + 1) * M)
                            EM.wait("p2_v", ("wst", TC - 1))
                            for half in range(2):
                                EM.op(lambda half=half, neidx=neidx,
                                      ntt=ntt: t.matmul(
                                    a_ps[half][:, :],
                                    sel_sb[:, (neidx - 1) * 128:neidx * 128],
                                    wstage[0:8, ntt * TTW + half * 512:
                                           ntt * TTW + (half + 1) * 512],
                                    start=True, stop=True),
                                    inc="c_wbc")
                        EM.wait("c_h", value=2 * (ph * M + M))
                        EM.wait("sDa" if ph % 2 == 0 else "sDb", ("dld", ph))
                        if eidx == 0:
                            EM.wait("sXl", ("x1ld", tt))
                        for tcc in range(TCC):
                            J = ph * TCC + tcc
                            if J >= 2:
                                EM.wait("c_dnE", value=J - 1)
                            for m in range(M):
                                for half in range(2):
                                    EM.op(lambda m=m, half=half, tcc=tcc,
                                          ph=ph: t.matmul(
                                        dn_ps[tcc % 2][:,
                                                       half * 512:(half + 1) * 512],
                                        h_sb[:, m,
                                             tcc * 128:(tcc + 1) * 128],
                                        d_sb[:, ph % 2, m,
                                             half * 512:(half + 1) * 512],
                                        start=(m == 0), stop=(m == M - 1)),
                                        inc="c_dn"
                                        if (m == M - 1 and half == 1) else None)
                            EM.mark(("dn", J))

                def p3_s(sc):
                    EM.eng = sc
                    EM.chain_on("c_sS")
                    for ph in range(NPH):
                        tt, eidx = ph_te(ph)
                        for m in range(M):
                            i = ph * M + m
                            for th in range(2):
                                sl = i * 2 + th
                                EM.wait("c_a", value=sl + 1)
                                if sl >= 2:
                                    EM.wait("c_h", value=sl - 1)
                                EM.op(lambda sl=sl: sc.activation(
                                    sa_sb[:, sl % 2, :], a_ps[sl % 2][:, :],
                                    AF.Silu), inc="c_sa")
                        if eidx == E:
                            for tcc in range(TCC):
                                tti = tt * TCC + tcc
                                EM.wait("c_dnE", value=ph * TCC + tcc + 1)
                                if tti >= 1:
                                    EM.wait("c_ln4v", ("ln4", tti - 1))
                                EM.op(lambda tcc=tcc: sc.activation(
                                    sd4a_sb[:, :], x1t_sb[:, tcc, :], AF.Copy,
                                    accum_out=st4_sb[:, 0:1]))
                                EM.op(lambda tcc=tcc: sc.activation(
                                    sd4b_sb[:, :], x1t_sb[:, tcc, :],
                                    AF.Square, accum_out=st4_sb[:, 1:2]))
                                EM.op(lambda: sc.activation(
                                    st4_sb[:, 2:3], st4_sb[:, 0:1], AF.Copy,
                                    scale=1.0 / C))
                                EM.op(lambda: sc.activation(
                                    st4_sb[:, 3:4], st4_sb[:, 2:3],
                                    AF.Square))
                                EM.op(lambda: sc.activation(
                                    st4_sb[:, 3:4], st4_sb[:, 3:4], AF.Copy,
                                    scale=-1.0, bias=EPS))
                                EM.op(lambda: sc.activation(
                                    st4_sb[:, 4:5], st4_sb[:, 1:2], AF.Ln,
                                    scale=1.0 / C, bias=st4_sb[:, 3:4]))
                                EM.op(lambda: sc.activation(
                                    st4_sb[:, 4:5], st4_sb[:, 4:5], AF.Exp,
                                    scale=-0.5), inc="c_ln4s")
                                EM.mark(("rstd4", tti))

                def p3_v(v):
                    EM.eng = v
                    EM.chain_on("c_sV")
                    for ph in range(NPH):
                        tt, eidx = ph_te(ph)
                        if eidx == 0:
                            EM.wait("sXl", ("x1ld", tt))
                            for tcc in range(TCC):
                                EM.op(lambda tcc=tcc: v.tensor_scalar(
                                    x1t_sb[:, tcc, :], x1t_sb[:, tcc, :],
                                    2.0, None, ALU.mult))
                            EM.op(lambda: v.engine_nop(), inc="c_dnE", k=0)                                 if False else None
                        if eidx > 0:
                            EM.wait("c_wbc", value=2 * rix(ph))
                            EM.op(lambda: v.tensor_copy(
                                wbc_sb[:, 0:512], a_ps[0][:, :]))
                            EM.op(lambda: v.tensor_copy(
                                wbc_sb[:, 512:1024], a_ps[1][:, :]),
                                inc="c_wbcV")
                            if ph >= 2:
                                EM.wait("c_b", value=2 * ph * M)
                            for jc in range(8):
                                EM.op(lambda jc=jc, tt=tt: v.tensor_tensor(
                                    xe_sb[:, jc, :],
                                    xf_fm[:, jc, tt * TTW:(tt + 1) * TTW],
                                    wbc_sb[:, :], ALU.mult),
                                    inc="c_xe" if jc == 7 else None)
                        for m in range(M):
                            i = ph * M + m
                            for th in range(2):
                                sl = i * 2 + th
                                EM.wait("c_sa", value=sl + 1)
                                EM.wait("c_b", value=sl + 1)
                                if m == 0 and th == 0 and ph >= 1:
                                    EM.wait("c_dn", value=ph * TCC)
                                EM.op(lambda m=m, th=th, sl=sl:
                                      v.tensor_tensor(
                                    h_sb[:, m, th * 512:(th + 1) * 512],
                                    sa_sb[:, sl % 2, :], b_ps[sl % 2][:, :],
                                    ALU.mult), inc="c_h")
                        for tcc in range(TCC):
                            J = ph * TCC + tcc
                            EM.wait("c_dn", value=J + 1)
                            EM.op(lambda tcc=tcc: v.tensor_tensor(
                                x1t_sb[:, tcc, :], x1t_sb[:, tcc, :],
                                dn_ps[tcc % 2][:, :], ALU.add), inc="c_dnE")
                        if eidx == E:
                            for tcc in range(TCC):
                                tti = tt * TCC + tcc
                                EM.wait("c_ln4s", ("rstd4", tti))
                                EM.op(lambda tcc=tcc: v.tensor_scalar(
                                    x1t_sb[:, tcc, :], x1t_sb[:, tcc, :],
                                    st4_sb[:, 2:3], st4_sb[:, 4:5],
                                    ALU.subtract, ALU.mult), inc="c_ln4v")
                                EM.mark(("ln4", tti))

                if "3" not in PHASES:
                    def p2_tail(s):
                        EM.eng = s
                        if "2" in PHASES:
                            EM.wait("sXS", "spill_all")
                    if EM.real and "2" in PHASES:
                        block.sync(p2_tail)
                    elif "2" in PHASES:
                        p2_tail(None)
                elif EM.real:
                    block.gpsimd(p3_gp); block.tensor(p3_pe)
                    block.scalar(p3_s); block.vector(p3_v)
                    block.sync(p3_sync)
                else:
                    p3_gp(None); p3_s(None); p3_v(None)
                    p3_pe(None); p3_sync(None)

        _PFX[0] = "dry_"
        with ExitStack() as dry1:
            p12_program(EM, dry1)
        with ExitStack() as dry2:
            p3_program(EM, dry2)
        EM.reset()
        _PFX[0] = ""
        EM.real = True
        with ExitStack() as real1:
            p12_program(EM, real1)
        with ExitStack() as real2:
            p3_program(EM, real2)

    return nc


_NC_CACHE = None


def _host_layouts(inputs):
    f32 = np.float32

    def wT(a):
        return np.ascontiguousarray(np.asarray(a, f32).T) \
            .reshape(8, 128, C).transpose(1, 0, 2).copy()

    shared = dict(
        wqT=wT(inputs["L0_attn_Wq"]),
        wkT=wT(inputs["L0_attn_Wk"]),
        wvT=wT(inputs["L0_attn_Wv"]),
        woT=wT(inputs["L0_attn_Wo"]),
        bq=np.asarray(inputs["L0_attn_bq"], f32).reshape(8, 128).T.copy(),
        bk_row=np.asarray(inputs["L0_attn_bk"], f32).reshape(1, C).copy(),
        bv_row=np.asarray(inputs["L0_attn_bv"], f32).reshape(1, C).copy(),
        bo_row=np.asarray(inputs["L0_attn_bo"], f32).reshape(1, C).copy(),
        lng_fm=np.asarray(inputs["L0_attn_ln_g"], f32)
            .reshape(8, 128).T.copy(),
        lnb_fm=np.asarray(inputs["L0_attn_ln_b"], f32)
            .reshape(8, 128).T.copy(),
        fw=np.ascontiguousarray(np.broadcast_to(
            np.asarray(inputs["L0_ffn_ln_w"], f32), (128, C))),
        fb=np.ascontiguousarray(np.broadcast_to(
            np.asarray(inputs["L0_ffn_ln_b"], f32), (128, C))),
        router=np.asarray(inputs["L0_ffn_router_w"], f32)
            .reshape(8, 128, E).transpose(1, 0, 2).copy(),
        gate=np.asarray(inputs["L0_ffn_gate_w"], f32)
            .reshape(E, 8, 128, M, 128).transpose(0, 3, 2, 1, 4).copy(),
        up=np.asarray(inputs["L0_ffn_up_w"], f32)
            .reshape(E, 8, 128, M, 128).transpose(0, 3, 2, 1, 4).copy(),
        down=np.asarray(inputs["L0_ffn_down_w"], f32)
            .reshape(E, M, 128, C).copy(),
        shg=np.asarray(inputs["L0_ffn_sh_gate"], f32)
            .reshape(8, 128, M, 128).transpose(2, 1, 0, 3).copy(),
        shu=np.asarray(inputs["L0_ffn_sh_up"], f32)
            .reshape(8, 128, M, 128).transpose(2, 1, 0, 3).copy(),
        shd=np.asarray(inputs["L0_ffn_sh_down"], f32)
            .reshape(M, 128, C).copy(),
        ones_bf=np.ones((1, 128), ml_dtypes.bfloat16),
        onescol_bf=np.ones((128, 1), ml_dtypes.bfloat16),
        idt_bf=np.eye(128).astype(ml_dtypes.bfloat16),
        eps_col=np.full((128, 1), EPS, f32),
    )
    ones2 = np.zeros((128, 2), f32)
    ones2[0:64, 0] = 1.0
    ones2[64:128, 1] = 1.0
    shared["ones2_bf"] = ones2.astype(ml_dtypes.bfloat16)
    sel = np.zeros((8, E * 128), f32)
    for e in range(E):
        sel[e, e * 128:(e + 1) * 128] = 1.0
    shared["sel_bf"] = sel.astype(ml_dtypes.bfloat16)
    return shared


def kernel(**inputs):
    global _NC_CACHE
    x = np.asarray(inputs["L0_attn_x"], np.float32)
    shared = _host_layouts(inputs)

    in_maps = []
    for c in range(NCORES):
        own = np.concatenate([x[0, c * 1024:(c + 1) * 1024],
                              x[1, c * 1024:(c + 1) * 1024]], 0)
        x_tm = own.reshape(TC, 128, C).transpose(1, 0, 2).copy()
        xT = np.ascontiguousarray(own.T).reshape(8, 128, TOK) \
            .transpose(1, 0, 2).copy()
        in_maps.append(dict(shared, x_tm=x_tm, xT=xT))

    if _NC_CACHE is None:
        _NC_CACHE = build_nc()
    res = run_bass_kernel_spmd(_NC_CACHE, in_maps, core_ids=list(range(NCORES)))

    out = np.empty((B, N, C), np.float32)
    for c in range(NCORES):
        o = np.asarray(res.results[c]["out"]).transpose(1, 0, 2).reshape(TOK, C)
        out[0, c * 1024:(c + 1) * 1024] = o[0:1024]
        out[1, c * 1024:(c + 1) * 1024] = o[1024:2048]
    return out


# version 12
# speedup vs baseline: 1.1979x; 1.1979x over previous
import numpy as np
import ml_dtypes
import concourse.bass as bass
import concourse.mybir as mybir
from concourse.bass_utils import run_bass_kernel_spmd
from contextlib import ExitStack

F32 = mybir.dt.float32
BF16 = mybir.dt.bfloat16
AF = mybir.ActivationFunctionType
ALU = mybir.AluOpType
AX = mybir.AxisListType

NCORES = 8
B = 2
N = 8192
C = 1024
H = 16
E = 8
M = 16
TOK = 2048
TC = 16
TCB = 8
NTT = 2
TTW = 1024
TCC = 8
EPS = 1e-5
DEBUG = False
PHASES = "123"
P2STOP = 9


class Emit:

    def __init__(self):
        self.real = False
        self.cnt = {}
        self.marks = {}
        self.sems = {}
        self.eng = None
        self.chain = None
        self.last = None

    def op(self, f, inc=None, k=1):
        if self.chain is not None:
            if self.last is not None:
                self.wait(self.last[0], value=self.last[1])
            if inc is None:
                inc = self.chain
        inst = f() if self.real else None
        if inc is not None:
            if self.real:
                inst.then_inc(self.sems[inc], k)
            self.cnt[inc] = self.cnt.get(inc, 0) + k
            if self.chain is not None:
                self.last = (inc, self.cnt[inc])
        return inst

    def chain_on(self, sem):
        self.chain = sem
        self.last = None

    def chain_off(self):
        self.chain = None
        self.last = None

    def dma(self, f, inc):
        return self.op(f, inc, 16)

    def mark(self, name):
        if not self.real:
            self.marks[name] = dict(self.cnt)

    def wait(self, sem, mark_name=None, value=None):
        if value is None:
            if mark_name not in self.marks:
                return
            value = self.marks[mark_name].get(sem, 0)
        if value <= 0:
            return
        if self.real:
            self.eng.wait_ge(self.sems[sem], value)

    def selfwait(self, sem):
        self.wait(sem, value=self.cnt.get(sem, 0))

    def reset(self):
        self.cnt = {}


SEM_NAMES = [
    "sCst", "sW", "sXa", "sXb", "sSa", "sSb", "sOa", "sOb", "sColl", "sXl", "sDa", "sDb", "sG0", "sG1", "sG2",
    "sWk", "sWv", "sWq", "sBkv", "sXg0", "sXg1", "sXg2", "sXg3",
    "c_q", "c_eq", "c_k", "c_ek", "c_v", "c_va", "c_U", "c_Ue", "c_stA",
    "p2_pe", "p2_v", "p2_s", "sCC",
    "c_xe", "c_a", "c_b", "c_sa", "c_h", "c_dn", "c_dnE",
    "c_wbc", "c_wbcV", "c_ln4s", "c_ln4v", "c_sV", "c_sS",
]
SG = ["sG0", "sG1", "sG2"]
_PFX = [""]


def build_nc():
    nc = bass.Bass("TRN2", target_bir_lowering=False)

    d = {}
    def din(name, shape, dt=F32):
        d[name] = nc.dram_tensor(name, list(shape), dt, kind="ExternalInput").ap()
        return d[name]

    xT_d = din("xT", [128, 8, TOK])
    x_d = din("x_tm", [128, TC, C])
    wqT_d = din("wqT", [128, 8, C])
    wkT_d = din("wkT", [128, 8, C])
    wvT_d = din("wvT", [128, 8, C])
    woT_d = din("woT", [128, 8, C])
    bq_d = din("bq", [128, 8])
    bk_d = din("bk_row", [1, C])
    bv_d = din("bv_row", [1, C])
    bo_d = din("bo_row", [1, C])
    lngf_d = din("lng_fm", [128, 8])
    lnbf_d = din("lnb_fm", [128, 8])
    fw_d = din("fw", [128, C])
    fb_d = din("fb", [128, C])
    rout_d = din("router", [128, 8, E])
    gate_d = din("gate", [E, M, 128, 8, 128])
    up_d = din("up", [E, M, 128, 8, 128])
    down_d = din("down", [E, M, 128, C])
    shg_d = din("shg", [M, 128, 8, 128])
    shu_d = din("shu", [M, 128, 8, 128])
    shd_d = din("shd", [M, 128, C])
    ones_d = din("ones_bf", [1, 128], BF16)
    onesc_d = din("onescol_bf", [128, 1], BF16)
    ones2_d = din("ones2_bf", [128, 2], BF16)
    idt_d = din("idt_bf", [128, 128], BF16)
    sel_d = din("sel_bf", [8, E * 128], BF16)
    eps_d = din("eps_col", [128, 1])

    out_d = nc.dram_tensor("out", [128, TC, C], F32, kind="ExternalOutput").ap()
    kd = "ExternalOutput" if DEBUG else "Internal"
    x1sp_d = nc.dram_tensor("x1_spill", [128, TC, C], F32, kind=kd).ap()
    cc_in = nc.dram_tensor("cc_in", [64, 2080], F32, kind="Internal").ap()
    cc_out = nc.dram_tensor("cc_out", [64, 2080], F32, kind="Internal").ap()
    if DEBUG:
        dbg_ccin = nc.dram_tensor("dbg_ccin", [64, 2080], F32,
                                  kind="ExternalOutput").ap()
        dbg_ccout = nc.dram_tensor("dbg_ccout", [64, 2080], F32,
                                   kind="ExternalOutput").ap()
        dbg_state = nc.dram_tensor("dbg_state", [128, B, H, 64], F32,
                                   kind="ExternalOutput").ap()
        dbg_eq = nc.dram_tensor("dbg_eq", [128, 8, TOK], F32,
                                kind="ExternalOutput").ap()

    EM = Emit()

    with ExitStack() as st:
        e_ = st.enter_context

        xf_fm = e_(nc.sbuf_tensor(_PFX[0] + "xf_fm", [128, 8, TOK], BF16))
        lngf_sb = e_(nc.sbuf_tensor(_PFX[0] + "lngf_sb", [128, 8], F32))
        lnbf_sb = e_(nc.sbuf_tensor(_PFX[0] + "lnbf_sb", [128, 8], BF16))
        ob_sb = e_(nc.sbuf_tensor(_PFX[0] + "ob_sb", [1, C], BF16))
        bq_sb = e_(nc.sbuf_tensor(_PFX[0] + "bq_sb", [128, 8], F32))
        rout_sb = e_(nc.sbuf_tensor(_PFX[0] + "rout_sb", [128, 8, E], BF16))
        ones_sb = e_(nc.sbuf_tensor(_PFX[0] + "ones_sb", [1, 128], BF16))
        onesc_sb = e_(nc.sbuf_tensor(_PFX[0] + "onesc_sb", [128, 1], BF16))
        ones2_sb = e_(nc.sbuf_tensor(_PFX[0] + "ones2_sb", [128, 2], BF16))
        idt_sb = e_(nc.sbuf_tensor(_PFX[0] + "idt_sb", [128, 128], BF16))
        sel_sb = e_(nc.sbuf_tensor(_PFX[0] + "sel_sb", [8, E * 128], BF16))
        eps_sb = e_(nc.sbuf_tensor(_PFX[0] + "eps_sb", [128, 1], F32))
        wstage = e_(nc.sbuf_tensor(_PFX[0] + "wstage", [8, TOK], BF16))
        state_sb = e_(nc.sbuf_tensor(_PFX[0] + "state_sb", [128, B, H, 64], BF16))

        for s in SEM_NAMES:
            EM.sems[s] = e_(nc.semaphore(s))
        block = e_(nc.Block())

        def p12_program(EM, scope):
            e2 = scope.enter_context
            x_fm = e2(nc.sbuf_tensor(_PFX[0] + "x_fm", [128, 8, TOK], BF16))
            fw_sb = e2(nc.sbuf_tensor(_PFX[0] + "fw_sb", [128, C], F32))
            fb_sb = e2(nc.sbuf_tensor(_PFX[0] + "fb_sb", [128, C], F32))
            bk_sb = e2(nc.sbuf_tensor(_PFX[0] + "bk_sb", [1, C], BF16))
            bv_sb = e2(nc.sbuf_tensor(_PFX[0] + "bv_sb", [1, C], BF16))
            bo_sb = e2(nc.sbuf_tensor(_PFX[0] + "bo_sb", [1, C], BF16))
            wq_sb = e2(nc.sbuf_tensor(_PFX[0] + "wq_sb", [128, 8, C], BF16))
            eq_sb = e2(nc.sbuf_tensor(_PFX[0] + "eq_sb", [128, 8, TOK], BF16))

            with ExitStack() as ps1:
                e3 = ps1.enter_context
                wk_sb = e3(nc.sbuf_tensor(_PFX[0] + "wk_sb", [128, 8, C], BF16))
                wv_sb = e3(nc.sbuf_tensor(_PFX[0] + "wv_sb", [128, 8, C], BF16))
                ek_sb = e3(nc.sbuf_tensor(_PFX[0] + "ek_sb", [128, 2, C], BF16))
                v_sb = e3(nc.sbuf_tensor(_PFX[0] + "v_sb", [128, 2, C], BF16))
                ustage = e3(nc.sbuf_tensor(_PFX[0] + "ustage", [64, 2080], F32))
                stateA = e3(nc.sbuf_tensor(_PFX[0] + "stateA", [64, B * C], BF16))
                rz_sb = e3(nc.sbuf_tensor(_PFX[0] + "rz_sb", [64, 32], F32))
                state_raw = ustage
                k_ps = e3(nc.psum_tensor(_PFX[0] + "k_ps", [128, C], F32))
                v_ps = e3(nc.psum_tensor(_PFX[0] + "v_ps", [128, C], F32))
                U_ps = e3(nc.psum_tensor(_PFX[0] + "U_ps", [64, C], F32))
                z_ps = e3(nc.psum_tensor(_PFX[0] + "z_ps", [64, 32], F32))
                q_ps = k_ps

                def p1b_sync(s):
                    EM.eng = s
                    EM.chain_off()
                    for src, dst in [(ones_d, ones_sb), (onesc_d, onesc_sb),
                                     (bq_d, bq_sb), (lngf_d, lngf_sb),
                                     (fw_d, fw_sb),
                                     (fb_d, fb_sb),
                                     (ones2_d, ones2_sb),
                                     (idt_d, idt_sb), (sel_d, sel_sb),
                                     (eps_d, eps_sb)]:
                        EM.dma(lambda src=src, dst=dst:
                               s.dma_start(out=dst[:], in_=src), "sCst")
                    EM.mark("consts")

                def p1b_gp(g):
                    EM.eng = g
                    EM.chain_off()
                    EM.dma(lambda: g.dma_start(out=wk_sb[:], in_=wkT_d),
                           "sWk")
                    EM.mark("wk")
                    EM.dma(lambda: g.dma_start(
                        out=x_fm[:, :, 0:512], in_=xT_d[:, :, 0:512]), "sXg0")
                    EM.mark(("xg", 0))
                    for src, dst in [(bk_d, bk_sb), (bv_d, bv_sb)]:
                        EM.dma(lambda src=src, dst=dst:
                               g.dma_start(out=dst[:], in_=src), "sBkv")
                    EM.mark("bkv")
                    EM.dma(lambda: g.dma_start(out=wv_sb[:], in_=wvT_d),
                           "sWv")
                    EM.mark("wv")
                    for gi in range(1, 4):
                        EM.dma(lambda gi=gi: g.dma_start(
                            out=x_fm[:, :, gi * 512:(gi + 1) * 512],
                            in_=xT_d[:, :, gi * 512:(gi + 1) * 512]),
                            "sXg%d" % gi)
                        EM.mark(("xg", gi))
                    EM.dma(lambda: g.dma_start(out=wq_sb[:], in_=wqT_d),
                           "sWq")
                    EM.mark("wq")
                    for src, dst in [(bo_d, bo_sb), (rout_d, rout_sb),
                                     (lnbf_d, lnbf_sb)]:
                        EM.dma(lambda src=src, dst=dst:
                               g.dma_start(out=dst[:], in_=src), "sW")
                    EM.mark("p1w")

                def p1b_pe(t):
                    EM.eng = t
                    EM.chain_off()
                    EM.wait("sCst", "consts")
                    EM.wait("sBkv", "bkv")
                    for tc in range(TC):
                        b = tc // TCB
                        EM.wait("sXg%d" % (tc // 4), ("xg", tc // 4))
                        if tc == 0:
                            EM.wait("sWk", "wk")
                        if tc >= 1:
                            EM.wait("c_ek", value=tc)
                        for jc in range(8):
                            for half in range(2):
                                EM.op(lambda jc=jc, half=half, tc=tc: t.matmul(
                                    k_ps[:, half * 512:(half + 1) * 512],
                                    x_fm[:, jc, tc * 128:(tc + 1) * 128],
                                    wk_sb[:, jc, half * 512:(half + 1) * 512],
                                    start=(jc == 0), stop=False))
                        for half in range(2):
                            EM.op(lambda half=half: t.matmul(
                                k_ps[:, half * 512:(half + 1) * 512],
                                ones_sb[0:1, :],
                                bk_sb[0:1, half * 512:(half + 1) * 512],
                                start=False, stop=True),
                                inc="c_k" if half == 1 else None)
                        if tc == 0:
                            EM.wait("sWv", "wv")
                        if tc >= 1:
                            EM.wait("c_va", value=tc)
                        for jc in range(8):
                            for half in range(2):
                                EM.op(lambda jc=jc, half=half, tc=tc: t.matmul(
                                    v_ps[:, half * 512:(half + 1) * 512],
                                    x_fm[:, jc, tc * 128:(tc + 1) * 128],
                                    wv_sb[:, jc, half * 512:(half + 1) * 512],
                                    start=(jc == 0), stop=False))
                        for half in range(2):
                            EM.op(lambda half=half: t.matmul(
                                v_ps[:, half * 512:(half + 1) * 512],
                                ones_sb[0:1, :],
                                bv_sb[0:1, half * 512:(half + 1) * 512],
                                start=False, stop=True),
                                inc="c_v" if half == 1 else None)
                        EM.wait("c_ek", value=tc + 1)
                        EM.wait("c_va", value=tc + 1)
                        if tc == TCB:
                            EM.wait("c_Ue", value=1)
                        for h in range(H):
                            ust = (tc % TCB == 0) and (h in (0, 8))
                            usp = (tc % TCB == TCB - 1) and (h in (7, 15))
                            EM.op(lambda h=h, tc=tc, ust=ust, usp=usp: t.matmul(
                                U_ps[0:64, h * 64:(h + 1) * 64],
                                ek_sb[:, tc % 2, h * 64:(h + 1) * 64],
                                v_sb[:, tc % 2, h * 64:(h + 1) * 64],
                                start=ust, stop=usp))
                            zst = (tc == 0) and (h == 0)
                            zsp = (tc == TC - 1) and (h == H - 1)
                            EM.op(lambda h=h, b=b, zst=zst, zsp=zsp: t.matmul(
                                z_ps[0:64, b * 16 + h:b * 16 + h + 1],
                                ek_sb[:, tc % 2, h * 64:(h + 1) * 64],
                                onesc_sb[:, 0:1],
                                start=zst, stop=zsp),
                                inc="c_U" if h == H - 1 else None)

                def p1b_s(sc):
                    EM.eng = sc
                    EM.chain_on("c_sS")
                    for tc in range(TC):
                        EM.wait("c_k", value=tc + 1)
                        if tc >= 2:
                            EM.wait("c_U", value=tc - 1)
                        EM.op(lambda tc=tc: sc.activation(
                            ek_sb[:, tc % 2, :], k_ps[:, :], AF.Exp),
                            inc="c_ek")

                def p1b_v(v):
                    EM.eng = v
                    EM.chain_on("c_sV")
                    for tc in range(TC):
                        EM.wait("c_v", value=tc + 1)
                        if tc >= 2:
                            EM.wait("c_U", value=tc - 1)
                        EM.op(lambda tc=tc: v.tensor_copy(
                            v_sb[:, tc % 2, :], v_ps[:, :]), inc="c_va")
                        if tc % TCB == TCB - 1:
                            b = tc // TCB
                            EM.wait("c_U", value=tc + 1)
                            EM.op(lambda b=b: v.tensor_copy(
                                ustage[0:64, b * C:(b + 1) * C], U_ps[0:64, :]),
                                inc="c_Ue")
                    EM.op(lambda: v.tensor_copy(ustage[0:64, 2048:2080],
                                                z_ps[0:64, :]), inc="c_Ue")

                if EM.real:
                    block.sync(p1b_sync); block.gpsimd(p1b_gp)
                    block.tensor(p1b_pe); block.scalar(p1b_s)
                    block.vector(p1b_v)
                else:
                    p1b_sync(None); p1b_gp(None); p1b_pe(None)
                    p1b_s(None); p1b_v(None)

                def cc_gp(g):
                    EM.eng = g
                    EM.chain_off()
                    EM.wait("c_Ue", value=3)
                    EM.dma(lambda: g.dma_start(out=cc_in, in_=ustage[0:64, :]),
                           "sColl")
                    EM.wait("sColl", value=16)
                    EM.op(lambda: g.collective_compute(
                        "AllReduce", ALU.add,
                        replica_groups=[list(range(NCORES))],
                        ins=[cc_in.opt()], outs=[cc_out.opt()]),
                        inc="sCC", k=1)
                    EM.wait("sCC", value=1)
                    EM.dma(lambda: g.dma_start(out=state_raw[0:64, :],
                                               in_=cc_out), "sColl")
                    if DEBUG:
                        EM.dma(lambda: g.dma_start(out=dbg_ccin, in_=cc_in),
                               "sColl")
                        EM.dma(lambda: g.dma_start(out=dbg_ccout, in_=cc_out),
                               "sColl")
                    EM.mark("cc_done")

                def cc_v(v):
                    EM.eng = v
                    EM.chain_on("c_sV")
                    EM.op(lambda: v.memset(state_sb[:], 0.0))
                    EM.wait("sColl", "cc_done")
                    EM.op(lambda: v.reciprocal(rz_sb[0:64, :],
                                               state_raw[0:64, 2048:2080]),
                          inc="c_sV")
                    EM.selfwait("c_sV")
                    for b in range(B):
                        for h in range(H):
                            EM.op(lambda b=b, h=h: v.tensor_single_scalar(
                                stateA[0:64, b * C + h * 64:b * C + (h + 1) * 64],
                                state_raw[0:64, b * C + h * 64:b * C + (h + 1) * 64],
                                rz_sb[0:64, b * 16 + h:b * 16 + h + 1],
                                ALU.mult))
                    EM.op(lambda: v.engine_nop(), inc="c_stA", k=1)

                def cc_gp2(g):
                    EM.eng = g
                    EM.chain_off()
                    EM.wait("c_stA", value=1)
                    srcA = stateA[0:64, :].rearrange(
                        "p (b h2 par e) -> p par b h2 e", b=B, h2=8, par=2)
                    dstE = state_sb[0:64, :, :, :].rearrange(
                        "p b (h2 par) e -> p par b h2 e", par=2)
                    dstO = state_sb[64:128, :, :, :].rearrange(
                        "p b (h2 par) e -> p par b h2 e", par=2)
                    EM.dma(lambda: g.dma_start(
                        out=dstE[:, 0], in_=srcA[:, 0]), "sColl")
                    EM.dma(lambda: g.dma_start(
                        out=dstO[:, 1], in_=srcA[:, 1]), "sColl")
                    if DEBUG:
                        EM.wait("sColl", value=EM.cnt.get("sColl", 0))
                        EM.wait("c_eq", value=32)
                        EM.dma(lambda: g.dma_start(
                            out=dbg_state, in_=state_sb[:]), "sColl")
                        EM.dma(lambda: g.dma_start(
                            out=dbg_eq, in_=eq_sb[:]), "sColl")
                    EM.mark("state_done")

                def p1a_pe(t):
                    EM.eng = t
                    EM.chain_off()
                    EM.wait("c_ek", value=TC)
                    EM.wait("sWq", "wq")
                    for p in range(16):
                        qi0 = 2 * p
                        ic, tq0 = qi0 // 4, qi0 % 4
                        if qi0 >= 2:
                            EM.wait("c_eq", value=qi0)
                        for jc in range(8):
                            for r in range(2):
                                EM.op(lambda jc=jc, ic=ic, tq=tq0 + r, r=r:
                                      t.matmul(
                                    q_ps[:, r * 512:r * 512 + 512],
                                    wq_sb[:, jc, ic * 128:(ic + 1) * 128],
                                    x_fm[:, jc, tq * 512:(tq + 1) * 512],
                                    start=(jc == 0), stop=(jc == 7)),
                                    inc="c_q" if jc == 7 else None)

                def p1a_s(sc):
                    EM.eng = sc
                    EM.chain_on("c_sS")
                    for ic in range(8):
                        for tq in range(4):
                            qi = ic * 4 + tq
                            EM.wait("c_q", value=qi + 1)
                            EM.op(lambda ic=ic, tq=tq, qi=qi: sc.activation(
                                eq_sb[:, ic, tq * 512:(tq + 1) * 512],
                                q_ps[:, (qi % 2) * 512:(qi % 2) * 512 + 512],
                                AF.Exp,
                                bias=bq_sb[:, ic:ic + 1], scale=1.0),
                                inc="c_eq")

                if EM.real:
                    block.gpsimd(cc_gp); block.tensor(p1a_pe)
                    block.scalar(p1a_s); block.vector(cc_v)
                    block.gpsimd(cc_gp2)
                else:
                    cc_gp(None); p1a_pe(None); p1a_s(None); cc_v(None)
                    cc_gp2(None)

            wo_sb = wq_sb
            x_sl = e2(nc.sbuf_tensor(_PFX[0] + "x_sl", [128, 2, C], F32))
            attn_sb = e2(nc.sbuf_tensor(_PFX[0] + "attn_sb", [128, C], F32))
            scr_sb = e2(nc.sbuf_tensor(_PFX[0] + "scr_sb", [128, C], F32))
            scr2_sb = e2(nc.sbuf_tensor(_PFX[0] + "scr2_sb", [128, C], F32))
            sdump_sb = e2(nc.sbuf_tensor(_PFX[0] + "sdump_sb", [128, C], BF16))
            sdump2_sb = e2(nc.sbuf_tensor(_PFX[0] + "sdump2_sb", [128, C], BF16))
            ln1_bf = e2(nc.sbuf_tensor(_PFX[0] + "ln1_bf", [128, C], BF16))
            lnfm_sb = e2(nc.sbuf_tensor(_PFX[0] + "lnfm_sb", [128, 8, 128], BF16))
            r_sb = e2(nc.sbuf_tensor(_PFX[0] + "r_sb", [128, C], F32))
            x1tc_sb = e2(nc.sbuf_tensor(_PFX[0] + "x1tc_sb", [128, 2, C], F32))
            xftm_sb = e2(nc.sbuf_tensor(_PFX[0] + "xftm_sb", [128, C], BF16))
            rzq_sb = e2(nc.sbuf_tensor(_PFX[0] + "rzq_sb", [128, 16], F32))
            stat_sb = e2(nc.sbuf_tensor(_PFX[0] + "stat_sb", [128, 24], F32))
            wall_sb = e2(nc.sbuf_tensor(_PFX[0] + "wall_sb", [128, E], F32))
            wbf_sb = e2(nc.sbuf_tensor(_PFX[0] + "wbf_sb", [128, E], BF16))

            with ExitStack() as ps2:
                e3 = ps2.enter_context
                out_ps = e3(nc.psum_tensor(_PFX[0] + "out_ps", [128, C], F32))
                o_ps = e3(nc.psum_tensor(_PFX[0] + "o_ps", [128, C], F32))
                small_ps = e3(nc.psum_tensor(_PFX[0] + "small_ps", [128, 16], F32))
                lg_ps = e3(nc.psum_tensor(_PFX[0] + "lg_ps", [128, E], F32))
                bfT_ps = e3(nc.psum_tensor(_PFX[0] + "bfT_ps", [128, C], BF16))

                def p2_pe(t):
                    EM.eng = t
                    EM.chain_off()
                    EM.wait("sColl", "state_done")
                    EM.wait("sW", "p1w")
                    EM.wait("c_eq", value=32)
                    for tc in range(TC):
                        b = tc // TCB
                        if tc >= 1:
                            EM.wait("p2_v", ("outn", tc - 1))
                            EM.wait("p2_v", ("rzq", tc - 1))
                        for h in range(H):
                            EM.op(lambda h=h, b=b, tc=tc: t.matmul(
                                out_ps[:, h * 64:(h + 1) * 64],
                                eq_sb[:, h // 2, tc * 128:(tc + 1) * 128],
                                state_sb[:, b, h, :],
                                start=True, stop=True))
                        for ic in range(8):
                            EM.op(lambda ic=ic, tc=tc: t.matmul(
                                small_ps[:, 2 * ic:2 * ic + 2],
                                eq_sb[:, ic, tc * 128:(tc + 1) * 128],
                                ones2_sb[:, :], start=True, stop=True),
                                inc="p2_pe" if ic == 7 else None)
                        EM.mark(("g0", tc))
                        if tc == 0:
                            EM.wait("sW", "woT_g")
                            for half in range(2):
                                for jc in range(8):
                                    EM.op(lambda jc=jc, half=half: t.matmul(
                                        o_ps[0:1, half * 512:(half + 1) * 512],
                                        lnbf_sb[:, jc:jc + 1],
                                        wo_sb[:, jc,
                                              half * 512:(half + 1) * 512],
                                        start=(jc == 0), stop=(jc == 7)),
                                        inc="p2_pe"
                                        if (jc == 7 and half == 1) else None)
                            EM.mark("obias_mm")
                        if P2STOP < 2:
                            continue
                        EM.wait("p2_v", ("ln1bf", tc))
                        if tc >= 1:
                            EM.wait("p2_v", ("wst", tc - 1))
                        for jc in range(8):
                            EM.op(lambda jc=jc: t.transpose(
                                bfT_ps[:, jc * 128:(jc + 1) * 128],
                                ln1_bf[:, jc * 128:(jc + 1) * 128],
                                idt_sb[:, :]),
                                inc="p2_pe" if jc == 7 else None)
                        EM.mark(("g1", tc))
                        if P2STOP < 3:
                            continue
                        EM.wait("p2_v", ("lnfm", tc))
                        if tc == 0:
                            EM.wait("p2_v", "wog")
                        if tc >= 1:
                            EM.wait("p2_v", ("r1", tc - 1))
                        for jc in range(8):
                            for half in range(2):
                                EM.op(lambda jc=jc, half=half: t.matmul(
                                    o_ps[:, half * 512:(half + 1) * 512],
                                    lnfm_sb[:, jc, :],
                                    wo_sb[:, jc, half * 512:(half + 1) * 512],
                                    start=(jc == 0), stop=False))
                        for half in range(2):
                            EM.op(lambda half=half: t.matmul(
                                o_ps[:, half * 512:(half + 1) * 512],
                                ones_sb[0:1, :],
                                ob_sb[0:1, half * 512:(half + 1) * 512],
                                start=False, stop=True),
                                inc="p2_pe" if half == 1 else None)
                        EM.mark(("g2", tc))
                        if P2STOP < 4:
                            continue
                        EM.wait("p2_v", ("xftm", tc))
                        if tc >= 1:
                            EM.wait("p2_v", ("xffm", tc - 1))
                        for jc in range(8):
                            EM.op(lambda jc=jc: t.transpose(
                                bfT_ps[:, jc * 128:(jc + 1) * 128],
                                xftm_sb[:, jc * 128:(jc + 1) * 128],
                                idt_sb[:, :]),
                                inc="p2_pe" if jc == 7 else None)
                        EM.mark(("g3", tc))
                        if P2STOP < 5:
                            continue
                        EM.wait("p2_v", ("xffm", tc))
                        if tc >= 1:
                            EM.wait("p2_v", ("wall", tc - 1))
                        for jc in range(8):
                            EM.op(lambda jc=jc, tc=tc: t.matmul(
                                lg_ps[:, :],
                                xf_fm[:, jc, tc * 128:(tc + 1) * 128],
                                rout_sb[:, jc, :],
                                start=(jc == 0), stop=(jc == 7)),
                                inc="p2_pe" if jc == 7 else None)
                        EM.mark(("g4", tc))
                        EM.wait("p2_s", ("wbf", tc))
                        if tc >= 1:
                            EM.wait("p2_v", ("xffm", tc))
                        EM.op(lambda: t.transpose(
                            bfT_ps[0:8, 0:128], wbf_sb[:, :], idt_sb[:, :]),
                            inc="p2_pe")
                        EM.mark(("g5", tc))

                def ln_stats_s(sc, EMx, y_ap, base, mark_st, mark_rs, tc):
                    EMx.op(lambda: sc.activation(
                        sdump_sb[:, :], y_ap, AF.Copy,
                        accum_out=stat_sb[:, base:base + 1]))
                    EMx.op(lambda: sc.activation(
                        sdump2_sb[:, :], y_ap, AF.Square,
                        accum_out=stat_sb[:, base + 1:base + 2]), inc="p2_s")
                    EMx.mark((mark_st, tc))
                    EMx.op(lambda: sc.activation(
                        stat_sb[:, base + 2:base + 3],
                        stat_sb[:, base:base + 1], AF.Copy, scale=1.0 / C))
                    EMx.op(lambda: sc.activation(
                        stat_sb[:, base + 3:base + 4],
                        stat_sb[:, base + 2:base + 3], AF.Square))
                    EMx.op(lambda: sc.activation(
                        stat_sb[:, base + 3:base + 4],
                        stat_sb[:, base + 3:base + 4], AF.Copy, scale=-1.0,
                        bias=EPS))
                    EMx.op(lambda: sc.activation(
                        stat_sb[:, base + 4:base + 5],
                        stat_sb[:, base + 1:base + 2], AF.Ln, scale=1.0 / C,
                        bias=stat_sb[:, base + 3:base + 4]))
                    EMx.op(lambda: sc.activation(
                        stat_sb[:, base + 4:base + 5],
                        stat_sb[:, base + 4:base + 5], AF.Exp, scale=-0.5),
                        inc="p2_s")
                    EMx.mark((mark_rs, tc))

                def p2_s(sc):
                    EM.eng = sc
                    EM.chain_on("c_sS")
                    for tc in range(TC):
                        if P2STOP < 2:
                            continue
                        EM.wait("p2_v", ("outn", tc))
                        if tc >= 1:
                            EM.wait("p2_v", ("ln1bf", tc - 1))
                        ln_stats_s(sc, EM, attn_sb[:, :], 0, "st1", "rstd1", tc)
                        if P2STOP < 3:
                            continue
                        EM.wait("p2_v", ("r1", tc))
                        if tc >= 1:
                            EM.wait("p2_v", ("x1", tc - 1))
                        ln_stats_s(sc, EM, r_sb[:, :], 5, "st2", "rstd2", tc)
                        if P2STOP < 4:
                            continue
                        EM.wait("p2_v", ("x1", tc))
                        if tc >= 1:
                            EM.wait("p2_v", ("xftm", tc - 1))
                        ln_stats_s(sc, EM,
                                   x1tc_sb[:, tc % 2, :], 10, "st3", "rstd3",
                                   tc)
                        if P2STOP < 5:
                            continue
                        EM.wait("p2_v", ("top", tc))
                        EM.op(lambda: sc.activation(
                            stat_sb[:, 16:17], stat_sb[:, 15:16], AF.Exp,
                            scale=-1.0))
                        EM.op(lambda: sc.activation(
                            stat_sb[:, 17:18], stat_sb[:, 16:17], AF.Copy,
                            bias=1.0), inc="p2_s")
                        EM.mark(("sneg", tc))
                        EM.wait("p2_v", ("wall", tc))
                        EM.op(lambda: sc.activation(
                            wbf_sb[:, :], wall_sb[:, :], AF.Copy),
                            inc="p2_s")
                        EM.mark(("wbf", tc))

                def p2_v(v):
                    EM.eng = v
                    EM.chain_on("c_sV")
                    EM.wait("p2_pe", "obias_mm")
                    EM.op(lambda: v.tensor_tensor(
                        ob_sb[0:1, :], bo_sb[0:1, :], o_ps[0:1, :], ALU.add))
                    for jc in range(8):
                        EM.op(lambda jc=jc: v.tensor_scalar(
                            wo_sb[:, jc, :], wo_sb[:, jc, :],
                            lngf_sb[:, jc:jc + 1], None, ALU.mult),
                            inc="p2_v" if jc == 7 else None)
                    EM.mark("wog")
                    for tc in range(TC):
                        EM.wait("p2_pe", ("g0", tc))
                        if tc >= 1:
                            EM.wait("p2_s", ("st1", tc - 1))
                        EM.op(lambda: v.reciprocal(
                            rzq_sb[:, :], small_ps[:, 0:16]), inc="p2_v")
                        EM.mark(("rzq", tc))
                        EM.selfwait("p2_v")
                        for h in range(H):
                            EM.op(lambda h=h: v.tensor_single_scalar(
                                attn_sb[:, h * 64:(h + 1) * 64],
                                out_ps[:, h * 64:(h + 1) * 64],
                                rzq_sb[:, h:h + 1], ALU.mult),
                                inc="p2_v" if h == H - 1 else None)
                        EM.mark(("outn", tc))
                        if P2STOP < 2:
                            EM.op(lambda tc=tc: v.tensor_copy(
                                x1tc_sb[:, tc % 2, :], attn_sb[:, :]),
                                inc="p2_v")
                            EM.mark(("x1", tc))
                            continue
                        EM.wait("p2_s", ("rstd1", tc))
                        if tc >= 1:
                            EM.wait("p2_pe", ("g1", tc - 1))
                        EM.op(lambda: v.tensor_scalar(
                            ln1_bf[:, :], attn_sb[:, :], stat_sb[:, 2:3],
                            stat_sb[:, 4:5], ALU.subtract, ALU.mult),
                            inc="p2_v")
                        EM.mark(("ln1bf", tc))
                        EM.wait("p2_pe", ("g1", tc))
                        if tc >= 1:
                            EM.wait("p2_pe", ("g2", tc - 1))
                        EM.op(lambda: v.tensor_copy(
                            lnfm_sb[:].rearrange("p a b -> p (a b)"),
                            bfT_ps[:, :]), inc="p2_v")
                        EM.mark(("lnfm", tc))
                        if P2STOP < 3:
                            EM.op(lambda tc=tc: v.tensor_copy(
                                x1tc_sb[:, tc % 2, :], attn_sb[:, :]),
                                inc="p2_v")
                            EM.mark(("x1", tc))
                            continue
                        EM.wait("p2_pe", ("g2", tc))
                        EM.wait("sXa" if tc % 2 == 0 else "sXb", ("xload", tc))
                        if tc >= 1:
                            EM.wait("p2_s", ("st2", tc - 1))
                        EM.op(lambda tc=tc: v.tensor_tensor(
                            r_sb[:, :], x_sl[:, tc % 2, :], o_ps[:, :],
                            ALU.add), inc="p2_v")
                        EM.mark(("r1", tc))
                        EM.wait("p2_s", ("rstd2", tc))
                        if tc >= 2:
                            EM.wait("p2_s", ("st3", tc - 2))
                            EM.wait("sSa" if tc % 2 == 0 else "sSb",
                                    ("spill", tc - 2))
                        EM.op(lambda tc=tc: v.tensor_scalar(
                            x1tc_sb[:, tc % 2, :], r_sb[:, :],
                            stat_sb[:, 7:8], stat_sb[:, 9:10],
                            ALU.subtract, ALU.mult), inc="p2_v")
                        EM.mark(("x1", tc))
                        if P2STOP < 4:
                            continue
                        EM.wait("p2_s", ("rstd3", tc))
                        EM.op(lambda tc=tc: v.tensor_scalar(
                            scr_sb[:, :], x1tc_sb[:, tc % 2, :],
                            stat_sb[:, 12:13], stat_sb[:, 14:15],
                            ALU.subtract, ALU.mult))
                        EM.op(lambda: v.tensor_tensor(
                            scr2_sb[:, :], scr_sb[:, :], fw_sb[:, :],
                            ALU.mult))
                        if tc >= 1:
                            EM.wait("p2_pe", ("g3", tc - 1))
                        EM.op(lambda: v.tensor_tensor(
                            xftm_sb[:, :], scr2_sb[:, :], fb_sb[:, :],
                            ALU.add), inc="p2_v")
                        EM.mark(("xftm", tc))
                        EM.wait("p2_pe", ("g3", tc))
                        EM.op(lambda tc=tc: v.tensor_copy(
                            xf_fm[:, :, tc * 128:(tc + 1) * 128],
                            bfT_ps[:, :].rearrange("p (a b) -> p a b", b=128)),
                            inc="p2_v")
                        EM.mark(("xffm", tc))
                        if P2STOP < 5:
                            continue
                        EM.wait("p2_pe", ("g4", tc))
                        if tc >= 1:
                            EM.wait("p2_s", ("sneg", tc - 1))
                        EM.op(lambda: v.reduce_max(
                            stat_sb[:, 15:16], lg_ps[:, :], axis=AX.X),
                            inc="p2_v")
                        EM.mark(("top", tc))
                        if tc >= 1:
                            EM.wait("p2_s", ("wbf", tc - 1))
                        EM.selfwait("p2_v")
                        EM.op(lambda: v.tensor_single_scalar(
                            wall_sb[:, :], lg_ps[:, :], stat_sb[:, 15:16],
                            ALU.is_equal), inc="c_sV")
                        EM.wait("p2_s", ("sneg", tc))
                        EM.op(lambda: v.reciprocal(
                            stat_sb[:, 18:19], stat_sb[:, 17:18]),
                            inc="p2_v")
                        EM.wait("p2_v", value=EM.cnt.get("p2_v", 0))
                        EM.op(lambda: v.tensor_single_scalar(
                            wall_sb[:, :], wall_sb[:, :], stat_sb[:, 18:19],
                            ALU.mult), inc="p2_v")
                        EM.mark(("wall", tc))
                        EM.wait("p2_pe", ("g5", tc))
                        EM.op(lambda tc=tc: v.tensor_copy(
                            wstage[0:8, tc * 128:(tc + 1) * 128],
                            bfT_ps[0:8, 0:128]), inc="p2_v")
                        EM.mark(("wst", tc))

                def p2_sync(s):
                    EM.eng = s
                    EM.chain_off()
                    EM.wait("c_k", value=TC)
                    EM.wait("c_v", value=TC)
                    EM.wait("sColl", "state_done")
                    EM.dma(lambda: s.dma_start(out=x_sl[:, 0, :],
                                               in_=x_d[:, 0, :]), "sXa")
                    EM.dma(lambda: s.dma_start(out=x_sl[:, 1, :],
                                               in_=x_d[:, 1, :]), "sXb")
                    EM.mark(("xload", 0))
                    EM.mark(("xload", 1))
                    for tc in range(TC):
                        EM.wait("p2_v", ("x1", tc))
                        EM.dma(lambda tc=tc: s.dma_start(
                            out=x1sp_d[:, tc, :], in_=x1tc_sb[:, tc % 2, :]),
                            "sSa" if tc % 2 == 0 else "sSb")
                        EM.mark(("spill", tc))
                        if tc + 2 < TC:
                            EM.wait("p2_v", ("r1", tc))
                            EM.dma(lambda tc=tc: s.dma_start(
                                out=x_sl[:, tc % 2, :], in_=x_d[:, tc + 2, :]),
                                "sXa" if tc % 2 == 0 else "sXb")
                            EM.mark(("xload", tc + 2))
                    EM.mark("spill_all")

                def p2_gp(g):
                    EM.eng = g
                    EM.chain_off()
                    for jcw in range(8):
                        EM.wait("c_q", value=(jcw + 1) * 4)
                        EM.dma(lambda jcw=jcw: g.dma_start(
                            out=wo_sb[:, jcw, :], in_=woT_d[:, jcw, :]), "sW")
                    EM.mark("woT_g")

                if "2" not in PHASES:
                    def p1_tail(s):
                        EM.eng = s
                        EM.wait("sColl", "state_done")
                        EM.wait("c_eq", value=32)
                    if EM.real:
                        block.sync(p1_tail)
                    else:
                        p1_tail(None)
                elif EM.real:
                    block.gpsimd(p2_gp); block.tensor(p2_pe)
                    block.scalar(p2_s); block.vector(p2_v)
                    block.sync(p2_sync)
                else:
                    p2_gp(None); p2_pe(None); p2_s(None); p2_v(None)
                    p2_sync(None)

        def p3_program(EM, scope):
            e2 = scope.enter_context
            x1t_sb = e2(nc.sbuf_tensor(_PFX[0] + "x1t_sb", [128, TCC, C], F32))
            d_sb = e2(nc.sbuf_tensor(_PFX[0] + "d_sb", [128, 2, M, C], BF16))
            h_sb = e2(nc.sbuf_tensor(_PFX[0] + "h_sb", [128, M, TTW], BF16))
            xe_sb = e2(nc.sbuf_tensor(_PFX[0] + "xe_sb", [128, 8, TTW], BF16))
            g_sb = e2(nc.sbuf_tensor(_PFX[0] + "g_sb", [128, 3, 8, 128], BF16))
            u_sb = e2(nc.sbuf_tensor(_PFX[0] + "u_sb", [128, 3, 8, 128], BF16))
            sa_sb = e2(nc.sbuf_tensor(_PFX[0] + "sa_sb", [128, 2, 512], BF16))
            wbc_sb = e2(nc.sbuf_tensor(_PFX[0] + "wbc_sb", [128, TTW], BF16))
            st4_sb = e2(nc.sbuf_tensor(_PFX[0] + "st4_sb", [128, 8], F32))
            sd4a_sb = e2(nc.sbuf_tensor(_PFX[0] + "sd4a_sb", [128, C], BF16))
            sd4b_sb = sd4a_sb

            with ExitStack() as ps3:
                e3 = ps3.enter_context
                a_ps = [e3(nc.psum_tensor(_PFX[0] + f"a_ps{k}", [128, 512], F32))
                        for k in range(2)]
                b_ps = [e3(nc.psum_tensor(_PFX[0] + f"b_ps{k}", [128, 512], F32))
                        for k in range(2)]
                dn_ps = [e3(nc.psum_tensor(_PFX[0] + f"dn_ps{k}", [128, C], F32))
                         for k in range(2)]

                NPH = NTT * (E + 1)

                def ph_te(ph):
                    return divmod(ph, E + 1)

                def rix(ph):
                    tt, eidx = ph_te(ph)
                    return tt * E + eidx

                def p3_gp(g):
                    EM.eng = g
                    EM.chain_off()
                    EM.wait("p2_v", ("wst", TC - 1))
                    EM.wait("p2_s", ("wbf", TC - 1))
                    EM.wait("p2_pe", ("g5", TC - 1))
                    EM.wait("sSa", "spill_all")
                    EM.wait("sSb", "spill_all")
                    for ph in range(NPH):
                        tt, eidx = ph_te(ph)
                        if ph >= 2:
                            EM.wait("c_dn", value=(ph - 1) * TCC)
                        for m in range(M):
                            dsrc = shd_d[m] if eidx == 0 else down_d[eidx - 1, m]
                            EM.dma(lambda dsrc=dsrc, m=m, ph=ph: g.dma_start(
                                out=d_sb[:, ph % 2, m], in_=dsrc),
                                "sDa" if ph % 2 == 0 else "sDb")
                        EM.mark(("dld", ph))
                        for m in range(M):
                            i = ph * M + m
                            if i >= 3:
                                EM.wait("c_b", value=2 * (i - 2))
                            gsrc = shg_d[m] if eidx == 0 else gate_d[eidx - 1, m]
                            usrc = shu_d[m] if eidx == 0 else up_d[eidx - 1, m]
                            EM.dma(lambda gsrc=gsrc, i=i: g.dma_start(
                                out=g_sb[:, i % 3], in_=gsrc), SG[i % 3])
                            EM.dma(lambda usrc=usrc, i=i: g.dma_start(
                                out=u_sb[:, i % 3], in_=usrc), SG[i % 3])

                def p3_sync(s):
                    EM.eng = s
                    EM.chain_off()
                    EM.wait("sSa", "spill_all")
                    EM.wait("sSb", "spill_all")
                    for tt in range(NTT):
                        if tt >= 1:
                            EM.wait("c_ln4v", ("ln4", tt * TCC - 1))
                        if tt >= 1:
                            EM.wait("sOa", value=16 * TCC)
                            EM.wait("sOb", value=16 * TCC)
                        EM.dma(lambda tt=tt: s.dma_start(
                            out=x1t_sb[:],
                            in_=x1sp_d[:, tt * TCC:(tt + 1) * TCC, :]), "sXl")
                        EM.mark(("x1ld", tt))
                        for tcc in range(TCC):
                            tti = tt * TCC + tcc
                            EM.wait("c_ln4v", ("ln4", tti))
                            EM.dma(lambda tt=tt, tcc=tcc: s.dma_start(
                                out=out_d[:, tt * TCC + tcc, 0:512],
                                in_=x1t_sb[:, tcc, 0:512]), "sOa")
                            EM.dma(lambda tt=tt, tcc=tcc: s.dma_start(
                                out=out_d[:, tt * TCC + tcc, 512:1024],
                                in_=x1t_sb[:, tcc, 512:1024]), "sOb")
                            EM.mark(("ost", tti))
                    EM.wait("sOa", value=16 * 2 * TCC)
                    EM.wait("sOb", value=16 * 2 * TCC)

                def p3_pe(t):
                    EM.eng = t
                    EM.chain_off()
                    EM.wait("p2_v", ("xffm", TC - 1))
                    EM.wait("p2_v", ("wst", TC - 1))
                    EM.wait("p2_s", ("wbf", TC - 1))
                    for ph in range(NPH):
                        tt, eidx = ph_te(ph)
                        if eidx > 0:
                            EM.wait("c_xe", value=rix(ph))
                        for m in range(M):
                            i = ph * M + m
                            EM.wait(SG[i % 3], value=32 * (i // 3 + 1))
                            rhs = xf_fm if eidx == 0 else xe_sb
                            roff = tt * TTW if eidx == 0 else 0
                            for th in range(2):
                                sl = i * 2 + th
                                if eidx > 0 and m == 0 and th == 0:
                                    EM.wait("c_wbcV", value=rix(ph))
                                elif sl >= 2:
                                    EM.wait("c_sa", value=sl - 1)
                                if sl >= 2:
                                    EM.wait("c_h", value=sl - 1)
                                for jc in range(8):
                                    EM.op(lambda jc=jc, i=i, th=th, rhs=rhs,
                                          roff=roff: t.matmul(
                                        a_ps[th][:, :],
                                        g_sb[:, i % 3, jc, :],
                                        rhs[:, jc,
                                            roff + th * 512:roff + (th + 1) * 512],
                                        start=(jc == 0), stop=(jc == 7)),
                                        inc="c_a" if jc == 7 else None)
                                    EM.op(lambda jc=jc, i=i, th=th, rhs=rhs,
                                          roff=roff: t.matmul(
                                        b_ps[th][:, :],
                                        u_sb[:, i % 3, jc, :],
                                        rhs[:, jc,
                                            roff + th * 512:roff + (th + 1) * 512],
                                        start=(jc == 0), stop=(jc == 7)),
                                        inc="c_b" if jc == 7 else None)
                        if ph + 1 < NPH and ph_te(ph + 1)[1] > 0:
                            ntt, neidx = ph_te(ph + 1)
                            EM.wait("c_sa", value=2 * (ph + 1) * M)
                            EM.wait("p2_v", ("wst", TC - 1))
                            for half in range(2):
                                EM.op(lambda half=half, neidx=neidx,
                                      ntt=ntt: t.matmul(
                                    a_ps[half][:, :],
                                    sel_sb[:, (neidx - 1) * 128:neidx * 128],
                                    wstage[0:8, ntt * TTW + half * 512:
                                           ntt * TTW + (half + 1) * 512],
                                    start=True, stop=True),
                                    inc="c_wbc")
                        EM.wait("c_h", value=2 * (ph * M + M))
                        EM.wait("sDa" if ph % 2 == 0 else "sDb", ("dld", ph))
                        if eidx == 0:
                            EM.wait("sXl", ("x1ld", tt))
                        for tcc in range(TCC):
                            J = ph * TCC + tcc
                            if J >= 2:
                                EM.wait("c_dnE", value=J - 1)
                            for m in range(M):
                                for half in range(2):
                                    EM.op(lambda m=m, half=half, tcc=tcc,
                                          ph=ph: t.matmul(
                                        dn_ps[tcc % 2][:,
                                                       half * 512:(half + 1) * 512],
                                        h_sb[:, m,
                                             tcc * 128:(tcc + 1) * 128],
                                        d_sb[:, ph % 2, m,
                                             half * 512:(half + 1) * 512],
                                        start=(m == 0), stop=(m == M - 1)),
                                        inc="c_dn"
                                        if (m == M - 1 and half == 1) else None)
                            EM.mark(("dn", J))

                def p3_s(sc):
                    EM.eng = sc
                    EM.chain_on("c_sS")
                    for ph in range(NPH):
                        tt, eidx = ph_te(ph)
                        for m in range(M):
                            i = ph * M + m
                            for th in range(2):
                                sl = i * 2 + th
                                EM.wait("c_a", value=sl + 1)
                                if sl >= 2:
                                    EM.wait("c_h", value=sl - 1)
                                EM.op(lambda sl=sl: sc.activation(
                                    sa_sb[:, sl % 2, :], a_ps[sl % 2][:, :],
                                    AF.Silu), inc="c_sa")
                        if eidx == E:
                            for tcc in range(TCC):
                                tti = tt * TCC + tcc
                                EM.wait("c_dnE", value=ph * TCC + tcc + 1)
                                if tti >= 1:
                                    EM.wait("c_ln4v", ("ln4", tti - 1))
                                EM.op(lambda tcc=tcc: sc.activation(
                                    sd4a_sb[:, :], x1t_sb[:, tcc, :], AF.Copy,
                                    accum_out=st4_sb[:, 0:1]))
                                EM.op(lambda tcc=tcc: sc.activation(
                                    sd4b_sb[:, :], x1t_sb[:, tcc, :],
                                    AF.Square, accum_out=st4_sb[:, 1:2]))
                                EM.op(lambda: sc.activation(
                                    st4_sb[:, 2:3], st4_sb[:, 0:1], AF.Copy,
                                    scale=1.0 / C))
                                EM.op(lambda: sc.activation(
                                    st4_sb[:, 3:4], st4_sb[:, 2:3],
                                    AF.Square))
                                EM.op(lambda: sc.activation(
                                    st4_sb[:, 3:4], st4_sb[:, 3:4], AF.Copy,
                                    scale=-1.0, bias=EPS))
                                EM.op(lambda: sc.activation(
                                    st4_sb[:, 4:5], st4_sb[:, 1:2], AF.Ln,
                                    scale=1.0 / C, bias=st4_sb[:, 3:4]))
                                EM.op(lambda: sc.activation(
                                    st4_sb[:, 4:5], st4_sb[:, 4:5], AF.Exp,
                                    scale=-0.5), inc="c_ln4s")
                                EM.mark(("rstd4", tti))

                def p3_v(v):
                    EM.eng = v
                    EM.chain_on("c_sV")

                    def emit_xe(phx):
                        ntt = ph_te(phx)[0]
                        EM.wait("c_wbc", value=2 * rix(phx))
                        EM.op(lambda: v.tensor_copy(
                            wbc_sb[:, 0:512], a_ps[0][:, :]))
                        EM.op(lambda: v.tensor_copy(
                            wbc_sb[:, 512:1024], a_ps[1][:, :]),
                            inc="c_wbcV")
                        for jc in range(8):
                            EM.op(lambda jc=jc, ntt=ntt: v.tensor_tensor(
                                xe_sb[:, jc, :],
                                xf_fm[:, jc, ntt * TTW:(ntt + 1) * TTW],
                                wbc_sb[:, :], ALU.mult),
                                inc="c_xe" if jc == 7 else None)

                    for ph in range(NPH):
                        tt, eidx = ph_te(ph)
                        if eidx == 0:
                            EM.wait("sXl", ("x1ld", tt))
                            for tcc in range(TCC):
                                EM.op(lambda tcc=tcc: v.tensor_scalar(
                                    x1t_sb[:, tcc, :], x1t_sb[:, tcc, :],
                                    2.0, None, ALU.mult))
                        for m in range(M):
                            i = ph * M + m
                            for th in range(2):
                                sl = i * 2 + th
                                EM.wait("c_sa", value=sl + 1)
                                EM.wait("c_b", value=sl + 1)
                                if m == 0 and th == 0 and ph >= 1:
                                    EM.wait("c_dn", value=ph * TCC)
                                EM.op(lambda m=m, th=th, sl=sl:
                                      v.tensor_tensor(
                                    h_sb[:, m, th * 512:(th + 1) * 512],
                                    sa_sb[:, sl % 2, :], b_ps[sl % 2][:, :],
                                    ALU.mult), inc="c_h")
                        if ph + 1 < NPH and ph_te(ph + 1)[1] > 0:
                            emit_xe(ph + 1)
                        for tcc in range(TCC):
                            J = ph * TCC + tcc
                            EM.wait("c_dn", value=J + 1)
                            EM.op(lambda tcc=tcc: v.tensor_tensor(
                                x1t_sb[:, tcc, :], x1t_sb[:, tcc, :],
                                dn_ps[tcc % 2][:, :], ALU.add), inc="c_dnE")
                        if eidx == E:
                            for tcc in range(TCC):
                                tti = tt * TCC + tcc
                                EM.wait("c_ln4s", ("rstd4", tti))
                                EM.op(lambda tcc=tcc: v.tensor_scalar(
                                    x1t_sb[:, tcc, :], x1t_sb[:, tcc, :],
                                    st4_sb[:, 2:3], st4_sb[:, 4:5],
                                    ALU.subtract, ALU.mult), inc="c_ln4v")
                                EM.mark(("ln4", tti))

                if "3" not in PHASES:
                    def p2_tail(s):
                        EM.eng = s
                        if "2" in PHASES:
                            EM.wait("sXS", "spill_all")
                    if EM.real and "2" in PHASES:
                        block.sync(p2_tail)
                    elif "2" in PHASES:
                        p2_tail(None)
                elif EM.real:
                    block.gpsimd(p3_gp); block.tensor(p3_pe)
                    block.scalar(p3_s); block.vector(p3_v)
                    block.sync(p3_sync)
                else:
                    p3_gp(None); p3_s(None); p3_v(None)
                    p3_pe(None); p3_sync(None)

        _PFX[0] = "dry_"
        with ExitStack() as dry1:
            p12_program(EM, dry1)
        with ExitStack() as dry2:
            p3_program(EM, dry2)
        EM.reset()
        _PFX[0] = ""
        EM.real = True
        with ExitStack() as real1:
            p12_program(EM, real1)
        with ExitStack() as real2:
            p3_program(EM, real2)

    return nc


_NC_CACHE = None


def _host_layouts(inputs):
    f32 = np.float32

    def wT(a):
        return np.ascontiguousarray(np.asarray(a, f32).T) \
            .reshape(8, 128, C).transpose(1, 0, 2).copy()

    shared = dict(
        wqT=wT(inputs["L0_attn_Wq"]),
        wkT=wT(inputs["L0_attn_Wk"]),
        wvT=wT(inputs["L0_attn_Wv"]),
        woT=wT(inputs["L0_attn_Wo"]),
        bq=np.asarray(inputs["L0_attn_bq"], f32).reshape(8, 128).T.copy(),
        bk_row=np.asarray(inputs["L0_attn_bk"], f32).reshape(1, C).copy(),
        bv_row=np.asarray(inputs["L0_attn_bv"], f32).reshape(1, C).copy(),
        bo_row=np.asarray(inputs["L0_attn_bo"], f32).reshape(1, C).copy(),
        lng_fm=np.asarray(inputs["L0_attn_ln_g"], f32)
            .reshape(8, 128).T.copy(),
        lnb_fm=np.asarray(inputs["L0_attn_ln_b"], f32)
            .reshape(8, 128).T.copy(),
        fw=np.ascontiguousarray(np.broadcast_to(
            np.asarray(inputs["L0_ffn_ln_w"], f32), (128, C))),
        fb=np.ascontiguousarray(np.broadcast_to(
            np.asarray(inputs["L0_ffn_ln_b"], f32), (128, C))),
        router=np.asarray(inputs["L0_ffn_router_w"], f32)
            .reshape(8, 128, E).transpose(1, 0, 2).copy(),
        gate=np.asarray(inputs["L0_ffn_gate_w"], f32)
            .reshape(E, 8, 128, M, 128).transpose(0, 3, 2, 1, 4).copy(),
        up=np.asarray(inputs["L0_ffn_up_w"], f32)
            .reshape(E, 8, 128, M, 128).transpose(0, 3, 2, 1, 4).copy(),
        down=np.asarray(inputs["L0_ffn_down_w"], f32)
            .reshape(E, M, 128, C).copy(),
        shg=np.asarray(inputs["L0_ffn_sh_gate"], f32)
            .reshape(8, 128, M, 128).transpose(2, 1, 0, 3).copy(),
        shu=np.asarray(inputs["L0_ffn_sh_up"], f32)
            .reshape(8, 128, M, 128).transpose(2, 1, 0, 3).copy(),
        shd=np.asarray(inputs["L0_ffn_sh_down"], f32)
            .reshape(M, 128, C).copy(),
        ones_bf=np.ones((1, 128), ml_dtypes.bfloat16),
        onescol_bf=np.ones((128, 1), ml_dtypes.bfloat16),
        idt_bf=np.eye(128).astype(ml_dtypes.bfloat16),
        eps_col=np.full((128, 1), EPS, f32),
    )
    ones2 = np.zeros((128, 2), f32)
    ones2[0:64, 0] = 1.0
    ones2[64:128, 1] = 1.0
    shared["ones2_bf"] = ones2.astype(ml_dtypes.bfloat16)
    sel = np.zeros((8, E * 128), f32)
    for e in range(E):
        sel[e, e * 128:(e + 1) * 128] = 1.0
    shared["sel_bf"] = sel.astype(ml_dtypes.bfloat16)
    return shared


def kernel(**inputs):
    global _NC_CACHE
    x = np.asarray(inputs["L0_attn_x"], np.float32)
    shared = _host_layouts(inputs)

    in_maps = []
    for c in range(NCORES):
        own = np.concatenate([x[0, c * 1024:(c + 1) * 1024],
                              x[1, c * 1024:(c + 1) * 1024]], 0)
        x_tm = own.reshape(TC, 128, C).transpose(1, 0, 2).copy()
        xT = np.ascontiguousarray(own.T).reshape(8, 128, TOK) \
            .transpose(1, 0, 2).copy()
        in_maps.append(dict(shared, x_tm=x_tm, xT=xT))

    if _NC_CACHE is None:
        _NC_CACHE = build_nc()
    res = run_bass_kernel_spmd(_NC_CACHE, in_maps, core_ids=list(range(NCORES)))

    out = np.empty((B, N, C), np.float32)
    for c in range(NCORES):
        o = np.asarray(res.results[c]["out"]).transpose(1, 0, 2).reshape(TOK, C)
        out[0, c * 1024:(c + 1) * 1024] = o[0:1024]
        out[1, c * 1024:(c + 1) * 1024] = o[1024:2048]
    return out


# version 13
# speedup vs baseline: 1.2198x; 1.0183x over previous
import numpy as np
import ml_dtypes
import concourse.bass as bass
import concourse.mybir as mybir
from concourse.bass_utils import run_bass_kernel_spmd
from contextlib import ExitStack

F32 = mybir.dt.float32
BF16 = mybir.dt.bfloat16
AF = mybir.ActivationFunctionType
ALU = mybir.AluOpType
AX = mybir.AxisListType

NCORES = 8
B = 2
N = 8192
C = 1024
H = 16
E = 8
M = 16
TOK = 2048
TC = 16
TCB = 8
NTT = 2
TTW = 1024
TCC = 8
EPS = 1e-5
DEBUG = False
PHASES = "123"
P2STOP = 9


class Emit:

    def __init__(self):
        self.real = False
        self.cnt = {}
        self.marks = {}
        self.sems = {}
        self.eng = None
        self.chain = None
        self.last = None

    def op(self, f, inc=None, k=1):
        if self.chain is not None:
            if self.last is not None:
                self.wait(self.last[0], value=self.last[1])
            if inc is None:
                inc = self.chain
        inst = f() if self.real else None
        if inc is not None:
            if self.real:
                inst.then_inc(self.sems[inc], k)
            self.cnt[inc] = self.cnt.get(inc, 0) + k
            if self.chain is not None:
                self.last = (inc, self.cnt[inc])
        return inst

    def chain_on(self, sem):
        self.chain = sem
        self.last = None

    def chain_off(self):
        self.chain = None
        self.last = None

    def dma(self, f, inc):
        return self.op(f, inc, 16)

    def mark(self, name):
        if not self.real:
            self.marks[name] = dict(self.cnt)

    def wait(self, sem, mark_name=None, value=None):
        if value is None:
            if mark_name not in self.marks:
                return
            value = self.marks[mark_name].get(sem, 0)
        if value <= 0:
            return
        if self.real:
            self.eng.wait_ge(self.sems[sem], value)

    def selfwait(self, sem):
        self.wait(sem, value=self.cnt.get(sem, 0))

    def reset(self):
        self.cnt = {}


SEM_NAMES = [
    "sCst", "sW", "sXa", "sXb", "sSa", "sSb", "sOa", "sOb", "sColl", "sXl", "sDa", "sDb", "sG0", "sG1", "sG2",
    "sWk", "sWv", "sWq", "sBkv", "sXg0", "sXg1", "sXg2", "sXg3",
    "c_q", "c_eq", "c_k", "c_ek", "c_v", "c_va", "c_U", "c_Ue", "c_stA",
    "p2_pe", "p2_v", "p2_s", "sCC",
    "c_xe", "c_a", "c_b", "c_sa", "c_h", "c_dn", "c_dnE",
    "c_wbc", "c_wbcV", "c_ln4s", "c_ln4v", "c_sV", "c_sS",
]
SG = ["sG0", "sG1", "sG2"]
_PFX = [""]


def build_nc():
    nc = bass.Bass("TRN2", target_bir_lowering=False)

    d = {}
    def din(name, shape, dt=F32):
        d[name] = nc.dram_tensor(name, list(shape), dt, kind="ExternalInput").ap()
        return d[name]

    xT_d = din("xT", [128, 8, TOK])
    x_d = din("x_tm", [128, TC, C])
    wqT_d = din("wqT", [128, 8, C])
    wkT_d = din("wkT", [128, 8, C])
    wvT_d = din("wvT", [128, 8, C])
    woT_d = din("woT", [128, 8, C])
    bq_d = din("bq", [128, 8])
    bk_d = din("bk_row", [1, C])
    bv_d = din("bv_row", [1, C])
    bo_d = din("bo_row", [1, C])
    lngf_d = din("lng_fm", [128, 8])
    lnbf_d = din("lnb_fm", [128, 8])
    fw_d = din("fw", [128, C])
    fb_d = din("fb", [128, C])
    rout_d = din("router", [128, 8, E])
    gate_d = din("gate", [E, M, 128, 8, 128])
    up_d = din("up", [E, M, 128, 8, 128])
    down_d = din("down", [E, M, 128, C])
    shg_d = din("shg", [M, 128, 8, 128])
    shu_d = din("shu", [M, 128, 8, 128])
    shd_d = din("shd", [M, 128, C])
    ones_d = din("ones_bf", [1, 128], BF16)
    onesc_d = din("onescol_bf", [128, 1], BF16)
    ones2_d = din("ones2_bf", [128, 2], BF16)
    idt_d = din("idt_bf", [128, 128], BF16)
    sel_d = din("sel_bf", [8, E * 128], BF16)
    eps_d = din("eps_col", [128, 1])

    out_d = nc.dram_tensor("out", [128, TC, C], F32, kind="ExternalOutput").ap()
    kd = "ExternalOutput" if DEBUG else "Internal"
    x1sp_d = nc.dram_tensor("x1_spill", [128, TC, C], F32, kind=kd).ap()
    cc_in = nc.dram_tensor("cc_in", [64, 2080], F32, kind="Internal").ap()
    cc_out = nc.dram_tensor("cc_out", [64, 2080], F32, kind="Internal").ap()
    if DEBUG:
        dbg_ccin = nc.dram_tensor("dbg_ccin", [64, 2080], F32,
                                  kind="ExternalOutput").ap()
        dbg_ccout = nc.dram_tensor("dbg_ccout", [64, 2080], F32,
                                   kind="ExternalOutput").ap()
        dbg_state = nc.dram_tensor("dbg_state", [128, B, H, 64], F32,
                                   kind="ExternalOutput").ap()
        dbg_eq = nc.dram_tensor("dbg_eq", [128, 8, TOK], F32,
                                kind="ExternalOutput").ap()

    EM = Emit()

    with ExitStack() as st:
        e_ = st.enter_context

        xf_fm = e_(nc.sbuf_tensor(_PFX[0] + "xf_fm", [128, 8, TOK], BF16))
        lngf_sb = e_(nc.sbuf_tensor(_PFX[0] + "lngf_sb", [128, 8], F32))
        lnbf_sb = e_(nc.sbuf_tensor(_PFX[0] + "lnbf_sb", [128, 8], BF16))
        ob_sb = e_(nc.sbuf_tensor(_PFX[0] + "ob_sb", [1, C], BF16))
        bq_sb = e_(nc.sbuf_tensor(_PFX[0] + "bq_sb", [128, 8], F32))
        rout_sb = e_(nc.sbuf_tensor(_PFX[0] + "rout_sb", [128, 8, E], BF16))
        ones_sb = e_(nc.sbuf_tensor(_PFX[0] + "ones_sb", [1, 128], BF16))
        onesc_sb = e_(nc.sbuf_tensor(_PFX[0] + "onesc_sb", [128, 1], BF16))
        ones2_sb = e_(nc.sbuf_tensor(_PFX[0] + "ones2_sb", [128, 2], BF16))
        idt_sb = e_(nc.sbuf_tensor(_PFX[0] + "idt_sb", [128, 128], BF16))
        sel_sb = e_(nc.sbuf_tensor(_PFX[0] + "sel_sb", [8, E * 128], BF16))
        eps_sb = e_(nc.sbuf_tensor(_PFX[0] + "eps_sb", [128, 1], F32))
        wstage = e_(nc.sbuf_tensor(_PFX[0] + "wstage", [8, TOK], BF16))
        state_sb = e_(nc.sbuf_tensor(_PFX[0] + "state_sb", [128, B, H, 64], BF16))

        for s in SEM_NAMES:
            EM.sems[s] = e_(nc.semaphore(s))
        block = e_(nc.Block())

        def p12_program(EM, scope):
            e2 = scope.enter_context
            x_fm = e2(nc.sbuf_tensor(_PFX[0] + "x_fm", [128, 8, TOK], BF16))
            fw_sb = e2(nc.sbuf_tensor(_PFX[0] + "fw_sb", [128, C], F32))
            fb_sb = e2(nc.sbuf_tensor(_PFX[0] + "fb_sb", [128, C], F32))
            bk_sb = e2(nc.sbuf_tensor(_PFX[0] + "bk_sb", [1, C], BF16))
            bv_sb = e2(nc.sbuf_tensor(_PFX[0] + "bv_sb", [1, C], BF16))
            bo_sb = e2(nc.sbuf_tensor(_PFX[0] + "bo_sb", [1, C], BF16))
            wq_sb = e2(nc.sbuf_tensor(_PFX[0] + "wq_sb", [128, 8, C], BF16))
            eq_sb = e2(nc.sbuf_tensor(_PFX[0] + "eq_sb", [128, 8, TOK], BF16))

            with ExitStack() as ps1:
                e3 = ps1.enter_context
                wk_sb = e3(nc.sbuf_tensor(_PFX[0] + "wk_sb", [128, 8, C], BF16))
                wv_sb = e3(nc.sbuf_tensor(_PFX[0] + "wv_sb", [128, 8, C], BF16))
                ek_sb = e3(nc.sbuf_tensor(_PFX[0] + "ek_sb", [128, 2, C], BF16))
                v_sb = e3(nc.sbuf_tensor(_PFX[0] + "v_sb", [128, 2, C], BF16))
                ustage = e3(nc.sbuf_tensor(_PFX[0] + "ustage", [64, 2080], F32))
                stateA = e3(nc.sbuf_tensor(_PFX[0] + "stateA", [64, B * C], BF16))
                rz_sb = e3(nc.sbuf_tensor(_PFX[0] + "rz_sb", [64, 32], F32))
                state_raw = ustage
                k_ps = e3(nc.psum_tensor(_PFX[0] + "k_ps", [128, C], F32))
                v_ps = e3(nc.psum_tensor(_PFX[0] + "v_ps", [128, C], F32))
                U_ps = e3(nc.psum_tensor(_PFX[0] + "U_ps", [64, C], F32))
                z_ps = e3(nc.psum_tensor(_PFX[0] + "z_ps", [64, 32], F32))
                q_ps = k_ps

                def p1b_sync(s):
                    EM.eng = s
                    EM.chain_off()
                    for src, dst in [(ones_d, ones_sb), (onesc_d, onesc_sb),
                                     (bq_d, bq_sb), (lngf_d, lngf_sb),
                                     (fw_d, fw_sb),
                                     (fb_d, fb_sb),
                                     (ones2_d, ones2_sb),
                                     (idt_d, idt_sb), (sel_d, sel_sb),
                                     (eps_d, eps_sb)]:
                        EM.dma(lambda src=src, dst=dst:
                               s.dma_start(out=dst[:], in_=src), "sCst")
                    EM.mark("consts")

                def p1b_gp(g):
                    EM.eng = g
                    EM.chain_off()
                    EM.dma(lambda: g.dma_start(out=wk_sb[:], in_=wkT_d),
                           "sWk")
                    EM.mark("wk")
                    EM.dma(lambda: g.dma_start(
                        out=x_fm[:, :, 0:512], in_=xT_d[:, :, 0:512]), "sXg0")
                    EM.mark(("xg", 0))
                    for src, dst in [(bk_d, bk_sb), (bv_d, bv_sb)]:
                        EM.dma(lambda src=src, dst=dst:
                               g.dma_start(out=dst[:], in_=src), "sBkv")
                    EM.mark("bkv")
                    EM.dma(lambda: g.dma_start(out=wv_sb[:], in_=wvT_d),
                           "sWv")
                    EM.mark("wv")
                    for gi in range(1, 4):
                        EM.dma(lambda gi=gi: g.dma_start(
                            out=x_fm[:, :, gi * 512:(gi + 1) * 512],
                            in_=xT_d[:, :, gi * 512:(gi + 1) * 512]),
                            "sXg%d" % gi)
                        EM.mark(("xg", gi))
                    EM.dma(lambda: g.dma_start(out=wq_sb[:], in_=wqT_d),
                           "sWq")
                    EM.mark("wq")
                    for src, dst in [(bo_d, bo_sb), (rout_d, rout_sb),
                                     (lnbf_d, lnbf_sb)]:
                        EM.dma(lambda src=src, dst=dst:
                               g.dma_start(out=dst[:], in_=src), "sW")
                    EM.mark("p1w")

                def p1b_pe(t):
                    EM.eng = t
                    EM.chain_off()
                    EM.wait("sCst", "consts")
                    EM.wait("sBkv", "bkv")
                    for tc in range(TC):
                        b = tc // TCB
                        EM.wait("sXg%d" % (tc // 4), ("xg", tc // 4))
                        if tc == 0:
                            EM.wait("sWk", "wk")
                        if tc >= 1:
                            EM.wait("c_ek", value=tc)
                        for jc in range(8):
                            for half in range(2):
                                EM.op(lambda jc=jc, half=half, tc=tc: t.matmul(
                                    k_ps[:, half * 512:(half + 1) * 512],
                                    x_fm[:, jc, tc * 128:(tc + 1) * 128],
                                    wk_sb[:, jc, half * 512:(half + 1) * 512],
                                    start=(jc == 0), stop=False))
                        for half in range(2):
                            EM.op(lambda half=half: t.matmul(
                                k_ps[:, half * 512:(half + 1) * 512],
                                ones_sb[0:1, :],
                                bk_sb[0:1, half * 512:(half + 1) * 512],
                                start=False, stop=True),
                                inc="c_k" if half == 1 else None)
                        if tc == 0:
                            EM.wait("sWv", "wv")
                        if tc >= 1:
                            EM.wait("c_va", value=tc)
                        for jc in range(8):
                            for half in range(2):
                                EM.op(lambda jc=jc, half=half, tc=tc: t.matmul(
                                    v_ps[:, half * 512:(half + 1) * 512],
                                    x_fm[:, jc, tc * 128:(tc + 1) * 128],
                                    wv_sb[:, jc, half * 512:(half + 1) * 512],
                                    start=(jc == 0), stop=False))
                        for half in range(2):
                            EM.op(lambda half=half: t.matmul(
                                v_ps[:, half * 512:(half + 1) * 512],
                                ones_sb[0:1, :],
                                bv_sb[0:1, half * 512:(half + 1) * 512],
                                start=False, stop=True),
                                inc="c_v" if half == 1 else None)
                        EM.wait("c_ek", value=tc + 1)
                        EM.wait("c_va", value=tc + 1)
                        if tc == TCB:
                            EM.wait("c_Ue", value=1)
                        for h in range(H):
                            ust = (tc % TCB == 0) and (h in (0, 8))
                            usp = (tc % TCB == TCB - 1) and (h in (7, 15))
                            EM.op(lambda h=h, tc=tc, ust=ust, usp=usp: t.matmul(
                                U_ps[0:64, h * 64:(h + 1) * 64],
                                ek_sb[:, tc % 2, h * 64:(h + 1) * 64],
                                v_sb[:, tc % 2, h * 64:(h + 1) * 64],
                                start=ust, stop=usp))
                            zst = (tc == 0) and (h == 0)
                            zsp = (tc == TC - 1) and (h == H - 1)
                            EM.op(lambda h=h, b=b, zst=zst, zsp=zsp: t.matmul(
                                z_ps[0:64, b * 16 + h:b * 16 + h + 1],
                                ek_sb[:, tc % 2, h * 64:(h + 1) * 64],
                                onesc_sb[:, 0:1],
                                start=zst, stop=zsp),
                                inc="c_U" if h == H - 1 else None)

                def p1b_s(sc):
                    EM.eng = sc
                    EM.chain_on("c_sS")
                    for tc in range(TC):
                        EM.wait("c_k", value=tc + 1)
                        if tc >= 2:
                            EM.wait("c_U", value=tc - 1)
                        EM.op(lambda tc=tc: sc.activation(
                            ek_sb[:, tc % 2, :], k_ps[:, :], AF.Exp),
                            inc="c_ek")

                def p1b_v(v):
                    EM.eng = v
                    EM.chain_on("c_sV")
                    for tc in range(TC):
                        EM.wait("c_v", value=tc + 1)
                        if tc >= 2:
                            EM.wait("c_U", value=tc - 1)
                        EM.op(lambda tc=tc: v.tensor_copy(
                            v_sb[:, tc % 2, :], v_ps[:, :]), inc="c_va")
                        if tc % TCB == TCB - 1:
                            b = tc // TCB
                            EM.wait("c_U", value=tc + 1)
                            EM.op(lambda b=b: v.tensor_copy(
                                ustage[0:64, b * C:(b + 1) * C], U_ps[0:64, :]),
                                inc="c_Ue")
                    EM.op(lambda: v.tensor_copy(ustage[0:64, 2048:2080],
                                                z_ps[0:64, :]), inc="c_Ue")

                if EM.real:
                    block.sync(p1b_sync); block.gpsimd(p1b_gp)
                    block.tensor(p1b_pe); block.scalar(p1b_s)
                    block.vector(p1b_v)
                else:
                    p1b_sync(None); p1b_gp(None); p1b_pe(None)
                    p1b_s(None); p1b_v(None)

                def cc_gp(g):
                    EM.eng = g
                    EM.chain_off()
                    EM.wait("c_Ue", value=3)
                    EM.dma(lambda: g.dma_start(out=cc_in, in_=ustage[0:64, :]),
                           "sColl")
                    EM.wait("sColl", value=16)
                    EM.op(lambda: g.collective_compute(
                        "AllReduce", ALU.add,
                        replica_groups=[list(range(NCORES))],
                        ins=[cc_in.opt()], outs=[cc_out.opt()]),
                        inc="sCC", k=1)
                    EM.wait("sCC", value=1)
                    EM.dma(lambda: g.dma_start(out=state_raw[0:64, :],
                                               in_=cc_out), "sColl")
                    if DEBUG:
                        EM.dma(lambda: g.dma_start(out=dbg_ccin, in_=cc_in),
                               "sColl")
                        EM.dma(lambda: g.dma_start(out=dbg_ccout, in_=cc_out),
                               "sColl")
                    EM.mark("cc_done")

                def cc_v(v):
                    EM.eng = v
                    EM.chain_on("c_sV")
                    EM.op(lambda: v.memset(state_sb[:], 0.0))
                    EM.wait("sColl", "cc_done")
                    EM.op(lambda: v.reciprocal(rz_sb[0:64, :],
                                               state_raw[0:64, 2048:2080]),
                          inc="c_sV")
                    EM.selfwait("c_sV")
                    for b in range(B):
                        for h in range(H):
                            EM.op(lambda b=b, h=h: v.tensor_single_scalar(
                                stateA[0:64, b * C + h * 64:b * C + (h + 1) * 64],
                                state_raw[0:64, b * C + h * 64:b * C + (h + 1) * 64],
                                rz_sb[0:64, b * 16 + h:b * 16 + h + 1],
                                ALU.mult))
                    EM.op(lambda: v.engine_nop(), inc="c_stA", k=1)

                def cc_gp2(g):
                    EM.eng = g
                    EM.chain_off()
                    EM.wait("c_stA", value=1)
                    srcA = stateA[0:64, :].rearrange(
                        "p (b h2 par e) -> p par b h2 e", b=B, h2=8, par=2)
                    dstE = state_sb[0:64, :, :, :].rearrange(
                        "p b (h2 par) e -> p par b h2 e", par=2)
                    dstO = state_sb[64:128, :, :, :].rearrange(
                        "p b (h2 par) e -> p par b h2 e", par=2)
                    EM.dma(lambda: g.dma_start(
                        out=dstE[:, 0], in_=srcA[:, 0]), "sColl")
                    EM.dma(lambda: g.dma_start(
                        out=dstO[:, 1], in_=srcA[:, 1]), "sColl")
                    if DEBUG:
                        EM.wait("sColl", value=EM.cnt.get("sColl", 0))
                        EM.wait("c_eq", value=32)
                        EM.dma(lambda: g.dma_start(
                            out=dbg_state, in_=state_sb[:]), "sColl")
                        EM.dma(lambda: g.dma_start(
                            out=dbg_eq, in_=eq_sb[:]), "sColl")
                    EM.mark("state_done")

                def p1a_pe(t):
                    EM.eng = t
                    EM.chain_off()
                    EM.wait("c_ek", value=TC)
                    EM.wait("sWq", "wq")
                    for p in range(16):
                        qi0 = 2 * p
                        ic, tq0 = qi0 // 4, qi0 % 4
                        if qi0 >= 2:
                            EM.wait("c_eq", value=qi0)
                        for jc in range(8):
                            for r in range(2):
                                EM.op(lambda jc=jc, ic=ic, tq=tq0 + r, r=r:
                                      t.matmul(
                                    q_ps[:, r * 512:r * 512 + 512],
                                    wq_sb[:, jc, ic * 128:(ic + 1) * 128],
                                    x_fm[:, jc, tq * 512:(tq + 1) * 512],
                                    start=(jc == 0), stop=(jc == 7)),
                                    inc="c_q" if jc == 7 else None)

                def p1a_s(sc):
                    EM.eng = sc
                    EM.chain_on("c_sS")
                    for ic in range(8):
                        for tq in range(4):
                            qi = ic * 4 + tq
                            EM.wait("c_q", value=qi + 1)
                            EM.op(lambda ic=ic, tq=tq, qi=qi: sc.activation(
                                eq_sb[:, ic, tq * 512:(tq + 1) * 512],
                                q_ps[:, (qi % 2) * 512:(qi % 2) * 512 + 512],
                                AF.Exp,
                                bias=bq_sb[:, ic:ic + 1], scale=1.0),
                                inc="c_eq")

                if EM.real:
                    block.gpsimd(cc_gp); block.tensor(p1a_pe)
                    block.scalar(p1a_s); block.vector(cc_v)
                    block.gpsimd(cc_gp2)
                else:
                    cc_gp(None); p1a_pe(None); p1a_s(None); cc_v(None)
                    cc_gp2(None)

            wo_sb = wq_sb
            x_sl = e2(nc.sbuf_tensor(_PFX[0] + "x_sl", [128, 2, C], F32))
            attn_sb = e2(nc.sbuf_tensor(_PFX[0] + "attn_sb", [128, C], F32))
            scr_sb = e2(nc.sbuf_tensor(_PFX[0] + "scr_sb", [128, C], F32))
            scr2_sb = e2(nc.sbuf_tensor(_PFX[0] + "scr2_sb", [128, C], F32))
            sdump_sb = e2(nc.sbuf_tensor(_PFX[0] + "sdump_sb", [128, C], BF16))
            sdump2_sb = e2(nc.sbuf_tensor(_PFX[0] + "sdump2_sb", [128, C], BF16))
            ln1_bf = e2(nc.sbuf_tensor(_PFX[0] + "ln1_bf", [128, C], BF16))
            lnfm_sb = e2(nc.sbuf_tensor(_PFX[0] + "lnfm_sb", [128, 8, 128], BF16))
            r_sb = e2(nc.sbuf_tensor(_PFX[0] + "r_sb", [128, C], F32))
            x1tc_sb = e2(nc.sbuf_tensor(_PFX[0] + "x1tc_sb", [128, 2, C], F32))
            xftm_sb = e2(nc.sbuf_tensor(_PFX[0] + "xftm_sb", [128, C], BF16))
            rzq_sb = e2(nc.sbuf_tensor(_PFX[0] + "rzq_sb", [128, 16], F32))
            stat_sb = e2(nc.sbuf_tensor(_PFX[0] + "stat_sb", [128, 24], F32))
            wall_sb = e2(nc.sbuf_tensor(_PFX[0] + "wall_sb", [128, E], F32))
            wbf_sb = e2(nc.sbuf_tensor(_PFX[0] + "wbf_sb", [128, E], BF16))

            with ExitStack() as ps2:
                e3 = ps2.enter_context
                out_ps = e3(nc.psum_tensor(_PFX[0] + "out_ps", [128, C], F32))
                o_ps = e3(nc.psum_tensor(_PFX[0] + "o_ps", [128, C], F32))
                small_ps = e3(nc.psum_tensor(_PFX[0] + "small_ps", [128, 16], F32))
                lg_ps = e3(nc.psum_tensor(_PFX[0] + "lg_ps", [128, E], F32))
                bfT_ps = e3(nc.psum_tensor(_PFX[0] + "bfT_ps", [128, C], BF16))

                def p2_pe(t):
                    EM.eng = t
                    EM.chain_off()
                    EM.wait("sColl", "state_done")
                    EM.wait("sW", "p1w")
                    EM.wait("c_eq", value=32)
                    for tc in range(TC):
                        b = tc // TCB
                        if tc >= 1:
                            EM.wait("p2_v", ("outn", tc - 1))
                            EM.wait("p2_v", ("rzq", tc - 1))
                        for h in range(H):
                            EM.op(lambda h=h, b=b, tc=tc: t.matmul(
                                out_ps[:, h * 64:(h + 1) * 64],
                                eq_sb[:, h // 2, tc * 128:(tc + 1) * 128],
                                state_sb[:, b, h, :],
                                start=True, stop=True))
                        for ic in range(8):
                            EM.op(lambda ic=ic, tc=tc: t.matmul(
                                small_ps[:, 2 * ic:2 * ic + 2],
                                eq_sb[:, ic, tc * 128:(tc + 1) * 128],
                                ones2_sb[:, :], start=True, stop=True),
                                inc="p2_pe" if ic == 7 else None)
                        EM.mark(("g0", tc))
                        if tc == 0:
                            EM.wait("sW", "woT_g")
                            for half in range(2):
                                for jc in range(8):
                                    EM.op(lambda jc=jc, half=half: t.matmul(
                                        o_ps[0:1, half * 512:(half + 1) * 512],
                                        lnbf_sb[:, jc:jc + 1],
                                        wo_sb[:, jc,
                                              half * 512:(half + 1) * 512],
                                        start=(jc == 0), stop=(jc == 7)),
                                        inc="p2_pe"
                                        if (jc == 7 and half == 1) else None)
                            EM.mark("obias_mm")
                        if P2STOP < 2:
                            continue
                        EM.wait("p2_v", ("ln1bf", tc))
                        if tc >= 1:
                            EM.wait("p2_v", ("wst", tc - 1))
                        for jc in range(8):
                            EM.op(lambda jc=jc: t.transpose(
                                bfT_ps[:, jc * 128:(jc + 1) * 128],
                                ln1_bf[:, jc * 128:(jc + 1) * 128],
                                idt_sb[:, :]),
                                inc="p2_pe" if jc == 7 else None)
                        EM.mark(("g1", tc))
                        if P2STOP < 3:
                            continue
                        EM.wait("p2_v", ("lnfm", tc))
                        if tc == 0:
                            EM.wait("p2_v", "wog")
                        if tc >= 1:
                            EM.wait("p2_v", ("r1", tc - 1))
                        for jc in range(8):
                            for half in range(2):
                                EM.op(lambda jc=jc, half=half: t.matmul(
                                    o_ps[:, half * 512:(half + 1) * 512],
                                    lnfm_sb[:, jc, :],
                                    wo_sb[:, jc, half * 512:(half + 1) * 512],
                                    start=(jc == 0), stop=False))
                        for half in range(2):
                            EM.op(lambda half=half: t.matmul(
                                o_ps[:, half * 512:(half + 1) * 512],
                                ones_sb[0:1, :],
                                ob_sb[0:1, half * 512:(half + 1) * 512],
                                start=False, stop=True),
                                inc="p2_pe" if half == 1 else None)
                        EM.mark(("g2", tc))
                        if P2STOP < 4:
                            continue
                        EM.wait("p2_v", ("xftm", tc))
                        if tc >= 1:
                            EM.wait("p2_v", ("xffm", tc - 1))
                        for jc in range(8):
                            EM.op(lambda jc=jc: t.transpose(
                                bfT_ps[:, jc * 128:(jc + 1) * 128],
                                xftm_sb[:, jc * 128:(jc + 1) * 128],
                                idt_sb[:, :]),
                                inc="p2_pe" if jc == 7 else None)
                        EM.mark(("g3", tc))
                        if P2STOP < 5:
                            continue
                        EM.wait("p2_v", ("xffm", tc))
                        if tc >= 1:
                            EM.wait("p2_v", ("wall", tc - 1))
                        for jc in range(8):
                            EM.op(lambda jc=jc, tc=tc: t.matmul(
                                lg_ps[:, :],
                                xf_fm[:, jc, tc * 128:(tc + 1) * 128],
                                rout_sb[:, jc, :],
                                start=(jc == 0), stop=(jc == 7)),
                                inc="p2_pe" if jc == 7 else None)
                        EM.mark(("g4", tc))
                        EM.wait("p2_s", ("wbf", tc))
                        if tc >= 1:
                            EM.wait("p2_v", ("xffm", tc))
                        EM.op(lambda: t.transpose(
                            bfT_ps[0:8, 0:128], wbf_sb[:, :], idt_sb[:, :]),
                            inc="p2_pe")
                        EM.mark(("g5", tc))

                def ln_stats_s(sc, EMx, y_ap, base, mark_st, mark_rs, tc):
                    EMx.op(lambda: sc.activation(
                        sdump_sb[:, :], y_ap, AF.Copy,
                        accum_out=stat_sb[:, base:base + 1]))
                    EMx.op(lambda: sc.activation(
                        sdump2_sb[:, :], y_ap, AF.Square,
                        accum_out=stat_sb[:, base + 1:base + 2]), inc="p2_s")
                    EMx.mark((mark_st, tc))
                    EMx.op(lambda: sc.activation(
                        stat_sb[:, base + 2:base + 3],
                        stat_sb[:, base:base + 1], AF.Copy, scale=1.0 / C))
                    EMx.op(lambda: sc.activation(
                        stat_sb[:, base + 3:base + 4],
                        stat_sb[:, base + 2:base + 3], AF.Square))
                    EMx.op(lambda: sc.activation(
                        stat_sb[:, base + 3:base + 4],
                        stat_sb[:, base + 3:base + 4], AF.Copy, scale=-1.0,
                        bias=EPS))
                    EMx.op(lambda: sc.activation(
                        stat_sb[:, base + 4:base + 5],
                        stat_sb[:, base + 1:base + 2], AF.Ln, scale=1.0 / C,
                        bias=stat_sb[:, base + 3:base + 4]))
                    EMx.op(lambda: sc.activation(
                        stat_sb[:, base + 4:base + 5],
                        stat_sb[:, base + 4:base + 5], AF.Exp, scale=-0.5),
                        inc="p2_s")
                    EMx.mark((mark_rs, tc))

                def p2_s(sc):
                    EM.eng = sc
                    EM.chain_on("c_sS")
                    for tc in range(TC):
                        if P2STOP < 2:
                            continue
                        EM.wait("p2_v", ("outn", tc))
                        if tc >= 1:
                            EM.wait("p2_v", ("ln1bf", tc - 1))
                        ln_stats_s(sc, EM, attn_sb[:, :], 0, "st1", "rstd1", tc)
                        if P2STOP < 3:
                            continue
                        EM.wait("p2_v", ("r1", tc))
                        if tc >= 1:
                            EM.wait("p2_v", ("x1", tc - 1))
                        ln_stats_s(sc, EM, r_sb[:, :], 5, "st2", "rstd2", tc)
                        if P2STOP < 4:
                            continue
                        EM.wait("p2_v", ("x1", tc))
                        if tc >= 1:
                            EM.wait("p2_v", ("xftm", tc - 1))
                        ln_stats_s(sc, EM,
                                   x1tc_sb[:, tc % 2, :], 10, "st3", "rstd3",
                                   tc)
                        if P2STOP < 5:
                            continue
                        EM.wait("p2_v", ("top", tc))
                        EM.op(lambda: sc.activation(
                            stat_sb[:, 16:17], stat_sb[:, 15:16], AF.Exp,
                            scale=-1.0))
                        EM.op(lambda: sc.activation(
                            stat_sb[:, 17:18], stat_sb[:, 16:17], AF.Copy,
                            bias=1.0), inc="p2_s")
                        EM.mark(("sneg", tc))
                        EM.wait("p2_v", ("wall", tc))
                        EM.op(lambda: sc.activation(
                            wbf_sb[:, :], wall_sb[:, :], AF.Copy),
                            inc="p2_s")
                        EM.mark(("wbf", tc))

                def p2_v(v):
                    EM.eng = v
                    EM.chain_on("c_sV")
                    EM.wait("p2_pe", "obias_mm")
                    EM.op(lambda: v.tensor_tensor(
                        ob_sb[0:1, :], bo_sb[0:1, :], o_ps[0:1, :], ALU.add))
                    for jc in range(8):
                        EM.op(lambda jc=jc: v.tensor_scalar(
                            wo_sb[:, jc, :], wo_sb[:, jc, :],
                            lngf_sb[:, jc:jc + 1], None, ALU.mult),
                            inc="p2_v" if jc == 7 else None)
                    EM.mark("wog")
                    for tc in range(TC):
                        EM.wait("p2_pe", ("g0", tc))
                        if tc >= 1:
                            EM.wait("p2_s", ("st1", tc - 1))
                        EM.op(lambda: v.reciprocal(
                            rzq_sb[:, :], small_ps[:, 0:16]), inc="p2_v")
                        EM.mark(("rzq", tc))
                        EM.selfwait("p2_v")
                        for h in range(H):
                            EM.op(lambda h=h: v.tensor_single_scalar(
                                attn_sb[:, h * 64:(h + 1) * 64],
                                out_ps[:, h * 64:(h + 1) * 64],
                                rzq_sb[:, h:h + 1], ALU.mult),
                                inc="p2_v" if h == H - 1 else None)
                        EM.mark(("outn", tc))
                        if P2STOP < 2:
                            EM.op(lambda tc=tc: v.tensor_copy(
                                x1tc_sb[:, tc % 2, :], attn_sb[:, :]),
                                inc="p2_v")
                            EM.mark(("x1", tc))
                            continue
                        EM.wait("p2_s", ("rstd1", tc))
                        if tc >= 1:
                            EM.wait("p2_pe", ("g1", tc - 1))
                        EM.op(lambda: v.tensor_scalar(
                            ln1_bf[:, :], attn_sb[:, :], stat_sb[:, 2:3],
                            stat_sb[:, 4:5], ALU.subtract, ALU.mult),
                            inc="p2_v")
                        EM.mark(("ln1bf", tc))
                        EM.wait("p2_pe", ("g1", tc))
                        if tc >= 1:
                            EM.wait("p2_pe", ("g2", tc - 1))
                        EM.op(lambda: v.tensor_copy(
                            lnfm_sb[:].rearrange("p a b -> p (a b)"),
                            bfT_ps[:, :]), inc="p2_v")
                        EM.mark(("lnfm", tc))
                        if P2STOP < 3:
                            EM.op(lambda tc=tc: v.tensor_copy(
                                x1tc_sb[:, tc % 2, :], attn_sb[:, :]),
                                inc="p2_v")
                            EM.mark(("x1", tc))
                            continue
                        EM.wait("p2_pe", ("g2", tc))
                        EM.wait("sXa" if tc % 2 == 0 else "sXb", ("xload", tc))
                        if tc >= 1:
                            EM.wait("p2_s", ("st2", tc - 1))
                        EM.op(lambda tc=tc: v.tensor_tensor(
                            r_sb[:, :], x_sl[:, tc % 2, :], o_ps[:, :],
                            ALU.add), inc="p2_v")
                        EM.mark(("r1", tc))
                        EM.wait("p2_s", ("rstd2", tc))
                        if tc >= 2:
                            EM.wait("p2_s", ("st3", tc - 2))
                            EM.wait("sSa" if tc % 2 == 0 else "sSb",
                                    ("spill", tc - 2))
                        EM.op(lambda tc=tc: v.tensor_scalar(
                            x1tc_sb[:, tc % 2, :], r_sb[:, :],
                            stat_sb[:, 7:8], stat_sb[:, 9:10],
                            ALU.subtract, ALU.mult), inc="p2_v")
                        EM.mark(("x1", tc))
                        if P2STOP < 4:
                            continue
                        EM.wait("p2_s", ("rstd3", tc))
                        EM.op(lambda tc=tc: v.tensor_scalar(
                            scr_sb[:, :], x1tc_sb[:, tc % 2, :],
                            stat_sb[:, 12:13], stat_sb[:, 14:15],
                            ALU.subtract, ALU.mult))
                        EM.op(lambda: v.tensor_tensor(
                            scr2_sb[:, :], scr_sb[:, :], fw_sb[:, :],
                            ALU.mult))
                        if tc >= 1:
                            EM.wait("p2_pe", ("g3", tc - 1))
                        EM.op(lambda: v.tensor_tensor(
                            xftm_sb[:, :], scr2_sb[:, :], fb_sb[:, :],
                            ALU.add), inc="p2_v")
                        EM.mark(("xftm", tc))
                        EM.wait("p2_pe", ("g3", tc))
                        EM.op(lambda tc=tc: v.tensor_copy(
                            xf_fm[:, :, tc * 128:(tc + 1) * 128],
                            bfT_ps[:, :].rearrange("p (a b) -> p a b", b=128)),
                            inc="p2_v")
                        EM.mark(("xffm", tc))
                        if P2STOP < 5:
                            continue
                        EM.wait("p2_pe", ("g4", tc))
                        if tc >= 1:
                            EM.wait("p2_s", ("sneg", tc - 1))
                        EM.op(lambda: v.reduce_max(
                            stat_sb[:, 15:16], lg_ps[:, :], axis=AX.X),
                            inc="p2_v")
                        EM.mark(("top", tc))
                        if tc >= 1:
                            EM.wait("p2_s", ("wbf", tc - 1))
                        EM.selfwait("p2_v")
                        EM.op(lambda: v.tensor_single_scalar(
                            wall_sb[:, :], lg_ps[:, :], stat_sb[:, 15:16],
                            ALU.is_equal), inc="c_sV")
                        EM.wait("p2_s", ("sneg", tc))
                        EM.op(lambda: v.reciprocal(
                            stat_sb[:, 18:19], stat_sb[:, 17:18]),
                            inc="p2_v")
                        EM.wait("p2_v", value=EM.cnt.get("p2_v", 0))
                        EM.op(lambda: v.tensor_single_scalar(
                            wall_sb[:, :], wall_sb[:, :], stat_sb[:, 18:19],
                            ALU.mult), inc="p2_v")
                        EM.mark(("wall", tc))
                        EM.wait("p2_pe", ("g5", tc))
                        EM.op(lambda tc=tc: v.tensor_copy(
                            wstage[0:8, tc * 128:(tc + 1) * 128],
                            bfT_ps[0:8, 0:128]), inc="p2_v")
                        EM.mark(("wst", tc))

                def p2_sync(s):
                    EM.eng = s
                    EM.chain_off()
                    EM.wait("c_k", value=TC)
                    EM.wait("c_v", value=TC)
                    EM.wait("sColl", "state_done")
                    EM.dma(lambda: s.dma_start(out=x_sl[:, 0, :],
                                               in_=x_d[:, 0, :]), "sXa")
                    EM.dma(lambda: s.dma_start(out=x_sl[:, 1, :],
                                               in_=x_d[:, 1, :]), "sXb")
                    EM.mark(("xload", 0))
                    EM.mark(("xload", 1))
                    for tc in range(TC):
                        EM.wait("p2_v", ("x1", tc))
                        EM.dma(lambda tc=tc: s.dma_start(
                            out=x1sp_d[:, tc, :], in_=x1tc_sb[:, tc % 2, :]),
                            "sSa" if tc % 2 == 0 else "sSb")
                        EM.mark(("spill", tc))
                        if tc + 2 < TC:
                            EM.wait("p2_v", ("r1", tc))
                            EM.dma(lambda tc=tc: s.dma_start(
                                out=x_sl[:, tc % 2, :], in_=x_d[:, tc + 2, :]),
                                "sXa" if tc % 2 == 0 else "sXb")
                            EM.mark(("xload", tc + 2))
                    EM.mark("spill_all")

                def p2_gp(g):
                    EM.eng = g
                    EM.chain_off()
                    for jcw in range(8):
                        EM.wait("c_q", value=(jcw + 1) * 4)
                        EM.dma(lambda jcw=jcw: g.dma_start(
                            out=wo_sb[:, jcw, :], in_=woT_d[:, jcw, :]), "sW")
                    EM.mark("woT_g")

                if "2" not in PHASES:
                    def p1_tail(s):
                        EM.eng = s
                        EM.wait("sColl", "state_done")
                        EM.wait("c_eq", value=32)
                    if EM.real:
                        block.sync(p1_tail)
                    else:
                        p1_tail(None)
                elif EM.real:
                    block.gpsimd(p2_gp); block.tensor(p2_pe)
                    block.scalar(p2_s); block.vector(p2_v)
                    block.sync(p2_sync)
                else:
                    p2_gp(None); p2_pe(None); p2_s(None); p2_v(None)
                    p2_sync(None)

        def p3_program(EM, scope):
            e2 = scope.enter_context
            x1t_sb = e2(nc.sbuf_tensor(_PFX[0] + "x1t_sb", [128, TCC, C], F32))
            d_sb = e2(nc.sbuf_tensor(_PFX[0] + "d_sb", [128, 2, M, C], BF16))
            h_sb = e2(nc.sbuf_tensor(_PFX[0] + "h_sb", [128, M, TTW], BF16))
            xe_sb = e2(nc.sbuf_tensor(_PFX[0] + "xe_sb", [128, 8, TTW], BF16))
            g_sb = e2(nc.sbuf_tensor(_PFX[0] + "g_sb", [128, 3, 8, 128], BF16))
            u_sb = e2(nc.sbuf_tensor(_PFX[0] + "u_sb", [128, 3, 8, 128], BF16))
            sa_sb = e2(nc.sbuf_tensor(_PFX[0] + "sa_sb", [128, 2, 512], BF16))
            wbc_sb = e2(nc.sbuf_tensor(_PFX[0] + "wbc_sb", [128, TTW], BF16))
            st4_sb = e2(nc.sbuf_tensor(_PFX[0] + "st4_sb", [128, 8], F32))
            sd4a_sb = e2(nc.sbuf_tensor(_PFX[0] + "sd4a_sb", [128, C], BF16))
            sd4b_sb = sd4a_sb

            with ExitStack() as ps3:
                e3 = ps3.enter_context
                a_ps = [e3(nc.psum_tensor(_PFX[0] + f"a_ps{k}", [128, 512], F32))
                        for k in range(2)]
                b_ps = [e3(nc.psum_tensor(_PFX[0] + f"b_ps{k}", [128, 512], F32))
                        for k in range(2)]
                dn_ps = [e3(nc.psum_tensor(_PFX[0] + f"dn_ps{k}", [128, C], F32))
                         for k in range(2)]

                NPH = NTT * (E + 1)

                def ph_te(ph):
                    return divmod(ph, E + 1)

                def rix(ph):
                    tt, eidx = ph_te(ph)
                    return tt * E + eidx

                def p3_gp(g):
                    EM.eng = g
                    EM.chain_off()
                    EM.wait("p2_v", ("wst", TC - 1))
                    EM.wait("p2_s", ("wbf", TC - 1))
                    EM.wait("p2_pe", ("g5", TC - 1))
                    EM.wait("sSa", "spill_all")
                    EM.wait("sSb", "spill_all")
                    for ph in range(NPH):
                        tt, eidx = ph_te(ph)
                        if ph >= 2:
                            EM.wait("c_dn", value=(ph - 1) * TCC)
                        for m in range(M):
                            dsrc = shd_d[m] if eidx == 0 else down_d[eidx - 1, m]
                            EM.dma(lambda dsrc=dsrc, m=m, ph=ph: g.dma_start(
                                out=d_sb[:, ph % 2, m], in_=dsrc),
                                "sDa" if ph % 2 == 0 else "sDb")
                        EM.mark(("dld", ph))
                        for m in range(M):
                            i = ph * M + m
                            if i >= 3:
                                EM.wait("c_b", value=2 * (i - 2))
                            gsrc = shg_d[m] if eidx == 0 else gate_d[eidx - 1, m]
                            usrc = shu_d[m] if eidx == 0 else up_d[eidx - 1, m]
                            EM.dma(lambda gsrc=gsrc, i=i: g.dma_start(
                                out=g_sb[:, i % 3], in_=gsrc), SG[i % 3])
                            EM.dma(lambda usrc=usrc, i=i: g.dma_start(
                                out=u_sb[:, i % 3], in_=usrc), SG[i % 3])

                def p3_sync(s):
                    EM.eng = s
                    EM.chain_off()
                    EM.wait("sSa", "spill_all")
                    EM.wait("sSb", "spill_all")
                    for tt in range(NTT):
                        if tt >= 1:
                            EM.wait("c_ln4v", ("ln4", tt * TCC - 1))
                        if tt >= 1:
                            EM.wait("sOa", value=16 * TCC)
                            EM.wait("sOb", value=16 * TCC)
                        EM.dma(lambda tt=tt: s.dma_start(
                            out=x1t_sb[:],
                            in_=x1sp_d[:, tt * TCC:(tt + 1) * TCC, :]), "sXl")
                        EM.mark(("x1ld", tt))
                        for tcc in range(TCC):
                            tti = tt * TCC + tcc
                            EM.wait("c_ln4v", ("ln4", tti))
                            EM.dma(lambda tt=tt, tcc=tcc: s.dma_start(
                                out=out_d[:, tt * TCC + tcc, 0:512],
                                in_=x1t_sb[:, tcc, 0:512]), "sOa")
                            EM.dma(lambda tt=tt, tcc=tcc: s.dma_start(
                                out=out_d[:, tt * TCC + tcc, 512:1024],
                                in_=x1t_sb[:, tcc, 512:1024]), "sOb")
                            EM.mark(("ost", tti))
                    EM.wait("sOa", value=16 * 2 * TCC)
                    EM.wait("sOb", value=16 * 2 * TCC)

                def p3_pe(t):
                    EM.eng = t
                    EM.chain_off()
                    EM.wait("p2_v", ("xffm", TC - 1))
                    EM.wait("p2_v", ("wst", TC - 1))
                    EM.wait("p2_s", ("wbf", TC - 1))
                    for ph in range(NPH):
                        tt, eidx = ph_te(ph)
                        if eidx > 0:
                            EM.wait("c_xe", value=rix(ph))
                        for m in range(M):
                            i = ph * M + m
                            EM.wait(SG[i % 3], value=32 * (i // 3 + 1))
                            rhs = xf_fm if eidx == 0 else xe_sb
                            roff = tt * TTW if eidx == 0 else 0
                            for th in range(2):
                                sl = i * 2 + th
                                if eidx > 0 and m == 0 and th == 0:
                                    EM.wait("c_wbcV", value=rix(ph))
                                elif sl >= 2:
                                    EM.wait("c_sa", value=sl - 1)
                                if sl >= 2:
                                    EM.wait("c_h", value=sl - 1)
                                for jc in range(8):
                                    EM.op(lambda jc=jc, i=i, th=th, rhs=rhs,
                                          roff=roff: t.matmul(
                                        a_ps[th][:, :],
                                        g_sb[:, i % 3, jc, :],
                                        rhs[:, jc,
                                            roff + th * 512:roff + (th + 1) * 512],
                                        start=(jc == 0), stop=(jc == 7)),
                                        inc="c_a" if jc == 7 else None)
                                    EM.op(lambda jc=jc, i=i, th=th, rhs=rhs,
                                          roff=roff: t.matmul(
                                        b_ps[th][:, :],
                                        u_sb[:, i % 3, jc, :],
                                        rhs[:, jc,
                                            roff + th * 512:roff + (th + 1) * 512],
                                        start=(jc == 0), stop=(jc == 7)),
                                        inc="c_b" if jc == 7 else None)
                        if ph + 1 < NPH and ph_te(ph + 1)[1] > 0:
                            ntt, neidx = ph_te(ph + 1)
                            EM.wait("c_sa", value=2 * (ph + 1) * M)
                            EM.wait("p2_v", ("wst", TC - 1))
                            for half in range(2):
                                EM.op(lambda half=half, neidx=neidx,
                                      ntt=ntt: t.matmul(
                                    a_ps[half][:, :],
                                    sel_sb[:, (neidx - 1) * 128:neidx * 128],
                                    wstage[0:8, ntt * TTW + half * 512:
                                           ntt * TTW + (half + 1) * 512],
                                    start=True, stop=True),
                                    inc="c_wbc")
                        EM.wait("c_h", value=2 * (ph * M + M))
                        EM.wait("sDa" if ph % 2 == 0 else "sDb", ("dld", ph))
                        if eidx == 0:
                            EM.wait("sXl", ("x1ld", tt))
                        for tcc in range(TCC):
                            J = ph * TCC + tcc
                            if J >= 2:
                                EM.wait("c_dnE", value=J - 1)
                            for m in range(M):
                                for half in range(2):
                                    EM.op(lambda m=m, half=half, tcc=tcc,
                                          ph=ph: t.matmul(
                                        dn_ps[tcc % 2][:,
                                                       half * 512:(half + 1) * 512],
                                        h_sb[:, m,
                                             tcc * 128:(tcc + 1) * 128],
                                        d_sb[:, ph % 2, m,
                                             half * 512:(half + 1) * 512],
                                        start=(m == 0), stop=(m == M - 1)),
                                        inc="c_dn"
                                        if (m == M - 1 and half == 1) else None)
                            EM.mark(("dn", J))

                def p3_s(sc):
                    EM.eng = sc
                    EM.chain_on("c_sS")

                    def emit_ln4s(tt, tcc):
                        tti = tt * TCC + tcc
                        ph0 = tt * (E + 1) + E
                        EM.wait("c_dnE", value=ph0 * TCC + tcc + 1)
                        if tti >= 1:
                            EM.wait("c_ln4v", ("ln4", tti - 1))
                        EM.op(lambda tcc=tcc: sc.activation(
                            sd4a_sb[:, :], x1t_sb[:, tcc, :], AF.Copy,
                            accum_out=st4_sb[:, 0:1]))
                        EM.op(lambda tcc=tcc: sc.activation(
                            sd4b_sb[:, :], x1t_sb[:, tcc, :],
                            AF.Square, accum_out=st4_sb[:, 1:2]))
                        EM.op(lambda: sc.activation(
                            st4_sb[:, 2:3], st4_sb[:, 0:1], AF.Copy,
                            scale=1.0 / C))
                        EM.op(lambda: sc.activation(
                            st4_sb[:, 3:4], st4_sb[:, 2:3],
                            AF.Square))
                        EM.op(lambda: sc.activation(
                            st4_sb[:, 3:4], st4_sb[:, 3:4], AF.Copy,
                            scale=-1.0, bias=EPS))
                        EM.op(lambda: sc.activation(
                            st4_sb[:, 4:5], st4_sb[:, 1:2], AF.Ln,
                            scale=1.0 / C, bias=st4_sb[:, 3:4]))
                        EM.op(lambda: sc.activation(
                            st4_sb[:, 4:5], st4_sb[:, 4:5], AF.Exp,
                            scale=-0.5), inc="c_ln4s")
                        EM.mark(("rstd4", tti))

                    pend = []
                    for ph in range(NPH):
                        tt, eidx = ph_te(ph)
                        for m in range(M):
                            i = ph * M + m
                            for th in range(2):
                                sl = i * 2 + th
                                EM.wait("c_a", value=sl + 1)
                                if sl >= 2:
                                    EM.wait("c_h", value=sl - 1)
                                EM.op(lambda sl=sl: sc.activation(
                                    sa_sb[:, sl % 2, :], a_ps[sl % 2][:, :],
                                    AF.Silu), inc="c_sa")
                            if pend:
                                emit_ln4s(*pend.pop(0))
                        if eidx == E:
                            if ph + 1 < NPH:
                                pend = [(tt, tcc) for tcc in range(TCC)]
                            else:
                                for tcc in range(TCC):
                                    emit_ln4s(tt, tcc)

                def p3_v(v):
                    EM.eng = v
                    EM.chain_on("c_sV")

                    def emit_xe(phx):
                        ntt = ph_te(phx)[0]
                        EM.wait("c_wbc", value=2 * rix(phx))
                        EM.op(lambda: v.tensor_copy(
                            wbc_sb[:, 0:512], a_ps[0][:, :]))
                        EM.op(lambda: v.tensor_copy(
                            wbc_sb[:, 512:1024], a_ps[1][:, :]),
                            inc="c_wbcV")
                        for jc in range(8):
                            EM.op(lambda jc=jc, ntt=ntt: v.tensor_tensor(
                                xe_sb[:, jc, :],
                                xf_fm[:, jc, ntt * TTW:(ntt + 1) * TTW],
                                wbc_sb[:, :], ALU.mult),
                                inc="c_xe" if jc == 7 else None)

                    def emit_ln4v(tt, tcc):
                        tti = tt * TCC + tcc
                        EM.wait("c_ln4s", ("rstd4", tti))
                        EM.op(lambda tcc=tcc: v.tensor_scalar(
                            x1t_sb[:, tcc, :], x1t_sb[:, tcc, :],
                            st4_sb[:, 2:3], st4_sb[:, 4:5],
                            ALU.subtract, ALU.mult), inc="c_ln4v")
                        EM.mark(("ln4", tti))

                    pend = []
                    for ph in range(NPH):
                        tt, eidx = ph_te(ph)
                        for m in range(M):
                            i = ph * M + m
                            for th in range(2):
                                sl = i * 2 + th
                                EM.wait("c_sa", value=sl + 1)
                                EM.wait("c_b", value=sl + 1)
                                if m == 0 and th == 0 and ph >= 1:
                                    EM.wait("c_dn", value=ph * TCC)
                                EM.op(lambda m=m, th=th, sl=sl:
                                      v.tensor_tensor(
                                    h_sb[:, m, th * 512:(th + 1) * 512],
                                    sa_sb[:, sl % 2, :], b_ps[sl % 2][:, :],
                                    ALU.mult), inc="c_h")
                            if pend:
                                emit_ln4v(*pend.pop(0))
                        if ph + 1 < NPH and ph_te(ph + 1)[1] > 0:
                            emit_xe(ph + 1)
                        if eidx == 0:
                            EM.wait("sXl", ("x1ld", tt))
                            for tcc in range(TCC):
                                EM.op(lambda tcc=tcc: v.tensor_scalar(
                                    x1t_sb[:, tcc, :], x1t_sb[:, tcc, :],
                                    2.0, None, ALU.mult))
                        for tcc in range(TCC):
                            J = ph * TCC + tcc
                            EM.wait("c_dn", value=J + 1)
                            EM.op(lambda tcc=tcc: v.tensor_tensor(
                                x1t_sb[:, tcc, :], x1t_sb[:, tcc, :],
                                dn_ps[tcc % 2][:, :], ALU.add), inc="c_dnE")
                        if eidx == E:
                            if ph + 1 < NPH:
                                pend = [(tt, tcc) for tcc in range(TCC)]
                            else:
                                for tcc in range(TCC):
                                    emit_ln4v(tt, tcc)

                if "3" not in PHASES:
                    def p2_tail(s):
                        EM.eng = s
                        if "2" in PHASES:
                            EM.wait("sXS", "spill_all")
                    if EM.real and "2" in PHASES:
                        block.sync(p2_tail)
                    elif "2" in PHASES:
                        p2_tail(None)
                elif EM.real:
                    block.gpsimd(p3_gp); block.tensor(p3_pe)
                    block.scalar(p3_s); block.vector(p3_v)
                    block.sync(p3_sync)
                else:
                    p3_gp(None); p3_s(None); p3_v(None)
                    p3_pe(None); p3_sync(None)

        _PFX[0] = "dry_"
        with ExitStack() as dry1:
            p12_program(EM, dry1)
        with ExitStack() as dry2:
            p3_program(EM, dry2)
        EM.reset()
        _PFX[0] = ""
        EM.real = True
        with ExitStack() as real1:
            p12_program(EM, real1)
        with ExitStack() as real2:
            p3_program(EM, real2)

    return nc


_NC_CACHE = None


def _host_layouts(inputs):
    f32 = np.float32

    def wT(a):
        return np.ascontiguousarray(np.asarray(a, f32).T) \
            .reshape(8, 128, C).transpose(1, 0, 2).copy()

    shared = dict(
        wqT=wT(inputs["L0_attn_Wq"]),
        wkT=wT(inputs["L0_attn_Wk"]),
        wvT=wT(inputs["L0_attn_Wv"]),
        woT=wT(inputs["L0_attn_Wo"]),
        bq=np.asarray(inputs["L0_attn_bq"], f32).reshape(8, 128).T.copy(),
        bk_row=np.asarray(inputs["L0_attn_bk"], f32).reshape(1, C).copy(),
        bv_row=np.asarray(inputs["L0_attn_bv"], f32).reshape(1, C).copy(),
        bo_row=np.asarray(inputs["L0_attn_bo"], f32).reshape(1, C).copy(),
        lng_fm=np.asarray(inputs["L0_attn_ln_g"], f32)
            .reshape(8, 128).T.copy(),
        lnb_fm=np.asarray(inputs["L0_attn_ln_b"], f32)
            .reshape(8, 128).T.copy(),
        fw=np.ascontiguousarray(np.broadcast_to(
            np.asarray(inputs["L0_ffn_ln_w"], f32), (128, C))),
        fb=np.ascontiguousarray(np.broadcast_to(
            np.asarray(inputs["L0_ffn_ln_b"], f32), (128, C))),
        router=np.asarray(inputs["L0_ffn_router_w"], f32)
            .reshape(8, 128, E).transpose(1, 0, 2).copy(),
        gate=np.asarray(inputs["L0_ffn_gate_w"], f32)
            .reshape(E, 8, 128, M, 128).transpose(0, 3, 2, 1, 4).copy(),
        up=np.asarray(inputs["L0_ffn_up_w"], f32)
            .reshape(E, 8, 128, M, 128).transpose(0, 3, 2, 1, 4).copy(),
        down=np.asarray(inputs["L0_ffn_down_w"], f32)
            .reshape(E, M, 128, C).copy(),
        shg=np.asarray(inputs["L0_ffn_sh_gate"], f32)
            .reshape(8, 128, M, 128).transpose(2, 1, 0, 3).copy(),
        shu=np.asarray(inputs["L0_ffn_sh_up"], f32)
            .reshape(8, 128, M, 128).transpose(2, 1, 0, 3).copy(),
        shd=np.asarray(inputs["L0_ffn_sh_down"], f32)
            .reshape(M, 128, C).copy(),
        ones_bf=np.ones((1, 128), ml_dtypes.bfloat16),
        onescol_bf=np.ones((128, 1), ml_dtypes.bfloat16),
        idt_bf=np.eye(128).astype(ml_dtypes.bfloat16),
        eps_col=np.full((128, 1), EPS, f32),
    )
    ones2 = np.zeros((128, 2), f32)
    ones2[0:64, 0] = 1.0
    ones2[64:128, 1] = 1.0
    shared["ones2_bf"] = ones2.astype(ml_dtypes.bfloat16)
    sel = np.zeros((8, E * 128), f32)
    for e in range(E):
        sel[e, e * 128:(e + 1) * 128] = 1.0
    shared["sel_bf"] = sel.astype(ml_dtypes.bfloat16)
    return shared


def kernel(**inputs):
    global _NC_CACHE
    x = np.asarray(inputs["L0_attn_x"], np.float32)
    shared = _host_layouts(inputs)

    in_maps = []
    for c in range(NCORES):
        own = np.concatenate([x[0, c * 1024:(c + 1) * 1024],
                              x[1, c * 1024:(c + 1) * 1024]], 0)
        x_tm = own.reshape(TC, 128, C).transpose(1, 0, 2).copy()
        xT = np.ascontiguousarray(own.T).reshape(8, 128, TOK) \
            .transpose(1, 0, 2).copy()
        in_maps.append(dict(shared, x_tm=x_tm, xT=xT))

    if _NC_CACHE is None:
        _NC_CACHE = build_nc()
    res = run_bass_kernel_spmd(_NC_CACHE, in_maps, core_ids=list(range(NCORES)))

    out = np.empty((B, N, C), np.float32)
    for c in range(NCORES):
        o = np.asarray(res.results[c]["out"]).transpose(1, 0, 2).reshape(TOK, C)
        out[0, c * 1024:(c + 1) * 1024] = o[0:1024]
        out[1, c * 1024:(c + 1) * 1024] = o[1024:2048]
    return out


# version 14 (final)
# speedup vs baseline: 1.2620x; 1.0346x over previous
import numpy as np
import ml_dtypes
import concourse.bass as bass
import concourse.mybir as mybir
from concourse.bass_utils import run_bass_kernel_spmd
from contextlib import ExitStack

F32 = mybir.dt.float32
BF16 = mybir.dt.bfloat16
AF = mybir.ActivationFunctionType
ALU = mybir.AluOpType
AX = mybir.AxisListType

NCORES = 8
B = 2
N = 8192
C = 1024
H = 16
E = 8
M = 16
TOK = 2048
TC = 16
TCB = 8
NTT = 2
TTW = 1024
TCC = 8
EPS = 1e-5
DEBUG = False
PHASES = "123"
P2STOP = 9


class Emit:

    def __init__(self):
        self.real = False
        self.cnt = {}
        self.marks = {}
        self.sems = {}
        self.eng = None
        self.chain = None
        self.last = None

    def op(self, f, inc=None, k=1):
        if self.chain is not None:
            if self.last is not None:
                self.wait(self.last[0], value=self.last[1])
            if inc is None:
                inc = self.chain
        inst = f() if self.real else None
        if inc is not None:
            if self.real:
                inst.then_inc(self.sems[inc], k)
            self.cnt[inc] = self.cnt.get(inc, 0) + k
            if self.chain is not None:
                self.last = (inc, self.cnt[inc])
        return inst

    def chain_on(self, sem):
        self.chain = sem
        self.last = None

    def chain_off(self):
        self.chain = None
        self.last = None

    def dma(self, f, inc):
        return self.op(f, inc, 16)

    def mark(self, name):
        if not self.real:
            self.marks[name] = dict(self.cnt)

    def wait(self, sem, mark_name=None, value=None):
        if value is None:
            if mark_name not in self.marks:
                return
            value = self.marks[mark_name].get(sem, 0)
        if value <= 0:
            return
        if self.real:
            self.eng.wait_ge(self.sems[sem], value)

    def selfwait(self, sem):
        self.wait(sem, value=self.cnt.get(sem, 0))

    def reset(self):
        self.cnt = {}


SEM_NAMES = [
    "sCst", "sW", "sXa", "sXb", "sSa", "sSb", "sOa", "sOb", "sColl", "sXl", "sDa", "sDb", "sG0", "sG1", "sG2",
    "sWk", "sWv", "sWq", "sBkv", "sXg0", "sXg1", "sXg2", "sXg3",
    "c_q", "c_eq", "c_k", "c_ek", "c_v", "c_va", "c_U", "c_Ue", "c_stA",
    "p2_pe", "p2_v", "p2_s", "sCC",
    "c_xe", "c_a", "c_b", "c_sa", "c_h", "c_dn", "c_dnE",
    "c_wbc", "c_wbcV", "c_ln4s", "c_ln4v", "c_sV", "c_sS",
]
SG = ["sG0", "sG1", "sG2"]
_PFX = [""]


def build_nc():
    nc = bass.Bass("TRN2", target_bir_lowering=False)

    d = {}
    def din(name, shape, dt=F32):
        d[name] = nc.dram_tensor(name, list(shape), dt, kind="ExternalInput").ap()
        return d[name]

    xT_d = din("xT", [128, 8, TOK])
    x_d = din("x_tm", [128, TC, C])
    wqT_d = din("wqT", [128, 8, C])
    wkT_d = din("wkT", [128, 8, C])
    wvT_d = din("wvT", [128, 8, C])
    woT_d = din("woT", [128, 8, C])
    bq_d = din("bq", [128, 8])
    bk_d = din("bk_row", [1, C])
    bv_d = din("bv_row", [1, C])
    bo_d = din("bo_row", [1, C])
    lngf_d = din("lng_fm", [128, 8])
    lnbf_d = din("lnb_fm", [128, 8])
    fw_d = din("fw", [128, C])
    fb_d = din("fb", [128, C])
    rout_d = din("router", [128, 8, E])
    gate_d = din("gate", [E, M, 128, 8, 128])
    up_d = din("up", [E, M, 128, 8, 128])
    down_d = din("down", [E, M, 128, C])
    shg_d = din("shg", [M, 128, 8, 128])
    shu_d = din("shu", [M, 128, 8, 128])
    shd_d = din("shd", [M, 128, C])
    ones_d = din("ones_bf", [1, 128], BF16)
    onesc_d = din("onescol_bf", [128, 1], BF16)
    ones2_d = din("ones2_bf", [128, 2], BF16)
    idt_d = din("idt_bf", [128, 128], BF16)
    sel_d = din("sel_bf", [8, E * 128], BF16)
    eps_d = din("eps_col", [128, 1])

    out_d = nc.dram_tensor("out", [128, TC, C], F32, kind="ExternalOutput").ap()
    kd = "ExternalOutput" if DEBUG else "Internal"
    x1sp_d = nc.dram_tensor("x1_spill", [128, TC, C], F32, kind=kd).ap()
    cc_in = nc.dram_tensor("cc_in", [64, 2080], F32, kind="Internal").ap()
    cc_out = nc.dram_tensor("cc_out", [64, 2080], F32, kind="Internal").ap()
    if DEBUG:
        dbg_ccin = nc.dram_tensor("dbg_ccin", [64, 2080], F32,
                                  kind="ExternalOutput").ap()
        dbg_ccout = nc.dram_tensor("dbg_ccout", [64, 2080], F32,
                                   kind="ExternalOutput").ap()
        dbg_state = nc.dram_tensor("dbg_state", [128, B, H, 64], F32,
                                   kind="ExternalOutput").ap()
        dbg_eq = nc.dram_tensor("dbg_eq", [128, 8, TOK], F32,
                                kind="ExternalOutput").ap()

    EM = Emit()

    with ExitStack() as st:
        e_ = st.enter_context

        xf_fm = e_(nc.sbuf_tensor(_PFX[0] + "xf_fm", [128, 8, TOK], BF16))
        lngf_sb = e_(nc.sbuf_tensor(_PFX[0] + "lngf_sb", [128, 8], F32))
        lnbf_sb = e_(nc.sbuf_tensor(_PFX[0] + "lnbf_sb", [128, 8], BF16))
        ob_sb = e_(nc.sbuf_tensor(_PFX[0] + "ob_sb", [1, C], BF16))
        bq_sb = e_(nc.sbuf_tensor(_PFX[0] + "bq_sb", [128, 8], F32))
        rout_sb = e_(nc.sbuf_tensor(_PFX[0] + "rout_sb", [128, 8, E], BF16))
        ones_sb = e_(nc.sbuf_tensor(_PFX[0] + "ones_sb", [1, 128], BF16))
        onesc_sb = e_(nc.sbuf_tensor(_PFX[0] + "onesc_sb", [128, 1], BF16))
        ones2_sb = e_(nc.sbuf_tensor(_PFX[0] + "ones2_sb", [128, 2], BF16))
        idt_sb = e_(nc.sbuf_tensor(_PFX[0] + "idt_sb", [128, 128], BF16))
        sel_sb = e_(nc.sbuf_tensor(_PFX[0] + "sel_sb", [8, E * 128], BF16))
        eps_sb = e_(nc.sbuf_tensor(_PFX[0] + "eps_sb", [128, 1], F32))
        wstage = e_(nc.sbuf_tensor(_PFX[0] + "wstage", [8, TOK], BF16))
        state_sb = e_(nc.sbuf_tensor(_PFX[0] + "state_sb", [128, B, H, 64], BF16))

        for s in SEM_NAMES:
            EM.sems[s] = e_(nc.semaphore(s))
        block = e_(nc.Block())

        def p12_program(EM, scope):
            e2 = scope.enter_context
            x_fm = e2(nc.sbuf_tensor(_PFX[0] + "x_fm", [128, 8, TOK], BF16))
            fw_sb = e2(nc.sbuf_tensor(_PFX[0] + "fw_sb", [128, C], F32))
            fb_sb = e2(nc.sbuf_tensor(_PFX[0] + "fb_sb", [128, C], F32))
            bk_sb = e2(nc.sbuf_tensor(_PFX[0] + "bk_sb", [1, C], BF16))
            bv_sb = e2(nc.sbuf_tensor(_PFX[0] + "bv_sb", [1, C], BF16))
            bo_sb = e2(nc.sbuf_tensor(_PFX[0] + "bo_sb", [1, C], BF16))
            wq_sb = e2(nc.sbuf_tensor(_PFX[0] + "wq_sb", [128, 8, C], BF16))
            eq_sb = e2(nc.sbuf_tensor(_PFX[0] + "eq_sb", [128, 8, TOK], BF16))

            with ExitStack() as ps1:
                e3 = ps1.enter_context
                wk_sb = e3(nc.sbuf_tensor(_PFX[0] + "wk_sb", [128, 8, C], BF16))
                wv_sb = e3(nc.sbuf_tensor(_PFX[0] + "wv_sb", [128, 8, C], BF16))
                ek_sb = e3(nc.sbuf_tensor(_PFX[0] + "ek_sb", [128, 2, C], BF16))
                v_sb = e3(nc.sbuf_tensor(_PFX[0] + "v_sb", [128, 2, C], BF16))
                ustage = e3(nc.sbuf_tensor(_PFX[0] + "ustage", [64, 2080], F32))
                stateA = e3(nc.sbuf_tensor(_PFX[0] + "stateA", [64, B * C], BF16))
                rz_sb = e3(nc.sbuf_tensor(_PFX[0] + "rz_sb", [64, 32], F32))
                state_raw = ustage
                k_ps = e3(nc.psum_tensor(_PFX[0] + "k_ps", [128, C], F32))
                v_ps = e3(nc.psum_tensor(_PFX[0] + "v_ps", [128, C], F32))
                U_ps = e3(nc.psum_tensor(_PFX[0] + "U_ps", [64, C], F32))
                z_ps = e3(nc.psum_tensor(_PFX[0] + "z_ps", [64, 32], F32))
                q_ps = k_ps

                def p1b_sync(s):
                    EM.eng = s
                    EM.chain_off()
                    for src, dst in [(ones_d, ones_sb), (onesc_d, onesc_sb),
                                     (bq_d, bq_sb), (lngf_d, lngf_sb),
                                     (fw_d, fw_sb),
                                     (fb_d, fb_sb),
                                     (ones2_d, ones2_sb),
                                     (idt_d, idt_sb), (sel_d, sel_sb),
                                     (eps_d, eps_sb)]:
                        EM.dma(lambda src=src, dst=dst:
                               s.dma_start(out=dst[:], in_=src), "sCst")
                    EM.mark("consts")

                def p1b_gp(g):
                    EM.eng = g
                    EM.chain_off()
                    EM.dma(lambda: g.dma_start(out=wk_sb[:], in_=wkT_d),
                           "sWk")
                    EM.mark("wk")
                    EM.dma(lambda: g.dma_start(
                        out=x_fm[:, :, 0:512], in_=xT_d[:, :, 0:512]), "sXg0")
                    EM.mark(("xg", 0))
                    for src, dst in [(bk_d, bk_sb), (bv_d, bv_sb)]:
                        EM.dma(lambda src=src, dst=dst:
                               g.dma_start(out=dst[:], in_=src), "sBkv")
                    EM.mark("bkv")
                    EM.dma(lambda: g.dma_start(out=wv_sb[:], in_=wvT_d),
                           "sWv")
                    EM.mark("wv")
                    for gi in range(1, 4):
                        EM.dma(lambda gi=gi: g.dma_start(
                            out=x_fm[:, :, gi * 512:(gi + 1) * 512],
                            in_=xT_d[:, :, gi * 512:(gi + 1) * 512]),
                            "sXg%d" % gi)
                        EM.mark(("xg", gi))
                    EM.dma(lambda: g.dma_start(out=wq_sb[:], in_=wqT_d),
                           "sWq")
                    EM.mark("wq")
                    for src, dst in [(bo_d, bo_sb), (rout_d, rout_sb),
                                     (lnbf_d, lnbf_sb)]:
                        EM.dma(lambda src=src, dst=dst:
                               g.dma_start(out=dst[:], in_=src), "sW")
                    EM.mark("p1w")

                def p1b_pe(t):
                    EM.eng = t
                    EM.chain_off()
                    EM.wait("sCst", "consts")
                    EM.wait("sBkv", "bkv")
                    for tc in range(TC):
                        b = tc // TCB
                        EM.wait("sXg%d" % (tc // 4), ("xg", tc // 4))
                        if tc == 0:
                            EM.wait("sWk", "wk")
                        if tc >= 1:
                            EM.wait("c_ek", value=tc)
                        for jc in range(8):
                            for half in range(2):
                                EM.op(lambda jc=jc, half=half, tc=tc: t.matmul(
                                    k_ps[:, half * 512:(half + 1) * 512],
                                    x_fm[:, jc, tc * 128:(tc + 1) * 128],
                                    wk_sb[:, jc, half * 512:(half + 1) * 512],
                                    start=(jc == 0), stop=False))
                        for half in range(2):
                            EM.op(lambda half=half: t.matmul(
                                k_ps[:, half * 512:(half + 1) * 512],
                                ones_sb[0:1, :],
                                bk_sb[0:1, half * 512:(half + 1) * 512],
                                start=False, stop=True),
                                inc="c_k" if half == 1 else None)
                        if tc == 0:
                            EM.wait("sWv", "wv")
                        if tc >= 1:
                            EM.wait("c_va", value=tc)
                        for jc in range(8):
                            for half in range(2):
                                EM.op(lambda jc=jc, half=half, tc=tc: t.matmul(
                                    v_ps[:, half * 512:(half + 1) * 512],
                                    x_fm[:, jc, tc * 128:(tc + 1) * 128],
                                    wv_sb[:, jc, half * 512:(half + 1) * 512],
                                    start=(jc == 0), stop=False))
                        for half in range(2):
                            EM.op(lambda half=half: t.matmul(
                                v_ps[:, half * 512:(half + 1) * 512],
                                ones_sb[0:1, :],
                                bv_sb[0:1, half * 512:(half + 1) * 512],
                                start=False, stop=True),
                                inc="c_v" if half == 1 else None)
                        EM.wait("c_ek", value=tc + 1)
                        EM.wait("c_va", value=tc + 1)
                        if tc == TCB:
                            EM.wait("c_Ue", value=1)
                        for h in range(H):
                            ust = (tc % TCB == 0) and (h in (0, 8))
                            usp = (tc % TCB == TCB - 1) and (h in (7, 15))
                            EM.op(lambda h=h, tc=tc, ust=ust, usp=usp: t.matmul(
                                U_ps[0:64, h * 64:(h + 1) * 64],
                                ek_sb[:, tc % 2, h * 64:(h + 1) * 64],
                                v_sb[:, tc % 2, h * 64:(h + 1) * 64],
                                start=ust, stop=usp))
                            zst = (tc == 0) and (h == 0)
                            zsp = (tc == TC - 1) and (h == H - 1)
                            EM.op(lambda h=h, b=b, zst=zst, zsp=zsp: t.matmul(
                                z_ps[0:64, b * 16 + h:b * 16 + h + 1],
                                ek_sb[:, tc % 2, h * 64:(h + 1) * 64],
                                onesc_sb[:, 0:1],
                                start=zst, stop=zsp),
                                inc="c_U" if h == H - 1 else None)

                def p1b_s(sc):
                    EM.eng = sc
                    EM.chain_on("c_sS")
                    for tc in range(TC):
                        EM.wait("c_k", value=tc + 1)
                        if tc >= 2:
                            EM.wait("c_U", value=tc - 1)
                        EM.op(lambda tc=tc: sc.activation(
                            ek_sb[:, tc % 2, :], k_ps[:, :], AF.Exp),
                            inc="c_ek")

                def p1b_v(v):
                    EM.eng = v
                    EM.chain_on("c_sV")
                    for tc in range(TC):
                        EM.wait("c_v", value=tc + 1)
                        if tc >= 2:
                            EM.wait("c_U", value=tc - 1)
                        EM.op(lambda tc=tc: v.tensor_copy(
                            v_sb[:, tc % 2, :], v_ps[:, :]), inc="c_va")
                        if tc % TCB == TCB - 1:
                            b = tc // TCB
                            EM.wait("c_U", value=tc + 1)
                            EM.op(lambda b=b: v.tensor_copy(
                                ustage[0:64, b * C:(b + 1) * C], U_ps[0:64, :]),
                                inc="c_Ue")
                    EM.op(lambda: v.tensor_copy(ustage[0:64, 2048:2080],
                                                z_ps[0:64, :]), inc="c_Ue")

                if EM.real:
                    block.sync(p1b_sync); block.gpsimd(p1b_gp)
                    block.tensor(p1b_pe); block.scalar(p1b_s)
                    block.vector(p1b_v)
                else:
                    p1b_sync(None); p1b_gp(None); p1b_pe(None)
                    p1b_s(None); p1b_v(None)

                def cc_gp(g):
                    EM.eng = g
                    EM.chain_off()
                    EM.wait("c_Ue", value=3)
                    EM.dma(lambda: g.dma_start(out=cc_in, in_=ustage[0:64, :]),
                           "sColl")
                    EM.wait("sColl", value=16)
                    EM.op(lambda: g.collective_compute(
                        "AllReduce", ALU.add,
                        replica_groups=[list(range(NCORES))],
                        ins=[cc_in.opt()], outs=[cc_out.opt()]),
                        inc="sCC", k=1)
                    EM.wait("sCC", value=1)
                    EM.dma(lambda: g.dma_start(out=state_raw[0:64, :],
                                               in_=cc_out), "sColl")
                    if DEBUG:
                        EM.dma(lambda: g.dma_start(out=dbg_ccin, in_=cc_in),
                               "sColl")
                        EM.dma(lambda: g.dma_start(out=dbg_ccout, in_=cc_out),
                               "sColl")
                    EM.mark("cc_done")

                def cc_v(v):
                    EM.eng = v
                    EM.chain_on("c_sV")
                    EM.op(lambda: v.memset(state_sb[:], 0.0))
                    EM.wait("sColl", "cc_done")
                    EM.op(lambda: v.reciprocal(rz_sb[0:64, :],
                                               state_raw[0:64, 2048:2080]),
                          inc="c_sV")
                    EM.selfwait("c_sV")
                    for b in range(B):
                        for h in range(H):
                            EM.op(lambda b=b, h=h: v.tensor_single_scalar(
                                stateA[0:64, b * C + h * 64:b * C + (h + 1) * 64],
                                state_raw[0:64, b * C + h * 64:b * C + (h + 1) * 64],
                                rz_sb[0:64, b * 16 + h:b * 16 + h + 1],
                                ALU.mult))
                    EM.op(lambda: v.engine_nop(), inc="c_stA", k=1)

                def cc_gp2(g):
                    EM.eng = g
                    EM.chain_off()
                    EM.wait("c_stA", value=1)
                    srcA = stateA[0:64, :].rearrange(
                        "p (b h2 par e) -> p par b h2 e", b=B, h2=8, par=2)
                    dstE = state_sb[0:64, :, :, :].rearrange(
                        "p b (h2 par) e -> p par b h2 e", par=2)
                    dstO = state_sb[64:128, :, :, :].rearrange(
                        "p b (h2 par) e -> p par b h2 e", par=2)
                    EM.dma(lambda: g.dma_start(
                        out=dstE[:, 0], in_=srcA[:, 0]), "sColl")
                    EM.dma(lambda: g.dma_start(
                        out=dstO[:, 1], in_=srcA[:, 1]), "sColl")
                    if DEBUG:
                        EM.wait("sColl", value=EM.cnt.get("sColl", 0))
                        EM.wait("c_eq", value=32)
                        EM.dma(lambda: g.dma_start(
                            out=dbg_state, in_=state_sb[:]), "sColl")
                        EM.dma(lambda: g.dma_start(
                            out=dbg_eq, in_=eq_sb[:]), "sColl")
                    EM.mark("state_done")

                def p1a_pe(t):
                    EM.eng = t
                    EM.chain_off()
                    EM.wait("c_ek", value=TC)
                    EM.wait("sWq", "wq")
                    for p in range(16):
                        qi0 = 2 * p
                        ic, tq0 = qi0 // 4, qi0 % 4
                        if qi0 >= 2:
                            EM.wait("c_eq", value=qi0)
                        for jc in range(8):
                            for r in range(2):
                                EM.op(lambda jc=jc, ic=ic, tq=tq0 + r, r=r:
                                      t.matmul(
                                    q_ps[:, r * 512:r * 512 + 512],
                                    wq_sb[:, jc, ic * 128:(ic + 1) * 128],
                                    x_fm[:, jc, tq * 512:(tq + 1) * 512],
                                    start=(jc == 0), stop=(jc == 7)),
                                    inc="c_q" if jc == 7 else None)

                def p1a_s(sc):
                    EM.eng = sc
                    EM.chain_on("c_sS")
                    for ic in range(8):
                        for tq in range(4):
                            qi = ic * 4 + tq
                            EM.wait("c_q", value=qi + 1)
                            EM.op(lambda ic=ic, tq=tq, qi=qi: sc.activation(
                                eq_sb[:, ic, tq * 512:(tq + 1) * 512],
                                q_ps[:, (qi % 2) * 512:(qi % 2) * 512 + 512],
                                AF.Exp,
                                bias=bq_sb[:, ic:ic + 1], scale=1.0),
                                inc="c_eq")

                if EM.real:
                    block.gpsimd(cc_gp); block.tensor(p1a_pe)
                    block.scalar(p1a_s); block.vector(cc_v)
                    block.gpsimd(cc_gp2)
                else:
                    cc_gp(None); p1a_pe(None); p1a_s(None); cc_v(None)
                    cc_gp2(None)

            wo_sb = wq_sb
            x_sl = e2(nc.sbuf_tensor(_PFX[0] + "x_sl", [128, 2, C], F32))
            attn_sb = e2(nc.sbuf_tensor(_PFX[0] + "attn_sb", [128, C], F32))
            scr_sb = e2(nc.sbuf_tensor(_PFX[0] + "scr_sb", [128, C], F32))
            scr2_sb = e2(nc.sbuf_tensor(_PFX[0] + "scr2_sb", [128, C], F32))
            sdump_sb = e2(nc.sbuf_tensor(_PFX[0] + "sdump_sb", [128, C], BF16))
            sdump2_sb = e2(nc.sbuf_tensor(_PFX[0] + "sdump2_sb", [128, C], BF16))
            ln1_bf = e2(nc.sbuf_tensor(_PFX[0] + "ln1_bf", [128, C], BF16))
            lnfm_sb = e2(nc.sbuf_tensor(_PFX[0] + "lnfm_sb", [128, 8, 128], BF16))
            r_sb = e2(nc.sbuf_tensor(_PFX[0] + "r_sb", [128, C], F32))
            x1tc_sb = e2(nc.sbuf_tensor(_PFX[0] + "x1tc_sb", [128, 2, C], F32))
            xftm_sb = e2(nc.sbuf_tensor(_PFX[0] + "xftm_sb", [128, C], BF16))
            rzq_sb = e2(nc.sbuf_tensor(_PFX[0] + "rzq_sb", [128, 16], F32))
            stat_sb = e2(nc.sbuf_tensor(_PFX[0] + "stat_sb", [128, 24], F32))
            wall_sb = e2(nc.sbuf_tensor(_PFX[0] + "wall_sb", [128, E], F32))
            wbf_sb = e2(nc.sbuf_tensor(_PFX[0] + "wbf_sb", [128, E], BF16))

            with ExitStack() as ps2:
                e3 = ps2.enter_context
                out_ps = e3(nc.psum_tensor(_PFX[0] + "out_ps", [128, C], F32))
                o_ps = e3(nc.psum_tensor(_PFX[0] + "o_ps", [128, C], F32))
                small_ps = e3(nc.psum_tensor(_PFX[0] + "small_ps", [128, 16], F32))
                lg_ps = e3(nc.psum_tensor(_PFX[0] + "lg_ps", [128, E], F32))
                bfT_ps = e3(nc.psum_tensor(_PFX[0] + "bfT_ps", [128, C], BF16))

                def p2_pe(t):
                    EM.eng = t
                    EM.chain_off()
                    EM.wait("sColl", "state_done")
                    EM.wait("sW", "p1w")
                    EM.wait("c_eq", value=32)
                    for tc in range(TC):
                        b = tc // TCB
                        if tc >= 1:
                            EM.wait("p2_v", ("outn", tc - 1))
                            EM.wait("p2_v", ("rzq", tc - 1))
                        for h in range(H):
                            EM.op(lambda h=h, b=b, tc=tc: t.matmul(
                                out_ps[:, h * 64:(h + 1) * 64],
                                eq_sb[:, h // 2, tc * 128:(tc + 1) * 128],
                                state_sb[:, b, h, :],
                                start=True, stop=True))
                        for ic in range(8):
                            EM.op(lambda ic=ic, tc=tc: t.matmul(
                                small_ps[:, 2 * ic:2 * ic + 2],
                                eq_sb[:, ic, tc * 128:(tc + 1) * 128],
                                ones2_sb[:, :], start=True, stop=True),
                                inc="p2_pe" if ic == 7 else None)
                        EM.mark(("g0", tc))
                        if tc == 0:
                            EM.wait("sW", "woT_g")
                            for half in range(2):
                                for jc in range(8):
                                    EM.op(lambda jc=jc, half=half: t.matmul(
                                        o_ps[0:1, half * 512:(half + 1) * 512],
                                        lnbf_sb[:, jc:jc + 1],
                                        wo_sb[:, jc,
                                              half * 512:(half + 1) * 512],
                                        start=(jc == 0), stop=(jc == 7)),
                                        inc="p2_pe"
                                        if (jc == 7 and half == 1) else None)
                            EM.mark("obias_mm")
                        if P2STOP < 2:
                            continue
                        EM.wait("p2_v", ("ln1bf", tc))
                        if tc >= 1:
                            EM.wait("p2_v", ("wst", tc - 1))
                        for jc in range(8):
                            EM.op(lambda jc=jc: t.transpose(
                                bfT_ps[:, jc * 128:(jc + 1) * 128],
                                ln1_bf[:, jc * 128:(jc + 1) * 128],
                                idt_sb[:, :]),
                                inc="p2_pe" if jc == 7 else None)
                        EM.mark(("g1", tc))
                        if P2STOP < 3:
                            continue
                        EM.wait("p2_v", ("lnfm", tc))
                        if tc == 0:
                            EM.wait("p2_v", "wog")
                        if tc >= 1:
                            EM.wait("p2_v", ("r1", tc - 1))
                        for jc in range(8):
                            for half in range(2):
                                EM.op(lambda jc=jc, half=half: t.matmul(
                                    o_ps[:, half * 512:(half + 1) * 512],
                                    lnfm_sb[:, jc, :],
                                    wo_sb[:, jc, half * 512:(half + 1) * 512],
                                    start=(jc == 0), stop=False))
                        for half in range(2):
                            EM.op(lambda half=half: t.matmul(
                                o_ps[:, half * 512:(half + 1) * 512],
                                ones_sb[0:1, :],
                                ob_sb[0:1, half * 512:(half + 1) * 512],
                                start=False, stop=True),
                                inc="p2_pe" if half == 1 else None)
                        EM.mark(("g2", tc))
                        if P2STOP < 4:
                            continue
                        EM.wait("p2_v", ("xftm", tc))
                        if tc >= 1:
                            EM.wait("p2_v", ("xffm", tc - 1))
                        for jc in range(8):
                            EM.op(lambda jc=jc: t.transpose(
                                bfT_ps[:, jc * 128:(jc + 1) * 128],
                                xftm_sb[:, jc * 128:(jc + 1) * 128],
                                idt_sb[:, :]),
                                inc="p2_pe" if jc == 7 else None)
                        EM.mark(("g3", tc))
                        if P2STOP < 5:
                            continue
                        EM.wait("p2_v", ("xffm", tc))
                        if tc >= 1:
                            EM.wait("p2_v", ("wall", tc - 1))
                        for jc in range(8):
                            EM.op(lambda jc=jc, tc=tc: t.matmul(
                                lg_ps[:, :],
                                xf_fm[:, jc, tc * 128:(tc + 1) * 128],
                                rout_sb[:, jc, :],
                                start=(jc == 0), stop=(jc == 7)),
                                inc="p2_pe" if jc == 7 else None)
                        EM.mark(("g4", tc))
                        EM.wait("p2_s", ("wbf", tc))
                        if tc >= 1:
                            EM.wait("p2_v", ("xffm", tc))
                        EM.op(lambda: t.transpose(
                            bfT_ps[0:8, 0:128], wbf_sb[:, :], idt_sb[:, :]),
                            inc="p2_pe")
                        EM.mark(("g5", tc))

                def ln_stats_s(sc, EMx, y_ap, base, mark_st, mark_rs, tc):
                    EMx.op(lambda: sc.activation(
                        sdump_sb[:, :], y_ap, AF.Copy,
                        accum_out=stat_sb[:, base:base + 1]))
                    EMx.op(lambda: sc.activation(
                        sdump2_sb[:, :], y_ap, AF.Square,
                        accum_out=stat_sb[:, base + 1:base + 2]), inc="p2_s")
                    EMx.mark((mark_st, tc))
                    EMx.op(lambda: sc.activation(
                        stat_sb[:, base + 2:base + 3],
                        stat_sb[:, base:base + 1], AF.Copy, scale=1.0 / C))
                    EMx.op(lambda: sc.activation(
                        stat_sb[:, base + 3:base + 4],
                        stat_sb[:, base + 2:base + 3], AF.Square))
                    EMx.op(lambda: sc.activation(
                        stat_sb[:, base + 3:base + 4],
                        stat_sb[:, base + 3:base + 4], AF.Copy, scale=-1.0,
                        bias=EPS))
                    EMx.op(lambda: sc.activation(
                        stat_sb[:, base + 4:base + 5],
                        stat_sb[:, base + 1:base + 2], AF.Ln, scale=1.0 / C,
                        bias=stat_sb[:, base + 3:base + 4]))
                    EMx.op(lambda: sc.activation(
                        stat_sb[:, base + 4:base + 5],
                        stat_sb[:, base + 4:base + 5], AF.Exp, scale=-0.5),
                        inc="p2_s")
                    EMx.mark((mark_rs, tc))

                def p2_s(sc):
                    EM.eng = sc
                    EM.chain_on("c_sS")
                    for tc in range(TC):
                        if P2STOP < 2:
                            continue
                        EM.wait("p2_v", ("outn", tc))
                        if tc >= 1:
                            EM.wait("p2_v", ("ln1bf", tc - 1))
                        ln_stats_s(sc, EM, attn_sb[:, :], 0, "st1", "rstd1", tc)
                        if P2STOP < 3:
                            continue
                        EM.wait("p2_v", ("r1", tc))
                        if tc >= 1:
                            EM.wait("p2_v", ("x1", tc - 1))
                        ln_stats_s(sc, EM, r_sb[:, :], 5, "st2", "rstd2", tc)
                        if P2STOP < 4:
                            continue
                        EM.wait("p2_v", ("x1", tc))
                        if tc >= 1:
                            EM.wait("p2_v", ("xftm", tc - 1))
                        ln_stats_s(sc, EM,
                                   x1tc_sb[:, tc % 2, :], 10, "st3", "rstd3",
                                   tc)
                        if P2STOP < 5:
                            continue
                        EM.wait("p2_v", ("top", tc))
                        EM.op(lambda: sc.activation(
                            stat_sb[:, 16:17], stat_sb[:, 15:16], AF.Exp,
                            scale=-1.0))
                        EM.op(lambda: sc.activation(
                            stat_sb[:, 17:18], stat_sb[:, 16:17], AF.Copy,
                            bias=1.0), inc="p2_s")
                        EM.mark(("sneg", tc))
                        EM.wait("p2_v", ("wall", tc))
                        EM.op(lambda: sc.activation(
                            wbf_sb[:, :], wall_sb[:, :], AF.Copy),
                            inc="p2_s")
                        EM.mark(("wbf", tc))

                def p2_v(v):
                    EM.eng = v
                    EM.chain_on("c_sV")
                    EM.wait("p2_pe", "obias_mm")
                    EM.op(lambda: v.tensor_tensor(
                        ob_sb[0:1, :], bo_sb[0:1, :], o_ps[0:1, :], ALU.add))
                    for jc in range(8):
                        EM.op(lambda jc=jc: v.tensor_scalar(
                            wo_sb[:, jc, :], wo_sb[:, jc, :],
                            lngf_sb[:, jc:jc + 1], None, ALU.mult),
                            inc="p2_v" if jc == 7 else None)
                    EM.mark("wog")
                    for tc in range(TC):
                        EM.wait("p2_pe", ("g0", tc))
                        if tc >= 1:
                            EM.wait("p2_s", ("st1", tc - 1))
                        EM.op(lambda: v.reciprocal(
                            rzq_sb[:, :], small_ps[:, 0:16]), inc="p2_v")
                        EM.mark(("rzq", tc))
                        EM.selfwait("p2_v")
                        for h in range(H):
                            EM.op(lambda h=h: v.tensor_single_scalar(
                                attn_sb[:, h * 64:(h + 1) * 64],
                                out_ps[:, h * 64:(h + 1) * 64],
                                rzq_sb[:, h:h + 1], ALU.mult),
                                inc="p2_v" if h == H - 1 else None)
                        EM.mark(("outn", tc))
                        if P2STOP < 2:
                            EM.op(lambda tc=tc: v.tensor_copy(
                                x1tc_sb[:, tc % 2, :], attn_sb[:, :]),
                                inc="p2_v")
                            EM.mark(("x1", tc))
                            continue
                        EM.wait("p2_s", ("rstd1", tc))
                        if tc >= 1:
                            EM.wait("p2_pe", ("g1", tc - 1))
                        EM.op(lambda: v.tensor_scalar(
                            ln1_bf[:, :], attn_sb[:, :], stat_sb[:, 2:3],
                            stat_sb[:, 4:5], ALU.subtract, ALU.mult),
                            inc="p2_v")
                        EM.mark(("ln1bf", tc))
                        EM.wait("p2_pe", ("g1", tc))
                        if tc >= 1:
                            EM.wait("p2_pe", ("g2", tc - 1))
                        EM.op(lambda: v.tensor_copy(
                            lnfm_sb[:].rearrange("p a b -> p (a b)"),
                            bfT_ps[:, :]), inc="p2_v")
                        EM.mark(("lnfm", tc))
                        if P2STOP < 3:
                            EM.op(lambda tc=tc: v.tensor_copy(
                                x1tc_sb[:, tc % 2, :], attn_sb[:, :]),
                                inc="p2_v")
                            EM.mark(("x1", tc))
                            continue
                        EM.wait("p2_pe", ("g2", tc))
                        EM.wait("sXa" if tc % 2 == 0 else "sXb", ("xload", tc))
                        if tc >= 1:
                            EM.wait("p2_s", ("st2", tc - 1))
                        EM.op(lambda tc=tc: v.tensor_tensor(
                            r_sb[:, :], x_sl[:, tc % 2, :], o_ps[:, :],
                            ALU.add), inc="p2_v")
                        EM.mark(("r1", tc))
                        EM.wait("p2_s", ("rstd2", tc))
                        if tc >= 2:
                            EM.wait("p2_s", ("st3", tc - 2))
                            EM.wait("sSa" if tc % 2 == 0 else "sSb",
                                    ("spill", tc - 2))
                        EM.op(lambda tc=tc: v.tensor_scalar(
                            x1tc_sb[:, tc % 2, :], r_sb[:, :],
                            stat_sb[:, 7:8], stat_sb[:, 9:10],
                            ALU.subtract, ALU.mult), inc="p2_v")
                        EM.mark(("x1", tc))
                        if P2STOP < 4:
                            continue
                        EM.wait("p2_s", ("rstd3", tc))
                        EM.op(lambda tc=tc: v.tensor_scalar(
                            scr_sb[:, :], x1tc_sb[:, tc % 2, :],
                            stat_sb[:, 12:13], stat_sb[:, 14:15],
                            ALU.subtract, ALU.mult))
                        EM.op(lambda: v.tensor_tensor(
                            scr2_sb[:, :], scr_sb[:, :], fw_sb[:, :],
                            ALU.mult))
                        if tc >= 1:
                            EM.wait("p2_pe", ("g3", tc - 1))
                        EM.op(lambda: v.tensor_tensor(
                            xftm_sb[:, :], scr2_sb[:, :], fb_sb[:, :],
                            ALU.add), inc="p2_v")
                        EM.mark(("xftm", tc))
                        EM.wait("p2_pe", ("g3", tc))
                        EM.op(lambda tc=tc: v.tensor_copy(
                            xf_fm[:, :, tc * 128:(tc + 1) * 128],
                            bfT_ps[:, :].rearrange("p (a b) -> p a b", b=128)),
                            inc="p2_v")
                        EM.mark(("xffm", tc))
                        if P2STOP < 5:
                            continue
                        EM.wait("p2_pe", ("g4", tc))
                        if tc >= 1:
                            EM.wait("p2_s", ("sneg", tc - 1))
                        EM.op(lambda: v.reduce_max(
                            stat_sb[:, 15:16], lg_ps[:, :], axis=AX.X),
                            inc="p2_v")
                        EM.mark(("top", tc))
                        if tc >= 1:
                            EM.wait("p2_s", ("wbf", tc - 1))
                        EM.selfwait("p2_v")
                        EM.op(lambda: v.tensor_single_scalar(
                            wall_sb[:, :], lg_ps[:, :], stat_sb[:, 15:16],
                            ALU.is_equal), inc="c_sV")
                        EM.wait("p2_s", ("sneg", tc))
                        EM.op(lambda: v.reciprocal(
                            stat_sb[:, 18:19], stat_sb[:, 17:18]),
                            inc="p2_v")
                        EM.wait("p2_v", value=EM.cnt.get("p2_v", 0))
                        EM.op(lambda: v.tensor_single_scalar(
                            wall_sb[:, :], wall_sb[:, :], stat_sb[:, 18:19],
                            ALU.mult), inc="p2_v")
                        EM.mark(("wall", tc))
                        EM.wait("p2_pe", ("g5", tc))
                        EM.op(lambda tc=tc: v.tensor_copy(
                            wstage[0:8, tc * 128:(tc + 1) * 128],
                            bfT_ps[0:8, 0:128]), inc="p2_v")
                        EM.mark(("wst", tc))

                def p2_sync(s):
                    EM.eng = s
                    EM.chain_off()
                    EM.wait("c_k", value=TC)
                    EM.wait("c_v", value=TC)
                    EM.wait("sColl", "state_done")
                    EM.dma(lambda: s.dma_start(out=x_sl[:, 0, :],
                                               in_=x_d[:, 0, :]), "sXa")
                    EM.dma(lambda: s.dma_start(out=x_sl[:, 1, :],
                                               in_=x_d[:, 1, :]), "sXb")
                    EM.mark(("xload", 0))
                    EM.mark(("xload", 1))
                    for tc in range(TC):
                        EM.wait("p2_v", ("x1", tc))
                        EM.dma(lambda tc=tc: s.dma_start(
                            out=x1sp_d[:, tc, :], in_=x1tc_sb[:, tc % 2, :]),
                            "sSa" if tc % 2 == 0 else "sSb")
                        EM.mark(("spill", tc))
                        if tc + 2 < TC:
                            EM.wait("p2_v", ("r1", tc))
                            EM.dma(lambda tc=tc: s.dma_start(
                                out=x_sl[:, tc % 2, :], in_=x_d[:, tc + 2, :]),
                                "sXa" if tc % 2 == 0 else "sXb")
                            EM.mark(("xload", tc + 2))
                    EM.mark("spill_all")

                def p2_gp(g):
                    EM.eng = g
                    EM.chain_off()
                    for icw in range(8):
                        EM.wait("c_q", value=(icw + 1) * 4)
                        EM.dma(lambda icw=icw: g.dma_start(
                            out=wo_sb[:, :, icw * 128:(icw + 1) * 128],
                            in_=woT_d[:, :, icw * 128:(icw + 1) * 128]), "sW")
                    EM.mark("woT_g")

                if "2" not in PHASES:
                    def p1_tail(s):
                        EM.eng = s
                        EM.wait("sColl", "state_done")
                        EM.wait("c_eq", value=32)
                    if EM.real:
                        block.sync(p1_tail)
                    else:
                        p1_tail(None)
                elif EM.real:
                    block.gpsimd(p2_gp); block.tensor(p2_pe)
                    block.scalar(p2_s); block.vector(p2_v)
                    block.sync(p2_sync)
                else:
                    p2_gp(None); p2_pe(None); p2_s(None); p2_v(None)
                    p2_sync(None)

        def p3_program(EM, scope):
            e2 = scope.enter_context
            x1t_sb = e2(nc.sbuf_tensor(_PFX[0] + "x1t_sb", [128, TCC, C], F32))
            d_sb = e2(nc.sbuf_tensor(_PFX[0] + "d_sb", [128, 2, M, C], BF16))
            h_sb = e2(nc.sbuf_tensor(_PFX[0] + "h_sb", [128, M, TTW], BF16))
            xe_sb = e2(nc.sbuf_tensor(_PFX[0] + "xe_sb", [128, 8, TTW], BF16))
            g_sb = e2(nc.sbuf_tensor(_PFX[0] + "g_sb", [128, 3, 8, 128], BF16))
            u_sb = e2(nc.sbuf_tensor(_PFX[0] + "u_sb", [128, 3, 8, 128], BF16))
            sa_sb = e2(nc.sbuf_tensor(_PFX[0] + "sa_sb", [128, 2, 512], BF16))
            wbc_sb = e2(nc.sbuf_tensor(_PFX[0] + "wbc_sb", [128, TTW], BF16))
            st4_sb = e2(nc.sbuf_tensor(_PFX[0] + "st4_sb", [128, 8], F32))
            sd4a_sb = e2(nc.sbuf_tensor(_PFX[0] + "sd4a_sb", [128, C], BF16))
            sd4b_sb = sd4a_sb

            with ExitStack() as ps3:
                e3 = ps3.enter_context
                a_ps = [e3(nc.psum_tensor(_PFX[0] + f"a_ps{k}", [128, 512], F32))
                        for k in range(2)]
                b_ps = [e3(nc.psum_tensor(_PFX[0] + f"b_ps{k}", [128, 512], F32))
                        for k in range(2)]
                dn_ps = [e3(nc.psum_tensor(_PFX[0] + f"dn_ps{k}", [128, C], F32))
                         for k in range(2)]

                NPH = NTT * (E + 1)

                def ph_te(ph):
                    return divmod(ph, E + 1)

                def rix(ph):
                    tt, eidx = ph_te(ph)
                    return tt * E + eidx

                def p3_gp(g):
                    EM.eng = g
                    EM.chain_off()
                    EM.wait("p2_v", ("wst", TC - 1))
                    EM.wait("p2_s", ("wbf", TC - 1))
                    EM.wait("p2_pe", ("g5", TC - 1))
                    EM.wait("sSa", "spill_all")
                    EM.wait("sSb", "spill_all")
                    for ph in range(NPH):
                        tt, eidx = ph_te(ph)
                        if ph >= 2:
                            EM.wait("c_dn", value=(ph - 1) * TCC)
                        for m in range(M):
                            dsrc = shd_d[m] if eidx == 0 else down_d[eidx - 1, m]
                            EM.dma(lambda dsrc=dsrc, m=m, ph=ph: g.dma_start(
                                out=d_sb[:, ph % 2, m], in_=dsrc),
                                "sDa" if ph % 2 == 0 else "sDb")
                        EM.mark(("dld", ph))
                        for m in range(M):
                            i = ph * M + m
                            if i >= 3:
                                EM.wait("c_b", value=2 * (i - 2))
                            gsrc = shg_d[m] if eidx == 0 else gate_d[eidx - 1, m]
                            usrc = shu_d[m] if eidx == 0 else up_d[eidx - 1, m]
                            EM.dma(lambda gsrc=gsrc, i=i: g.dma_start(
                                out=g_sb[:, i % 3], in_=gsrc), SG[i % 3])
                            EM.dma(lambda usrc=usrc, i=i: g.dma_start(
                                out=u_sb[:, i % 3], in_=usrc), SG[i % 3])

                def p3_sync(s):
                    EM.eng = s
                    EM.chain_off()
                    EM.wait("sSa", "spill_all")
                    EM.wait("sSb", "spill_all")
                    for tt in range(NTT):
                        if tt >= 1:
                            EM.wait("c_ln4v", ("ln4", tt * TCC - 1))
                        if tt >= 1:
                            EM.wait("sOa", value=16 * TCC)
                            EM.wait("sOb", value=16 * TCC)
                        EM.dma(lambda tt=tt: s.dma_start(
                            out=x1t_sb[:],
                            in_=x1sp_d[:, tt * TCC:(tt + 1) * TCC, :]), "sXl")
                        EM.mark(("x1ld", tt))
                        for tcc in range(TCC):
                            tti = tt * TCC + tcc
                            EM.wait("c_ln4v", ("ln4", tti))
                            EM.dma(lambda tt=tt, tcc=tcc: s.dma_start(
                                out=out_d[:, tt * TCC + tcc, 0:512],
                                in_=x1t_sb[:, tcc, 0:512]), "sOa")
                            EM.dma(lambda tt=tt, tcc=tcc: s.dma_start(
                                out=out_d[:, tt * TCC + tcc, 512:1024],
                                in_=x1t_sb[:, tcc, 512:1024]), "sOb")
                            EM.mark(("ost", tti))
                    EM.wait("sOa", value=16 * 2 * TCC)
                    EM.wait("sOb", value=16 * 2 * TCC)

                def p3_pe(t):
                    EM.eng = t
                    EM.chain_off()
                    EM.wait("p2_v", ("xffm", TC - 1))
                    EM.wait("p2_v", ("wst", TC - 1))
                    EM.wait("p2_s", ("wbf", TC - 1))
                    for ph in range(NPH):
                        tt, eidx = ph_te(ph)
                        if eidx > 0:
                            EM.wait("c_xe", value=rix(ph))
                        for m in range(M):
                            i = ph * M + m
                            EM.wait(SG[i % 3], value=32 * (i // 3 + 1))
                            rhs = xf_fm if eidx == 0 else xe_sb
                            roff = tt * TTW if eidx == 0 else 0
                            for th in range(2):
                                sl = i * 2 + th
                                if eidx > 0 and m == 0 and th == 0:
                                    EM.wait("c_wbcV", value=rix(ph))
                                elif sl >= 2:
                                    EM.wait("c_sa", value=sl - 1)
                                if sl >= 2:
                                    EM.wait("c_h", value=sl - 1)
                                for jc in range(8):
                                    EM.op(lambda jc=jc, i=i, th=th, rhs=rhs,
                                          roff=roff: t.matmul(
                                        a_ps[th][:, :],
                                        g_sb[:, i % 3, jc, :],
                                        rhs[:, jc,
                                            roff + th * 512:roff + (th + 1) * 512],
                                        start=(jc == 0), stop=(jc == 7)),
                                        inc="c_a" if jc == 7 else None)
                                    EM.op(lambda jc=jc, i=i, th=th, rhs=rhs,
                                          roff=roff: t.matmul(
                                        b_ps[th][:, :],
                                        u_sb[:, i % 3, jc, :],
                                        rhs[:, jc,
                                            roff + th * 512:roff + (th + 1) * 512],
                                        start=(jc == 0), stop=(jc == 7)),
                                        inc="c_b" if jc == 7 else None)
                        if ph + 1 < NPH and ph_te(ph + 1)[1] > 0:
                            ntt, neidx = ph_te(ph + 1)
                            EM.wait("c_sa", value=2 * (ph + 1) * M)
                            EM.wait("p2_v", ("wst", TC - 1))
                            for half in range(2):
                                EM.op(lambda half=half, neidx=neidx,
                                      ntt=ntt: t.matmul(
                                    a_ps[half][:, :],
                                    sel_sb[:, (neidx - 1) * 128:neidx * 128],
                                    wstage[0:8, ntt * TTW + half * 512:
                                           ntt * TTW + (half + 1) * 512],
                                    start=True, stop=True),
                                    inc="c_wbc")
                        EM.wait("c_h", value=2 * (ph * M + M))
                        EM.wait("sDa" if ph % 2 == 0 else "sDb", ("dld", ph))
                        if eidx == 0:
                            EM.wait("sXl", ("x1ld", tt))
                        for tcc in range(TCC):
                            J = ph * TCC + tcc
                            if J >= 2:
                                EM.wait("c_dnE", value=J - 1)
                            for m in range(M):
                                for half in range(2):
                                    EM.op(lambda m=m, half=half, tcc=tcc,
                                          ph=ph: t.matmul(
                                        dn_ps[tcc % 2][:,
                                                       half * 512:(half + 1) * 512],
                                        h_sb[:, m,
                                             tcc * 128:(tcc + 1) * 128],
                                        d_sb[:, ph % 2, m,
                                             half * 512:(half + 1) * 512],
                                        start=(m == 0), stop=(m == M - 1)),
                                        inc="c_dn"
                                        if (m == M - 1 and half == 1) else None)
                            EM.mark(("dn", J))

                def p3_s(sc):
                    EM.eng = sc
                    EM.chain_on("c_sS")

                    def emit_ln4s(tt, tcc):
                        tti = tt * TCC + tcc
                        ph0 = tt * (E + 1) + E
                        EM.wait("c_dnE", value=ph0 * TCC + tcc + 1)
                        if tti >= 1:
                            EM.wait("c_ln4v", ("ln4", tti - 1))
                        EM.op(lambda tcc=tcc: sc.activation(
                            sd4a_sb[:, :], x1t_sb[:, tcc, :], AF.Copy,
                            accum_out=st4_sb[:, 0:1]))
                        EM.op(lambda tcc=tcc: sc.activation(
                            sd4b_sb[:, :], x1t_sb[:, tcc, :],
                            AF.Square, accum_out=st4_sb[:, 1:2]))
                        EM.op(lambda: sc.activation(
                            st4_sb[:, 2:3], st4_sb[:, 0:1], AF.Copy,
                            scale=1.0 / C))
                        EM.op(lambda: sc.activation(
                            st4_sb[:, 3:4], st4_sb[:, 2:3],
                            AF.Square))
                        EM.op(lambda: sc.activation(
                            st4_sb[:, 3:4], st4_sb[:, 3:4], AF.Copy,
                            scale=-1.0, bias=EPS))
                        EM.op(lambda: sc.activation(
                            st4_sb[:, 4:5], st4_sb[:, 1:2], AF.Ln,
                            scale=1.0 / C, bias=st4_sb[:, 3:4]))
                        EM.op(lambda: sc.activation(
                            st4_sb[:, 4:5], st4_sb[:, 4:5], AF.Exp,
                            scale=-0.5), inc="c_ln4s")
                        EM.mark(("rstd4", tti))

                    pend = []
                    for ph in range(NPH):
                        tt, eidx = ph_te(ph)
                        for m in range(M):
                            i = ph * M + m
                            for th in range(2):
                                sl = i * 2 + th
                                EM.wait("c_a", value=sl + 1)
                                if sl >= 2:
                                    EM.wait("c_h", value=sl - 1)
                                EM.op(lambda sl=sl: sc.activation(
                                    sa_sb[:, sl % 2, :], a_ps[sl % 2][:, :],
                                    AF.Silu), inc="c_sa")
                            if pend:
                                emit_ln4s(*pend.pop(0))
                        if eidx == E:
                            if ph + 1 < NPH:
                                pend = [(tt, tcc) for tcc in range(TCC)]
                            else:
                                for tcc in range(TCC):
                                    emit_ln4s(tt, tcc)

                def p3_v(v):
                    EM.eng = v
                    EM.chain_on("c_sV")

                    def emit_xe(phx):
                        ntt = ph_te(phx)[0]
                        EM.wait("c_wbc", value=2 * rix(phx))
                        EM.op(lambda: v.tensor_copy(
                            wbc_sb[:, 0:512], a_ps[0][:, :]))
                        EM.op(lambda: v.tensor_copy(
                            wbc_sb[:, 512:1024], a_ps[1][:, :]),
                            inc="c_wbcV")
                        for jc in range(8):
                            EM.op(lambda jc=jc, ntt=ntt: v.tensor_tensor(
                                xe_sb[:, jc, :],
                                xf_fm[:, jc, ntt * TTW:(ntt + 1) * TTW],
                                wbc_sb[:, :], ALU.mult),
                                inc="c_xe" if jc == 7 else None)

                    def emit_ln4v(tt, tcc):
                        tti = tt * TCC + tcc
                        EM.wait("c_ln4s", ("rstd4", tti))
                        EM.op(lambda tcc=tcc: v.tensor_scalar(
                            x1t_sb[:, tcc, :], x1t_sb[:, tcc, :],
                            st4_sb[:, 2:3], st4_sb[:, 4:5],
                            ALU.subtract, ALU.mult), inc="c_ln4v")
                        EM.mark(("ln4", tti))

                    pend = []
                    for ph in range(NPH):
                        tt, eidx = ph_te(ph)
                        for m in range(M):
                            i = ph * M + m
                            for th in range(2):
                                sl = i * 2 + th
                                EM.wait("c_sa", value=sl + 1)
                                EM.wait("c_b", value=sl + 1)
                                if m == 0 and th == 0 and ph >= 1:
                                    EM.wait("c_dn", value=ph * TCC)
                                EM.op(lambda m=m, th=th, sl=sl:
                                      v.tensor_tensor(
                                    h_sb[:, m, th * 512:(th + 1) * 512],
                                    sa_sb[:, sl % 2, :], b_ps[sl % 2][:, :],
                                    ALU.mult), inc="c_h")
                            if pend:
                                emit_ln4v(*pend.pop(0))
                        if ph + 1 < NPH and ph_te(ph + 1)[1] > 0:
                            emit_xe(ph + 1)
                        if eidx == 0:
                            EM.wait("sXl", ("x1ld", tt))
                            for tcc in range(TCC):
                                EM.op(lambda tcc=tcc: v.tensor_scalar(
                                    x1t_sb[:, tcc, :], x1t_sb[:, tcc, :],
                                    2.0, None, ALU.mult))
                        for tcc in range(TCC):
                            J = ph * TCC + tcc
                            EM.wait("c_dn", value=J + 1)
                            EM.op(lambda tcc=tcc: v.tensor_tensor(
                                x1t_sb[:, tcc, :], x1t_sb[:, tcc, :],
                                dn_ps[tcc % 2][:, :], ALU.add), inc="c_dnE")
                        if eidx == E:
                            if ph + 1 < NPH:
                                pend = [(tt, tcc) for tcc in range(TCC)]
                            else:
                                for tcc in range(TCC):
                                    emit_ln4v(tt, tcc)

                if "3" not in PHASES:
                    def p2_tail(s):
                        EM.eng = s
                        if "2" in PHASES:
                            EM.wait("sXS", "spill_all")
                    if EM.real and "2" in PHASES:
                        block.sync(p2_tail)
                    elif "2" in PHASES:
                        p2_tail(None)
                elif EM.real:
                    block.gpsimd(p3_gp); block.tensor(p3_pe)
                    block.scalar(p3_s); block.vector(p3_v)
                    block.sync(p3_sync)
                else:
                    p3_gp(None); p3_s(None); p3_v(None)
                    p3_pe(None); p3_sync(None)

        _PFX[0] = "dry_"
        with ExitStack() as dry1:
            p12_program(EM, dry1)
        with ExitStack() as dry2:
            p3_program(EM, dry2)
        EM.reset()
        _PFX[0] = ""
        EM.real = True
        with ExitStack() as real1:
            p12_program(EM, real1)
        with ExitStack() as real2:
            p3_program(EM, real2)

    return nc


_NC_CACHE = None


def _host_layouts(inputs):
    f32 = np.float32

    def wT(a):
        return np.ascontiguousarray(np.asarray(a, f32).T) \
            .reshape(8, 128, C).transpose(1, 0, 2).copy()

    shared = dict(
        wqT=wT(inputs["L0_attn_Wq"]),
        wkT=wT(inputs["L0_attn_Wk"]),
        wvT=wT(inputs["L0_attn_Wv"]),
        woT=wT(inputs["L0_attn_Wo"]),
        bq=np.asarray(inputs["L0_attn_bq"], f32).reshape(8, 128).T.copy(),
        bk_row=np.asarray(inputs["L0_attn_bk"], f32).reshape(1, C).copy(),
        bv_row=np.asarray(inputs["L0_attn_bv"], f32).reshape(1, C).copy(),
        bo_row=np.asarray(inputs["L0_attn_bo"], f32).reshape(1, C).copy(),
        lng_fm=np.asarray(inputs["L0_attn_ln_g"], f32)
            .reshape(8, 128).T.copy(),
        lnb_fm=np.asarray(inputs["L0_attn_ln_b"], f32)
            .reshape(8, 128).T.copy(),
        fw=np.ascontiguousarray(np.broadcast_to(
            np.asarray(inputs["L0_ffn_ln_w"], f32), (128, C))),
        fb=np.ascontiguousarray(np.broadcast_to(
            np.asarray(inputs["L0_ffn_ln_b"], f32), (128, C))),
        router=np.asarray(inputs["L0_ffn_router_w"], f32)
            .reshape(8, 128, E).transpose(1, 0, 2).copy(),
        gate=np.asarray(inputs["L0_ffn_gate_w"], f32)
            .reshape(E, 8, 128, M, 128).transpose(0, 3, 2, 1, 4).copy(),
        up=np.asarray(inputs["L0_ffn_up_w"], f32)
            .reshape(E, 8, 128, M, 128).transpose(0, 3, 2, 1, 4).copy(),
        down=np.asarray(inputs["L0_ffn_down_w"], f32)
            .reshape(E, M, 128, C).copy(),
        shg=np.asarray(inputs["L0_ffn_sh_gate"], f32)
            .reshape(8, 128, M, 128).transpose(2, 1, 0, 3).copy(),
        shu=np.asarray(inputs["L0_ffn_sh_up"], f32)
            .reshape(8, 128, M, 128).transpose(2, 1, 0, 3).copy(),
        shd=np.asarray(inputs["L0_ffn_sh_down"], f32)
            .reshape(M, 128, C).copy(),
        ones_bf=np.ones((1, 128), ml_dtypes.bfloat16),
        onescol_bf=np.ones((128, 1), ml_dtypes.bfloat16),
        idt_bf=np.eye(128).astype(ml_dtypes.bfloat16),
        eps_col=np.full((128, 1), EPS, f32),
    )
    ones2 = np.zeros((128, 2), f32)
    ones2[0:64, 0] = 1.0
    ones2[64:128, 1] = 1.0
    shared["ones2_bf"] = ones2.astype(ml_dtypes.bfloat16)
    sel = np.zeros((8, E * 128), f32)
    for e in range(E):
        sel[e, e * 128:(e + 1) * 128] = 1.0
    shared["sel_bf"] = sel.astype(ml_dtypes.bfloat16)
    return shared


def kernel(**inputs):
    global _NC_CACHE
    x = np.asarray(inputs["L0_attn_x"], np.float32)
    shared = _host_layouts(inputs)

    in_maps = []
    for c in range(NCORES):
        own = np.concatenate([x[0, c * 1024:(c + 1) * 1024],
                              x[1, c * 1024:(c + 1) * 1024]], 0)
        x_tm = own.reshape(TC, 128, C).transpose(1, 0, 2).copy()
        xT = np.ascontiguousarray(own.T).reshape(8, 128, TOK) \
            .transpose(1, 0, 2).copy()
        in_maps.append(dict(shared, x_tm=x_tm, xT=xT))

    if _NC_CACHE is None:
        _NC_CACHE = build_nc()
    res = run_bass_kernel_spmd(_NC_CACHE, in_maps, core_ids=list(range(NCORES)))

    out = np.empty((B, N, C), np.float32)
    for c in range(NCORES):
        o = np.asarray(res.results[c]["out"]).transpose(1, 0, 2).reshape(TOK, C)
        out[0, c * 1024:(c + 1) * 1024] = o[0:1024]
        out[1, c * 1024:(c + 1) * 1024] = o[1024:2048]
    return out
